# version 12
# baseline (speedup 1.0000x reference)
import numpy as np
import ml_dtypes
from contextlib import ExitStack

import concourse.bass as bass
import concourse.mybir as mybir
from concourse.bass_utils import run_bass_kernel_spmd

F32 = mybir.dt.float32
BF16 = mybir.dt.bfloat16
AF = mybir.ActivationFunctionType
ALU = mybir.AluOpType
AX = mybir.AxisListType
BF = ml_dtypes.bfloat16

B, S, D = 2, 4096, 2048
NCORE = 8
TOK = 1024
NSLOT = 8
NCH = 16
NH, HD = 16, 128
QKV_O = 2560
FFH = 8192
GUH = 16384
MAGIC = 12582912.0
SCALE0 = 0.08838834764831845
CAP = 30.0
LNLOG_SCALE0 = float(np.log(np.float64(SCALE0)))
TFF = 256
NSLAB = TOK // TFF

_STAGE = 99
_DUMPS = ()
_LAST_RESULTS = None
_WPREP = True
_WPARTS = {'ag0', 'l0', 'l1', 'ag1'}
_FFN0 = True
_ATTN0 = True


def _apd(t, off, dims):
    return bass.AP(t, off, [list(d) for d in dims])


class Sem:
    def __init__(self, h):
        self.h = h
        self.n = 0

    def inc(self, inst, k=1):
        inst.then_inc(self.h, k)
        self.n += k
        return self.n

    def wait(self, eng, val=None):
        eng.wait_ge(self.h, self.n if val is None else val)


class Ctx:

    def __init__(self, nc, stack):
        self.nc = nc
        self.stack = stack
        en = stack.enter_context
        self.sem = {}
        for name in ("LD", "GLD", "MM", "VE", "SC", "GP", "CC", "SLD",
                     "ATT", "ATV", "ATS", "PRJ", "FFN", "TRP", "TEV"):
            self.sem[name] = Sem(en(nc.semaphore(name.lower())))
        self.sb = {}
        self.ps = {}

    def sbuf(self, name, shape, dt):
        scope = self.scopes[-1] if getattr(self, "scopes", None) else self.stack
        if not hasattr(self, "_used_names"):
            self._used_names = {}
        k = self._used_names.get(name, 0)
        self._used_names[name] = k + 1
        uname = name if k == 0 else f"{name}__{k}"
        self.sb[name] = scope.enter_context(self.nc.sbuf_tensor(uname, shape, dt))
        if getattr(self, "scopes", None) and len(self.scopes) > 1:
            self.scope_names[-1].append(name)
        return self.sb[name]

    def push_scope(self):
        if not hasattr(self, "scopes"):
            self.scopes = [self.stack]
            self.scope_names = [[]]
        st = ExitStack()
        st.__enter__()
        self.scopes.append(st)
        self.scope_names.append([])
        return st

    def pop_scope(self):
        self.fence()
        st = self.scopes.pop()
        for n in self.scope_names.pop():
            self.sb.pop(n, None)
        st.__exit__(None, None, None)

    def fence(self):
        for ename in ("sync", "gpsimd", "vector", "scalar", "tensor"):
            eng = getattr(self.nc, ename)
            for s in ("VE", "SC", "GP", "MM", "LD", "GLD", "SLD", "TRP", "TEV"):
                self.sem[s].wait(eng)

    def psum(self, name, shape, dt=F32):
        self.ps[name] = self.stack.enter_context(self.nc.psum_tensor(name, shape, dt))
        return self.ps[name]

    def v(self, inst):
        s = self.sem["VE"]
        n = s.inc(inst)
        self.nc.vector.wait_ge(s.h, n)
        return n

    def s(self, inst):
        s = self.sem["SC"]
        n = s.inc(inst)
        self.nc.scalar.wait_ge(s.h, n)
        return n

    def g(self, inst):
        s = self.sem["GP"]
        n = s.inc(inst)
        self.nc.gpsimd.wait_ge(s.h, n)
        return n


def _build(dumps=(), stage=99):
    nc = bass.Bass("TRN2", target_bir_lowering=False, debug=False)
    dumps = set(dumps)

    def di(name, shape, dt=F32):
        return nc.dram_tensor(name, shape, dt, kind="ExternalInput")

    x0 = di("x0", [TOK, D])
    w_qkv = di("w_qkv", [D, QKV_O])
    w_o0 = di("w_o0", [D, D])
    wg_sl = di("wg_sl", [FFH // 8, D])
    wu_sl = di("wu_sl", [FFH // 8, D])
    wd_sl = di("wd_sl", [D // 8, FFH])
    w_q1 = di("w_q1", [D, D])
    w_k1 = di("w_k1", [D, D])
    w_v1 = di("w_v1", [D, D])
    w_g1 = di("w_g1", [D, D])
    w_o1 = di("w_o1", [D, D])
    b_q1 = di("b_q1", [1, D])
    b_k1 = di("b_k1", [1, D])
    b_v1 = di("b_v1", [1, D])
    b_g1 = di("b_g1", [1, D])
    b_o1 = di("b_o1", [1, D])
    gn_w = di("gn_w", [1, D])
    gn_b = di("gn_b", [1, D])
    decay = di("decay", [1, NH])
    wgu_sl = di("wgu_sl", [GUH // 8, D])
    wp_sl = di("wp_sl", [D // 8, FFH])
    pos2048 = di("pos2048", [1, TOK])
    cos_t = di("cos_t", [128, TOK])
    sin_t = di("sin_t", [128, TOK])
    swap128 = di("swap128", [128, 128], F32)
    rotm = di("rotm", [7 * 128, 128], F32)
    bmask = di("bmask", [4 * 128, 128])
    ident32 = di("ident32", [128, 128], F32)
    identbf = di("identbf", [128, 128], BF16)

    out_tok = nc.dram_tensor("out_tok", [TOK, D], F32, kind="ExternalOutput")

    dump_t = {}
    for dn, shape, dt in [
        ("d_xn", [D, TOK], F32),
        ("d_xres", [D, TOK], F32),
        ("d_q0", [NH * 2 * 128, TOK], BF16),
        ("d_ag2", [4096, 1024], BF16),
        ("d_attn0", [D, TOK], F32),
        ("d_x1", [D, TOK], F32),
        ("d_r", [1, TOK], F32),
        ("d_xq", [D, TOK], BF16),
        ("d_wsc", [1, 4], F32),
        ("d_h", [FFH, TFF], F32),
        ("d_hq", [FFH, TFF], BF16),
        ("d_ag1", [2048, 1024], BF16),
        ("d_ag1d", [2048, 1024], BF16),
        ("d_xn1", [D, TOK], F32),
        ("d_q1", [NH * 2 * 128, TOK], BF16),
        ("d_ag3", [8192, 1024], BF16),
        ("d_sig", [128, 128], F32),
        ("d_attn1", [D, TOK], F32),
        ("d_gate1", [D, TOK], F32),
        ("d_hn1", [FFH, TFF], BF16),
    ]:
        if dn in dumps:
            dump_t[dn] = nc.dram_tensor(dn, shape, dt, kind="ExternalOutput")

    st_mu = nc.dram_tensor("st_mu", [1, TOK], F32)
    st_sq = nc.dram_tensor("st_sq", [1, TOK], F32)
    st_mx = nc.dram_tensor("st_mx", [1, TOK], F32)
    x_res = nc.dram_tensor("x_res", [D, TOK], F32)
    xn_hbm = nc.dram_tensor("xn_hbm", [D, TOK], F32)
    q_spill = nc.dram_tensor("q_spill", [2 * NH * 128, TOK], BF16)
    gate1_h = nc.dram_tensor("gate1_h", [D, TOK], F32)
    h_hbm = nc.dram_tensor("h_hbm", [FFH, TOK], F32)
    ag0_in = nc.dram_tensor("ag0_in", [1, 4], F32)
    ag0_out = nc.dram_tensor("ag0_out", [8, 4], F32)
    ag1_in = nc.dram_tensor("ag1_in", [12288, 1024], BF16)
    ag1_out = nc.dram_tensor("ag1_out", [98304, 1024], BF16)
    ag2_in = nc.dram_tensor("ag2_in", [1024, 1024], BF16)
    ag2_out = nc.dram_tensor("ag2_out", [4096, 1024], BF16)
    ag3_in = nc.dram_tensor("ag3_in", [8192, 1024], BF16)
    ag3_out = nc.dram_tensor("ag3_out", [32768, 1024], BF16)
    ag3s_in = nc.dram_tensor("ag3s_in", [128, 128], F32)
    ag3s_out = nc.dram_tensor("ag3s_out", [512, 128], F32)

    with ExitStack() as stack:
        cx = Ctx(nc, stack)
        en = stack.enter_context
        blk = en(nc.Block())
        _emit(cx, locals(), dumps, dump_t, stage)
    return nc



def _emit(cx, T, dumps, dump_t, stage):
    nc = cx.nc
    LD, GLD, MM, VE, SC, GP, CC = (cx.sem[k] for k in
                                   ("LD", "GLD", "MM", "VE", "SC", "GP", "CC"))

    ident32 = cx.sbuf("s_id32", [128, 128], F32)
    identbf = cx.sbuf("s_idbf", [128, 128], BF16)
    ones_col = cx.sbuf("s_ones", [128, 1], F32)
    ones_row = cx.sbuf("s_onesr", [1, 128], F32)
    bmask_sb = cx.sbuf("s_bmask", [128, 4, 128], F32)
    cos_sb = cx.sbuf("s_cos", [128, TOK], F32)
    sin_sb = cx.sbuf("s_sin", [128, TOK], F32)
    swap_sb = cx.sbuf("s_swap", [128, 128], F32)
    rotm_sb = cx.sbuf("s_rotm", [128, 7, 128], F32)

    P = [cx.psum(f"p{i}", [128, 512], F32) for i in range(6)]
    PTB = [cx.psum(f"ptb{i}", [128, 128], BF16) for i in range(2)]

    nc.sync.dma_start(out=ident32[:, :], in_=T["ident32"][:, :]).then_inc(LD.h, 16); LD.n += 16
    nc.sync.dma_start(out=identbf[:, :], in_=T["identbf"][:, :]).then_inc(LD.h, 16); LD.n += 16
    nc.sync.dma_start(out=bmask_sb[:, :, :],
                      in_=_apd(T["bmask"], 0, [[128, 128], [128 * 128, 4], [1, 128]])).then_inc(LD.h, 16); LD.n += 16
    nc.sync.dma_start(out=cos_sb[:, :], in_=T["cos_t"][:, :]).then_inc(LD.h, 16); LD.n += 16
    nc.sync.dma_start(out=sin_sb[:, :], in_=T["sin_t"][:, :]).then_inc(LD.h, 16); LD.n += 16
    nc.sync.dma_start(out=swap_sb[:, :], in_=T["swap128"][:, :]).then_inc(LD.h, 16); LD.n += 16
    nc.sync.dma_start(out=rotm_sb[:, :, :],
                      in_=_apd(T["rotm"], 0, [[128, 128], [128 * 128, 7], [1, 128]])).then_inc(LD.h, 16); LD.n += 16
    cx.g(nc.gpsimd.memset(ones_col[:, :], 1.0))
    cx.g(nc.gpsimd.memset(ones_row[:, :], 1.0))

    cx.push_scope()
    xin = cx.sbuf("s_xin", [128, D], F32)
    tstg = cx.sbuf("s_tstg", [128, NCH * 128], F32)
    xsc = cx.sbuf("s_xsc", [128, D], F32)
    stt3 = cx.sbuf("s_stt3", [128, 3], F32)

    def _stats_hook(tt, buf):
        LD.wait(nc.scalar)
        VE.wait(nc.scalar)
        cx.s(nc.scalar.activation(xsc[:, :], buf[:, :], AF.Square, bias=0.0, scale=1.0))
        cx.sem["TRP"].wait(nc.vector)
        SC.wait(nc.vector)
        cx.v(nc.vector.tensor_reduce(stt3[:, 0:1], buf[:, :], AX.X, ALU.add))
        cx.v(nc.vector.tensor_reduce(stt3[:, 1:2], xsc[:, :], AX.X, ALU.add))
        cx.v(nc.vector.tensor_scalar(stt3[:, 0:1], stt3[:, 0:1], 1.0 / D, None, ALU.mult))
        cx.v(nc.vector.tensor_scalar(stt3[:, 1:2], stt3[:, 1:2], 1.0 / D, None, ALU.mult))
        cx.v(nc.vector.tensor_scalar(xsc[:, :], buf[:, :], stt3[:, 0:1], None, ALU.subtract))
        cx.v(nc.vector.tensor_reduce(stt3[:, 2:3], xsc[:, :], AX.X, ALU.max,
                                       apply_absolute_value=True))
        VE.wait(nc.sync)
        for k, dst in enumerate(("st_mu", "st_sq", "st_mx")):
            cx.sem["TRP"].inc(nc.sync.dma_start(
                out=_apd(T[dst], tt * 128, [[1, 128], [1, 1]]),
                in_=stt3[:, k:k + 1]), 16)

    _transpose_128xD(cx, src_load=lambda tt: nc.sync.dma_start(
        out=xin[:, :], in_=T["x0"][tt * 128:(tt + 1) * 128, :]),
        xin=xin, tstg=tstg, ident=ident32,
        dst_ap=lambda tt: _apd(T["x_res"], tt * 128,
                               [[TOK, 128], [128 * TOK, NCH], [1, 128]]),
        tstg_ap=_apd(tstg, 0, [[NCH * 128, 128], [128, NCH], [1, 128]]),
        per_tile=_stats_hook)
    cx.pop_scope()

    if stage < 1:
        _finish_stub(cx, T)
        return

    cx.push_scope()
    xq = cx.sbuf("s_xq", [128, NCH, TOK], BF16)
    u1g = cx.sbuf("s_u1g", [128, TOK], F32)
    u1u = cx.sbuf("s_u1u", [128, TOK], F32)
    cx.sbuf("s_wsc", [1, 4], F32)
    cx.sbuf("s_wscb", [128, 4], F32)

    cx.push_scope()
    xnh = cx.sbuf("s_xnh", [128, NCH, TOK], BF16)
    xnl = cx.sbuf("s_xnl", [128, NCH, TOK], BF16)
    rb = cx.sbuf("s_rb", [128, TOK], F32)
    sb_s = cx.sbuf("s_qs", [1, TOK], F32)

    _emit_ln(cx, T, layer=0, xnh=xnh, xnl=xnl, rb=rb, sb_s=sb_s)
    if "d_xn" in dumps:
        _dump_xn(cx, T, dump_t["d_xn"], xnh, xnl)
    if "d_r" in dumps:
        VE.wait(nc.sync)
        nc.sync.dma_start(out=dump_t["d_r"][:, :], in_=sb_s[:, :]).then_inc(LD.h, 16); LD.n += 16
    if _WPREP:
        _emit_wprep(cx, T, dumps, dump_t)
    if stage >= 2:
        _emit_proj0(cx, T, xnh, xnl, dumps, dump_t)
    if stage >= 3 and _FFN0:
        _emit_ffn0_xq(cx, T, xnh, xnl, rb, sb_s, xq, u1g, u1u, dumps, dump_t)
    cx.pop_scope()

    if stage < 2:
        cx.pop_scope()
        _finish_stub(cx, T)
        return

    if stage >= 3 and _FFN0:
        _emit_ffn0_main(cx, T, xq, u1g, u1u, dumps, dump_t)
    cx.pop_scope()
    if stage < 4:
        _finish_stub(cx, T)
        return

    cx.push_scope()
    attn_sb = cx.sbuf("s_attn", [128, NCH, TOK], F32)
    _emit_attn0(cx, T, attn_sb, dumps, dump_t)
    if stage >= 5:
        _emit_wo(cx, T, attn_sb, w="w_o0", bias=None, layer=0)
    cx.pop_scope()
    if "d_x1" in dumps:
        _dump_xres(cx, T, dump_t["d_x1"])
    if stage < 6:
        _finish_stub(cx, T)
        return

    cx.push_scope()
    invrow = cx.sbuf("s_invrow", [1, TOK], F32)

    cx.push_scope()
    xnh = cx.sbuf("s_xnh", [128, NCH, TOK], BF16)
    xnl = cx.sbuf("s_xnl", [128, NCH, TOK], BF16)
    _emit_ln(cx, T, layer=1, xnh=xnh, xnl=xnl, rb=None, sb_s=None)
    if "d_xn1" in dumps:
        _dump_xn(cx, T, dump_t["d_xn1"], xnh, xnl)
    _emit_proj1(cx, T, xnh, xnl, dumps, dump_t)
    if stage >= 7:
        _emit_ffn1_gu(cx, T, xnh, xnl, invrow, dumps, dump_t)
    cx.pop_scope()
    if stage < 7:
        cx.pop_scope()
        _finish_stub(cx, T)
        return

    _emit_ffn1_proj(cx, T, invrow, dumps, dump_t)
    cx.pop_scope()
    if stage < 8:
        _finish_stub(cx, T)
        return

    cx.push_scope()
    attn_sb = cx.sbuf("s_attn", [128, NCH, TOK], F32)
    _emit_attn1(cx, T, attn_sb, dumps, dump_t)
    if stage >= 9:
        _emit_asm1(cx, T, attn_sb, dumps, dump_t)
    cx.pop_scope()

    _emit_output(cx, T)


def _tev_copy(cx, inst):
    s = cx.sem["TEV"]
    n = s.inc(inst)
    cx.nc.vector.wait_ge(s.h, n)
    return n


def _transpose_128xD(cx, src_load, xin, tstg, ident, dst_ap, tstg_ap=None,
                     per_tile=None):
    nc = cx.nc
    LD, MM, TEV, TRP = cx.sem["LD"], cx.sem["MM"], cx.sem["TEV"], cx.sem["TRP"]
    PT = [cx.ps["p4"], cx.ps["p5"]]
    for tt in range(NSLOT):
        LD.wait(nc.sync)
        if tt > 0:
            MM.wait(nc.sync)
            if per_tile is not None:
                cx.sem["VE"].wait(nc.sync)
                cx.sem["SC"].wait(nc.sync)
        src_load(tt).then_inc(LD.h, 16); LD.n += 16
        if per_tile is not None:
            per_tile(tt, xin)
        LD.wait(nc.tensor)
        if tt > 0:
            TRP.wait(nc.tensor)
        for ch in range(NCH):
            pt = PT[ch % 2]
            if ch >= 2:
                TEV.wait(nc.tensor, TEV.n - 1)
            MM.inc(nc.tensor.transpose(pt[:, 0:128], xin[:, ch * 128:(ch + 1) * 128], ident[:, :]))
            MM.wait(nc.vector)
            _tev_copy(cx, nc.vector.tensor_copy(tstg[:, ch * 128:(ch + 1) * 128], pt[:, 0:128]))
        TEV.wait(nc.sync)
        TRP.inc(nc.sync.dma_start(
            out=dst_ap(tt),
            in_=(tstg_ap if tstg_ap is not None else tstg[:, :])), 16)
    TRP.wait(nc.sync)


def _finish_stub(cx, T):
    _emit_output(cx, T)


def _emit_output(cx, T):
    nc = cx.nc
    cx.sem["GLD"].wait(nc.sync)
    cx.sem["GP"].wait(nc.sync)
    cx.sem["TRP"].wait(nc.sync)
    cx.push_scope()
    xf = cx.sbuf("s_xin", [128, D], F32)
    tstg = cx.sbuf("s_tstg", [128, NCH * 128], F32)
    _transpose_128xD(cx, src_load=lambda tt: nc.sync.dma_start(
        out=_apd(xf, 0, [[D, 128], [128, NCH], [1, 128]]),
        in_=_apd(T["x_res"], tt * 128, [[TOK, 128], [128 * TOK, NCH], [1, 128]])),
        xin=xf, tstg=tstg, ident=cx.sb["s_id32"],
        dst_ap=lambda tt: T["out_tok"][tt * 128:(tt + 1) * 128, :])
    cx.sem["LD"].wait(nc.sync)
    cx.pop_scope()


def _dump_xres(cx, T, dst):
    nc = cx.nc
    GLD, LD = cx.sem["GLD"], cx.sem["LD"]
    GLD.wait(nc.sync)
    LD.inc(nc.sync.dma_start(out=dst[:, :], in_=T["x_res"][:, :]), 16)
    LD.wait(nc.sync)


def _dump_xn(cx, T, dst, xnh, xnl):
    nc = cx.nc
    VE, LD = cx.sem["VE"], cx.sem["LD"]
    tmp = cx.sb.get("s_dmp") or cx.sbuf("s_dmp", [128, TOK], F32)
    for ch in range(NCH):
        LD.wait(nc.vector)
        cx.v(nc.vector.tensor_tensor(tmp[:, :], xnh[:, ch, :], xnl[:, ch, :], ALU.add))
        VE.wait(nc.sync)
        LD.inc(nc.sync.dma_start(out=dst[ch * 128:(ch + 1) * 128, :], in_=tmp[:, :]), 16)
        LD.wait(nc.sync)


def _emit_wprep(cx, T, dumps, dump_t):
    nc = cx.nc
    wsc = cx.sb["s_wsc"]
    wscb = cx.sb["s_wscb"]
    cx.push_scope()
    LD, GLD, MM, VE, SC, GP, CC = (cx.sem[k] for k in
                                   ("LD", "GLD", "MM", "VE", "SC", "GP", "CC"))
    identbf = cx.sb["s_idbf"]

    wsl = cx.sbuf("s_wsl", [128, 2, 2048], F32)
    wbs = cx.sbuf("s_wbs", [128, 2048], BF16)
    wq = cx.sbuf("s_wq", [128, 2048], BF16)
    wq32 = cx.sbuf("s_wq32", [128, 2048], F32)
    wqt = cx.sbuf("s_wqt", [128, 128], BF16)
    scl = cx.sbuf("s_wscl", [128, 32], F32)
    psums = cx.sbuf("s_wps", [128, 1], F32)
    acc128 = cx.sbuf("s_acc128", [128, 4], F32)
    acc4 = cx.sbuf("s_acc4", [1, 4], F32)
    g8 = cx.sbuf("s_g8", [8, 4], F32)

    do = _WPARTS
    cx.g(nc.gpsimd.memset(acc128[:, :], 0.0))
    GP.wait(nc.vector)
    specs0 = [("wg_sl", FFH // 8, D, 0), ("wu_sl", FFH // 8, D, 1), ("wd_sl", D // 8, FFH, 2)]
    SLD = cx.sem["SLD"]
    ve_wm = [0, 0]
    bufi = [0]

    def _wsl_load(name, rt, seg):
        buf = bufi[0] % 2
        bufi[0] += 1
        sem_, eng_ = (LD, nc.sync) if buf == 0 else (SLD, nc.scalar)
        VE.wait(eng_, ve_wm[buf])
        sem_.inc(eng_.dma_start(
            out=wsl[:, buf, :],
            in_=T[name][rt * 128:(rt + 1) * 128, seg * 2048:(seg + 1) * 2048]), 16)
        sem_.wait(nc.vector)
        return buf

    for name, rows, cols, slot in specs0:
        for rt in range(rows // 128):
            for seg in range(cols // 2048):
                buf = _wsl_load(name, rt, seg)
                ve_wm[buf] = cx.v(nc.vector.tensor_reduce(psums[:, :], wsl[:, buf, :],
                                               AX.X, ALU.add,
                                               apply_absolute_value=True))
                cx.v(nc.vector.tensor_tensor(acc128[:, slot:slot + 1], acc128[:, slot:slot + 1],
                                               psums[:, :], ALU.add))
    VE.wait(nc.tensor)
    MM.inc(nc.tensor.matmul(cx.ps["p0"][0:1, 0:4], cx.sb["s_ones"][:, :], acc128[:, :],
                            start=True, stop=True))
    MM.wait(nc.vector)
    cx.v(nc.vector.tensor_copy(acc4[:, :], cx.ps["p0"][0:1, 0:4]))
    VE.wait(nc.gpsimd)
    if 'ag0' in do:
        GLD.inc(nc.gpsimd.dma_start(out=T["ag0_in"][:, :], in_=acc4[:, :]), 16)
        GLD.wait(nc.gpsimd)
        CC.inc(nc.gpsimd.collective_compute(
            "AllGather", ALU.bypass, replica_groups=[list(range(8))],
            ins=[T["ag0_in"].ap().opt()], outs=[T["ag0_out"].ap().opt()]))
        CC.wait(nc.sync)
        LD.inc(nc.sync.dma_start(out=g8[:, :], in_=T["ag0_out"][:, :]), 16)
        LD.wait(nc.tensor)
        MM.inc(nc.tensor.matmul(cx.ps["p0"][0:1, 0:4], cx.sb["s_ones"][0:8, :], g8[:, :],
                                start=True, stop=True))
    else:
        VE.wait(nc.tensor)
        MM.inc(nc.tensor.matmul(cx.ps["p0"][0:1, 0:4], cx.sb["s_ones"][:, :], acc128[:, :],
                                start=True, stop=True))
    MM.wait(nc.vector)
    n0 = float(FFH * D) / (1.0 if 'ag0' in do else 8.0)
    cx.v(nc.vector.tensor_scalar(wsc[0:1, 0:4], cx.ps["p0"][0:1, 0:4], 1.0 / n0, 1e-5,
                                   ALU.mult, ALU.add))
    wrec = cx.sbuf("s_wrec", [1, 4], F32)
    cx.v(nc.vector.reciprocal(wrec[0:1, 0:4], wsc[0:1, 0:4]))
    if "d_wsc" in dumps:
        VE.wait(nc.sync)
        LD.inc(nc.sync.dma_start(out=dump_t["d_wsc"][:, :], in_=wsc[:, :]), 16)

    wrec_b = cx.sbuf("s_wrecb", [128, 4], F32)
    VE.wait(nc.tensor)
    MM.inc(nc.tensor.matmul(cx.ps["p0"][:, 0:4], cx.sb["s_onesr"][:, :], wrec[0:1, 0:4],
                            start=True, stop=True))
    MM.wait(nc.vector)
    cx.v(nc.vector.tensor_copy(wrec_b[:, :], cx.ps["p0"][:, 0:4]))
    VE.wait(nc.tensor)
    MM.inc(nc.tensor.matmul(cx.ps["p0"][:, 0:4], cx.sb["s_onesr"][:, :], wsc[0:1, 0:4],
                            start=True, stop=True))
    MM.wait(nc.vector)
    cx.v(nc.vector.tensor_copy(wscb[:, :], cx.ps["p0"][:, 0:4]))

    def quant_l0(name, rows, cols, slot, row_base):
        for rt in range(rows // 128):
            for seg in range(cols // 2048):
                buf = _wsl_load(name, rt, seg)
                cx.sem["TRP"].wait(nc.vector)
                GP.wait(nc.vector)
                ve_wm[buf] = cx.v(nc.vector.tensor_scalar(wq32[:, :], wsl[:, buf, :],
                                               wrec_b[:, slot:slot + 1], None, ALU.mult))
                cx.v(nc.vector.tensor_scalar(wbs[:, :], wq32[:, :], 0.5, None, ALU.is_gt))
                cx.v(nc.vector.tensor_scalar(wq32[:, :], wq32[:, :], -0.5, None, ALU.is_lt))
                VE.wait(nc.gpsimd)
                MM.wait(nc.gpsimd)
                cx.g(nc.gpsimd.tensor_tensor(wq[:, :], wbs[:, :], wq32[:, :],
                                               ALU.subtract))
                GP.wait(nc.tensor)
                _wq_transpose(cx, T, wq, 2048, row_base, rt, seg, cols)

    if 'l0' in do:
        quant_l0("wg_sl", FFH // 8, D, 0, 0)
        quant_l0("wu_sl", FFH // 8, D, 1, 2048)
        quant_l0("wd_sl", D // 8, FFH, 2, 4096)

    def quant_l1(name, rows, cols, row_base):
        ntile = rows // 128
        nseg = cols // 2048
        for rt in range(ntile):
            for seg in range(nseg):
                buf = _wsl_load(name, rt, seg)
                cx.sem["TRP"].wait(nc.vector)
                GP.wait(nc.vector)
                ve_wm[buf] = cx.v(nc.vector.tensor_copy(wbs[:, :2048], wsl[:, buf, :]))
                cx.v(nc.vector.tensor_reduce(
                    scl[:, :], _apd(wbs, 0, [[2048, 128], [64, 32], [1, 64]]),
                    AX.X, ALU.add, apply_absolute_value=True))
                sclb = cx.sb.get("s_sclb") or cx.sbuf("s_sclb", [128, 32], BF16)
                cx.v(nc.vector.tensor_scalar(sclb[:, :], scl[:, :], 1.0 / 64.0, None, ALU.mult))
                cx.v(nc.vector.tensor_scalar_max(scl[:, :], sclb[:, :], 1e-8))
                recg = cx.sb.get("s_recg") or cx.sbuf("s_recg", [128, 32], F32)
                cx.v(nc.vector.reciprocal(recg[:, :], scl[:, :]))
                qv = cx.sb.get("s_qv") or cx.sbuf("s_qv", [128, 2048], BF16)
                cx.v(nc.vector.tensor_tensor(
                    _apd(qv, 0, [[2048, 128], [64, 32], [1, 64]]),
                    _apd(wbs, 0, [[2048, 128], [64, 32], [1, 64]]),
                    _apd(recg, 0, [[32, 128], [1, 32], [0, 64]]), ALU.mult))
                t1 = cx.sb.get("s_t1w") or cx.sbuf("s_t1w", [128, 2048], F32)
                cx.v(nc.vector.tensor_scalar(t1[:, :2048], qv[:, :2048], 0.5, None, ALU.is_gt))
                MM.wait(nc.vector)
                cx.v(nc.vector.tensor_scalar(wq[:, :2048], qv[:, :2048], -0.5, None, ALU.is_lt))
                VE.wait(nc.gpsimd)
                cx.g(nc.gpsimd.tensor_tensor(t1[:, :2048], t1[:, :2048], wq[:, :2048],
                                               ALU.subtract))
                GP.wait(nc.vector)
                cx.v(nc.vector.tensor_tensor(
                    _apd(wq, 0, [[2048, 128], [64, 32], [1, 64]]),
                    _apd(t1, 0, [[2048, 128], [64, 32], [1, 64]]),
                    _apd(scl, 0, [[32, 128], [1, 32], [0, 64]]), ALU.mult))
                _wq_transpose(cx, T, wq, 2048, row_base, rt, seg, cols)

    if 'l1' in do:
        quant_l1("wgu_sl", GUH // 8, D, 6144)
        quant_l1("wp_sl", D // 8, FFH, 10240)

    if 'ag1' in do:
        cx.sem["TRP"].wait(nc.gpsimd)
        CC.inc(nc.gpsimd.collective_compute(
            "AllGather", ALU.bypass, replica_groups=[list(range(8))],
            ins=[T["ag1_in"].ap().opt()], outs=[T["ag1_out"].ap().opt()]))
        if "d_ag1" in dumps:
            CC.wait(nc.sync)
            LD.inc(nc.sync.dma_start(out=dump_t["d_ag1"][:, :],
                                     in_=T["ag1_out"][0:2048, :]), 16)
            LD.wait(nc.sync)
        if "d_ag1d" in dumps:
            CC.wait(nc.sync)
            LD.inc(nc.sync.dma_start(out=dump_t["d_ag1d"][:, :],
                                     in_=T["ag1_out"][6144:8192, :]), 16)
            LD.wait(nc.sync)
    cx.pop_scope()


def _wq_transpose(cx, T, wq, cols, row_base, rt, seg, full_cols):
    nc = cx.nc
    MM, VE, TRP = cx.sem["MM"], cx.sem["VE"], cx.sem["TRP"]
    identbf = cx.sb["s_idbf"]
    wqt = cx.sb["s_wqt"]
    slice_rows = {2048: 1024, 8192: 256, 16384: 2048}
    srows = {0: 1024, 2048: 1024, 4096: 256, 6144: 2048, 10240: 256}[row_base]
    TEV = cx.sem["TEV"]
    cx.sem["VE"].wait(nc.tensor)
    cx.sem["GP"].wait(nc.tensor)
    for c in range(cols // 128):
        pt = cx.ps["ptb%d" % (c % 2)]
        if TEV.n >= 1:
            TEV.wait(nc.tensor, TEV.n - 1)
        MM.inc(nc.tensor.transpose(pt[:, :], wq[:, c * 128:(c + 1) * 128], identbf[:, :]))
        MM.wait(nc.vector)
        TRP.wait(nc.vector)
        _tev_copy(cx, nc.vector.tensor_copy(wqt[:, :], pt[:, :]))
        TEV.wait(nc.sync)
        i0 = seg * 2048 + c * 128
        flat = row_base * 1024 + i0 * srows + rt * 128
        TRP.inc(nc.sync.dma_start(
            out=_apd(T["ag1_in"], flat, [[srows, 128], [1, 128]]),
            in_=wqt[:, :]), 16)


def _bcast_row(cx, row_ap_fn, dst, n=TOK):
    nc = cx.nc
    MM, VE = cx.sem["MM"], cx.sem["VE"]
    P0 = cx.ps["p0"]
    VE.wait(nc.tensor)
    for th in range(n // 512):
        MM.inc(nc.tensor.matmul(P0[:, :], cx.sb["s_onesr"][:, :],
                                row_ap_fn(th), start=True, stop=True))
        MM.wait(nc.vector)
        cx.v(nc.vector.tensor_copy(dst[:, th * 512:(th + 1) * 512], P0[:, :]))
        VE.wait(nc.tensor)


def _emit_ln(cx, T, layer, xnh, xnl, rb, sb_s):
    nc = cx.nc
    cx.push_scope()
    LD, GLD, MM, VE, SC, GP = (cx.sem[k] for k in ("LD", "GLD", "MM", "VE", "SC", "GP"))
    P0, P1, P2, P3 = (cx.ps[f"p{i}"] for i in range(4))
    xch = [cx.sbuf("s_lnx0", [128, TOK], F32),
           cx.sbuf("s_lnx1", [128, TOK], F32)]
    xsq = cx.sbuf("s_lnsq", [128, TOK], F32)
    stat = cx.sbuf("s_stat", [1, 3 * TOK], F32)
    lns2 = cx.sbuf("s_lns2", [1, TOK], F32)
    invb = cx.sbuf("s_invb", [128, TOK], F32)
    m2b = cx.sbuf("s_m2b", [128, TOK], F32)

    cx.sem["TRP"].wait(nc.sync)
    GLD.wait(nc.sync)
    GLD.wait(nc.tensor)

    mu = stat[0:1, 0:TOK]
    m2 = stat[0:1, 1 * TOK:2 * TOK]
    var = stat[0:1, 2 * TOK:3 * TOK]
    inv = var

    if layer == 0:
        LD.inc(nc.sync.dma_start(out=mu[:, :], in_=T["st_mu"][:, :]), 16)
        LD.inc(nc.sync.dma_start(out=var[:, :], in_=T["st_sq"][:, :]), 16)
        LD.inc(nc.sync.dma_start(out=lns2[0:1, :], in_=T["st_mx"][:, :]), 16)
        LD.wait(nc.vector)
        cx.v(nc.vector.tensor_tensor(m2[:, :], mu[:, :], mu[:, :], ALU.mult))
        cx.v(nc.vector.tensor_tensor(var[:, :], var[:, :], m2[:, :], ALU.subtract))
    else:
        for ch in range(NCH):
            buf = xch[ch % 2]
            if ch >= 2:
                MM.wait(nc.sync)
                SC.wait(nc.sync)
            LD.inc(nc.sync.dma_start(out=buf[:, :],
                                     in_=_apd(T["x_res"], ch * 128 * TOK,
                                              [[TOK, 128], [1, TOK]])), 16)
            LD.wait(nc.scalar)
            cx.s(nc.scalar.activation(xsq[:, :], buf[:, :], AF.Square, bias=0.0, scale=1.0))
            LD.wait(nc.tensor)
            SC.wait(nc.tensor)
            for th in range(2):
                MM.inc(nc.tensor.matmul(P0[0:1, :] if th == 0 else P1[0:1, :],
                                        cx.sb["s_ones"][:, :], buf[:, th * 512:(th + 1) * 512],
                                        start=(ch == 0), stop=(ch == NCH - 1)))
                MM.inc(nc.tensor.matmul(P2[0:1, :] if th == 0 else P3[0:1, :],
                                        cx.sb["s_ones"][:, :], xsq[:, th * 512:(th + 1) * 512],
                                        start=(ch == 0), stop=(ch == NCH - 1)))
        MM.wait(nc.vector)
        for th, (ps_s, ps_q) in enumerate([(P0, P2), (P1, P3)]):
            sl = slice(th * 512, (th + 1) * 512)
            cx.v(nc.vector.tensor_scalar(mu[:, sl], ps_s[0:1, :], 1.0 / D, None, ALU.mult))
            cx.v(nc.vector.tensor_scalar(var[:, sl], ps_q[0:1, :], 1.0 / D, None, ALU.mult))
        cx.v(nc.vector.tensor_tensor(m2[:, :], mu[:, :], mu[:, :], ALU.mult))
        cx.v(nc.vector.tensor_tensor(var[:, :], var[:, :], m2[:, :], ALU.subtract))

    cx.v(nc.vector.tensor_scalar(var[:, :], var[:, :], 1e-5, None, ALU.add))
    VE.wait(nc.scalar)
    cx.s(nc.scalar.activation(inv[:, :], var[:, :], AF.Sqrt, bias=0.0, scale=1.0))
    SC.wait(nc.vector)
    cx.v(nc.vector.reciprocal(inv[:, :], inv[:, :]))
    cx.v(nc.vector.tensor_tensor(m2[:, :], mu[:, :], inv[:, :], ALU.mult))
    _bcast_row(cx, lambda th: inv[:, th * 512:(th + 1) * 512], invb)
    _bcast_row(cx, lambda th: m2[:, th * 512:(th + 1) * 512], m2b)

    for ch in range(NCH):
        buf = xch[ch % 2]
        VE.wait(nc.sync)
        MM.wait(nc.sync)
        SC.wait(nc.sync)
        LD.inc(nc.sync.dma_start(out=buf[:, :],
                                 in_=_apd(T["x_res"], ch * 128 * TOK, [[TOK, 128], [1, TOK]])), 16)
        LD.wait(nc.vector)
        xn = xsq
        cx.v(nc.vector.scalar_tensor_tensor(xn[:, :], buf[:, :], 1.0, invb[:, :],
                                              ALU.mult, ALU.mult))
        cx.v(nc.vector.tensor_tensor(xn[:, :], xn[:, :], m2b[:, :], ALU.subtract))
        if layer == 0:
            VE.wait(nc.sync)
            cx.sem["TRP"].inc(nc.sync.dma_start(
                out=_apd(T["xn_hbm"], ch * 128 * TOK, [[TOK, 128], [1, TOK]]),
                in_=xn[:, :]), 16)
        cx.v(nc.vector.tensor_copy(xnh[:, ch, :], xn[:, :]))
        cx.v(nc.vector.tensor_tensor(xnl[:, ch, :], xn[:, :], xnh[:, ch, :], ALU.subtract))

    if layer == 0:
        amx = lns2[0:1, :]
        t = mu
        cx.v(nc.vector.tensor_tensor(t[:, :], amx[:, :], inv[:, :], ALU.mult))
        cx.v(nc.vector.tensor_scalar(sb_s[0:1, :], t[:, :], 1.0 / 127.0, 1e-5,
                                       ALU.mult, ALU.add))
        r0 = amx
        cx.v(nc.vector.reciprocal(r0[:, :], sb_s[0:1, :]))
        cx.v(nc.vector.tensor_tensor(t[:, :], sb_s[0:1, :], r0[:, :], ALU.mult))
        cx.v(nc.vector.tensor_scalar(t[:, :], t[:, :], -1.0, 2.0, ALU.mult, ALU.add))
        cx.v(nc.vector.tensor_tensor(r0[:, :], r0[:, :], t[:, :], ALU.mult))
        _bcast_row(cx, lambda th: r0[:, th * 512:(th + 1) * 512], rb)
    cx.pop_scope()


def _load_w_hilo(cx, T, name, col0, ncols, whi, wlo, w32, buf, key=None):
    nc = cx.nc
    LD, GLD, GP, MM = cx.sem["LD"], cx.sem["GLD"], cx.sem["GP"], cx.sem["MM"]
    if not hasattr(cx, "wmk"):
        cx.wmk = {}
        cx.wmk_gp = {}
    wm = cx.wmk.setdefault(key or name, [0, 0])
    wmg = cx.wmk_gp.setdefault(key or name, [0, 0])
    SLD = cx.sem["SLD"]
    wt = T[name]
    wcols = wt.shape[1]
    src = _apd(wt, col0, [[wcols, 128], [wcols * 128, 16], [1, ncols]])
    srcA = _apd(wt, col0, [[wcols, 128], [wcols * 128, 8], [1, ncols]])
    srcB = _apd(wt, col0 + 8 * wcols * 128, [[wcols, 128], [wcols * 128, 8], [1, ncols]])
    MM.wait(nc.gpsimd, wm[buf])
    MM.wait(nc.sync, wm[buf])
    MM.wait(nc.scalar, wm[buf])
    GLD.inc(nc.gpsimd.dma_start(out=whi[:, buf, :, :], in_=src), 16)
    GP.wait(nc.sync, wmg[buf])
    GP.wait(nc.scalar, wmg[buf])
    LD.inc(nc.sync.dma_start(out=w32[:, buf, 0:8, 0:ncols], in_=srcA), 16)
    SLD.inc(nc.scalar.dma_start(out=w32[:, buf, 8:16, 0:ncols], in_=srcB), 16)
    GLD.wait(nc.gpsimd)
    LD.wait(nc.gpsimd)
    SLD.wait(nc.gpsimd)
    wmg[buf] = cx.g(nc.gpsimd.tensor_tensor(wlo[:, buf, :, :],
                                            w32[:, buf, :, 0:ncols],
                                            whi[:, buf, :, :], ALU.subtract))


def _wmark(cx, key, buf):
    cx.wmk.setdefault(key, [0, 0])[buf] = cx.sem["MM"].n



def _load_w_hilo2(cx, T, name, col0, ncols, whi, wlo, w32):
    nc = cx.nc
    LD, GLD, GP, MM = cx.sem["LD"], cx.sem["GLD"], cx.sem["GP"], cx.sem["MM"]
    SLD = cx.sem["SLD"]
    wt = T[name]
    wcols = wt.shape[1]
    src = _apd(wt, col0, [[wcols, 128], [wcols * 128, 16], [1, ncols]])
    srcA = _apd(wt, col0, [[wcols, 128], [wcols * 128, 8], [1, ncols]])
    srcB = _apd(wt, col0 + 8 * wcols * 128, [[wcols, 128], [wcols * 128, 8], [1, ncols]])
    MM.wait(nc.gpsimd)
    MM.wait(nc.sync)
    MM.wait(nc.scalar)
    GLD.inc(nc.gpsimd.dma_start(out=whi[:, :, :], in_=src), 16)
    GP.wait(nc.sync)
    GP.wait(nc.scalar)
    LD.inc(nc.sync.dma_start(out=w32[:, 0:8, 0:ncols], in_=srcA), 16)
    SLD.inc(nc.scalar.dma_start(out=w32[:, 8:16, 0:ncols], in_=srcB), 16)
    GLD.wait(nc.gpsimd)
    LD.wait(nc.gpsimd)
    SLD.wait(nc.gpsimd)
    cx.g(nc.gpsimd.tensor_tensor(wlo[:, :, :],
                                 w32[:, :, 0:ncols],
                                 whi[:, :, :], ALU.subtract))


def _rotary_evac(cx, ps, psB, th, qrot, q_sb):
    nc = cx.nc
    MM, VE = cx.sem["MM"], cx.sem["VE"]
    cos = cx.sb["s_cos"]
    sin = cx.sb["s_sin"]
    ta = cx.sb["s_rota"]
    sl = slice(th * 512, (th + 1) * 512)
    MM.wait(nc.vector)
    cx.v(nc.vector.tensor_copy(q_sb[:, :], ps[:, :]))
    VE.wait(nc.tensor)
    MM.inc(nc.tensor.matmul(psB[:, :], cx.sb["s_swap"][:, :], q_sb[:, :],
                            start=True, stop=True))
    cx.v(nc.vector.tensor_tensor(qrot[:, :], q_sb[:, :], cos[:, sl], ALU.mult))
    MM.wait(nc.vector)
    cx.v(nc.vector.scalar_tensor_tensor(ta[:, :], psB[:, :], 1.0, sin[:, sl],
                                        ALU.mult, ALU.mult))
    cx.v(nc.vector.tensor_tensor(qrot[:, :], qrot[:, :], ta[:, :], ALU.add))


def _emit_proj0(cx, T, xnh, xnl, dumps, dump_t):
    nc = cx.nc
    cx.push_scope()
    LD, GLD, MM, VE, SC, GP, CC = (cx.sem[k] for k in
                                   ("LD", "GLD", "MM", "VE", "SC", "GP", "CC"))
    P = [cx.ps[f"p{i}"] for i in range(6)]
    cx.push_scope()
    whi = cx.sbuf("s_pwhi", [128, 2, 16, 128], BF16)
    wlo = cx.sbuf("s_pwlo", [128, 2, 16, 128], BF16)
    w32 = cx.sbuf("s_pw32", [128, 2, 16, 256], F32)
    qrot = cx.sb.get("s_qrot") or cx.sbuf("s_qrot", [128, 512], F32)
    qh = cx.sb.get("s_qh") or cx.sbuf("s_qh", [128, 512], BF16)
    ql = cx.sb.get("s_ql") or cx.sbuf("s_ql", [128, 512], BF16)
    cx.sbuf("s_rota", [128, 512], F32) if "s_rota" not in cx.sb else None
    cx.sbuf("s_qsb", [128, 512], F32) if "s_qsb" not in cx.sb else None

    def xnh_sl(ic, th):
        return _apd(xnh, ic * TOK + th * 512, [[NCH * TOK, 128], [1, 512]])

    def xnl_sl(ic, th):
        return _apd(xnl, ic * TOK + th * 512, [[NCH * TOK, 128], [1, 512]])

    VE.wait(nc.tensor)
    for oc in range(18):
        buf = oc % 2
        _load_w_hilo(cx, T, "w_qkv", oc * 128, 128, whi, wlo, w32, buf, key="qkv")
        GP.wait(nc.tensor)
        for th in range(2):
            ps = P[(oc * 2 + th) % 2]
            psB = P[2 + (oc * 2 + th) % 2]
            VE.wait(nc.tensor)
            for ic in range(NCH):
                MM.inc(nc.tensor.matmul(ps[:, :], whi[:, buf, ic, :], xnh_sl(ic, th),
                                        start=(ic == 0), stop=False))
                MM.inc(nc.tensor.matmul(ps[:, :], whi[:, buf, ic, :], xnl_sl(ic, th),
                                        start=False, stop=False))
            for ic in range(NCH):
                MM.inc(nc.tensor.matmul(ps[:, :], wlo[:, buf, ic, :], xnh_sl(ic, th),
                                        start=False, stop=(ic == NCH - 1)))
            MM.wait(nc.vector)
            _rotary_evac(cx, ps, psB, th, qrot, cx.sb["s_qsb"])
            LD.wait(nc.vector)
            cx.v(nc.vector.tensor_copy(qh[:, :], qrot[:, :]))
            cx.v(nc.vector.tensor_tensor(ql[:, :], qrot[:, :], qh[:, :], ALU.subtract))
            VE.wait(nc.sync)
            if oc < 16:
                for half, t in ((0, qh), (1, ql)):
                    LD.inc(nc.sync.dma_start(
                        out=_apd(T["q_spill"], (half * 2048 + oc * 128) * TOK + th * 512,
                                 [[TOK, 128], [1, 512]]),
                        in_=t[:, :]), 16)
            else:
                kvh = oc - 16
                for half, t in ((0, qh), (1, ql)):
                    LD.inc(nc.sync.dma_start(
                        out=_apd(T["ag2_in"], (kvh * 256 + half * 128) * 1024 + th * 512,
                                 [[1024, 128], [1, 512]]),
                        in_=t[:, :]), 16)
        _wmark(cx, "qkv", buf)

    cx.pop_scope()
    wvhi = cx.sb.get("s_wvhi") or cx.sbuf("s_wvhi", [128, 16, 256], BF16)
    wvlo = cx.sb.get("s_wvlo") or cx.sbuf("s_wvlo", [128, 16, 256], BF16)
    wv32 = cx.sb.get("s_wv32") or cx.sbuf("s_wv32", [128, 16, 256], F32)
    _load_w_hilo2(cx, T, "w_qkv", 2304, 256, wvhi, wvlo, wv32)
    GP.wait(nc.tensor)
    for tt in range(NSLOT):
        ps = P[4 + tt % 2]
        VE.wait(nc.tensor)
        for ic in range(NCH):
            xh = _apd(xnh, ic * TOK + tt * 128, [[NCH * TOK, 128], [1, 128]])
            xl = _apd(xnl, ic * TOK + tt * 128, [[NCH * TOK, 128], [1, 128]])
            MM.inc(nc.tensor.matmul(ps[:, 0:256], xh, wvhi[:, ic, :],
                                    start=(ic == 0), stop=False))
            MM.inc(nc.tensor.matmul(ps[:, 0:256], xl, wvhi[:, ic, :],
                                    start=False, stop=False))
            MM.inc(nc.tensor.matmul(ps[:, 0:256], xh, wvlo[:, ic, :],
                                    start=False, stop=(ic == NCH - 1)))
        MM.wait(nc.vector)
        LD.wait(nc.vector)
        cx.v(nc.vector.tensor_copy(qh[:, 0:256], ps[:, 0:256]))
        cx.v(nc.vector.tensor_tensor(ql[:, 0:256], ps[:, 0:256], qh[:, 0:256], ALU.subtract))
        VE.wait(nc.sync)
        for kvh in range(2):
            for half, t in ((0, qh), (1, ql)):
                LD.inc(nc.sync.dma_start(
                    out=_apd(T["ag2_in"], (512 + (kvh * 2 + half) * 128) * 1024 + tt * 128 * 128,
                             [[128, 128], [1, 128]]),
                    in_=t[:, kvh * 128:(kvh + 1) * 128]), 16)

    LD.wait(nc.gpsimd)
    CC.inc(nc.gpsimd.collective_compute(
        "AllGather", ALU.bypass,
        replica_groups=[[0, 1, 2, 3], [4, 5, 6, 7]],
        ins=[T["ag2_in"].ap().opt()], outs=[T["ag2_out"].ap().opt()]))
    if "d_q0" in dumps:
        LD.wait(nc.sync)
        LD.inc(nc.sync.dma_start(out=dump_t["d_q0"][:, :], in_=T["q_spill"][:, :]), 16)
        LD.wait(nc.sync)
    if "d_ag2" in dumps:
        CC.wait(nc.sync)
        LD.inc(nc.sync.dma_start(out=dump_t["d_ag2"][:, :], in_=T["ag2_out"][:, :]), 16)
        LD.wait(nc.sync)
    cx.pop_scope()


def _part_max_tree(cx, hmx, n, ps):
    nc = cx.nc
    MM, VE = cx.sem["MM"], cx.sem["VE"]
    rotm = cx.sb["s_rotm"]
    for j in range(7):
        VE.wait(nc.tensor)
        MM.inc(nc.tensor.matmul(ps[:, 0:n], rotm[:, j, :], hmx[:, 0:n],
                                start=True, stop=True))
        MM.wait(nc.vector)
        cx.v(nc.vector.tensor_tensor(hmx[:, 0:n], hmx[:, 0:n], ps[:, 0:n], ALU.max))


def _emit_ffn0_xq(cx, T, xnh, xnl, rb, sb_s, xq, u1g, u1u, dumps, dump_t):
    nc = cx.nc
    LD, VE = cx.sem["LD"], cx.sem["VE"]
    wsc = cx.sb["s_wsc"]
    xsc = cx.sbuf("s_xqsc", [128, TOK], F32)
    urow = cx.sbuf("s_urow", [1, TOK], F32)
    cx.sem["TRP"].wait(nc.sync)
    for ch in range(NCH):
        VE.wait(nc.sync)
        LD.inc(nc.sync.dma_start(
            out=xsc[:, :],
            in_=_apd(T["xn_hbm"], ch * 128 * TOK, [[TOK, 128], [1, TOK]])), 16)
        LD.wait(nc.vector)
        cx.v(nc.vector.tensor_tensor(xsc[:, :], xsc[:, :], rb[:, :], ALU.mult))
        cx.v(nc.vector.tensor_scalar(xq[:, ch, :], xsc[:, :], MAGIC, MAGIC, ALU.add, ALU.subtract))
    if "d_xq" in dumps:
        VE.wait(nc.sync)
        LD.inc(nc.sync.dma_start(
            out=_apd(dump_t["d_xq"], 0, [[TOK, 128], [128 * TOK, NCH], [1, TOK]]),
            in_=_apd(xq, 0, [[NCH * TOK, 128], [TOK, NCH], [1, TOK]])), 16)
        LD.wait(nc.sync)
    cx.v(nc.vector.tensor_scalar(urow[0:1, :], sb_s[0:1, :],
                                 wsc[0:1, 0:1], None, ALU.mult))
    _bcast_row(cx, lambda th: urow[0:1, th * 512:(th + 1) * 512], u1g)
    cx.v(nc.vector.tensor_scalar(urow[0:1, :], sb_s[0:1, :],
                                 wsc[0:1, 1:2], None, ALU.mult))
    _bcast_row(cx, lambda th: urow[0:1, th * 512:(th + 1) * 512], u1u)


def _emit_ffn0_main(cx, T, xq, u1g, u1u, dumps, dump_t):
    nc = cx.nc
    cx.push_scope()
    LD, GLD, MM, VE, SC, GP, CC = (cx.sem[k] for k in
                                   ("LD", "GLD", "MM", "VE", "SC", "GP", "CC"))
    P = [cx.ps[f"p{i}"] for i in range(6)]
    hbuf = cx.sbuf("s_h", [128, 64, TFF], F32)
    hq = cx.sbuf("s_hq", [128, 64, TFF], BF16)
    wg = cx.sbuf("s_fwg", [128, 2, 16, 128], BF16)
    wu = cx.sbuf("s_fwu", [128, 2, 16, 128], BF16)
    wd = cx.sbuf("s_fwd", [128, 64, 128], BF16)
    u2b = cx.sbuf("s_u2b", [128, TFF], F32)
    r2b = cx.sbuf("s_r2b", [128, TFF], F32)
    hmx = cx.sbuf("s_hmx", [128, TFF], F32)
    tg = cx.sbuf("s_tg", [128, TFF], F32)
    gs = cx.sbuf("s_gs", [128, TFF], F32)
    tu = cx.sbuf("s_tu", [128, TFF], F32)
    fout = cx.sbuf("s_fout", [128, 2, TFF], F32)
    wsc = cx.sb["s_wsc"]

    CC.wait(nc.sync)
    CC.wait(nc.gpsimd)
    CC.wait(nc.tensor)

    def gate_tile_ap(oc, ic):
        s = oc // 8
        off = (s * 12288 + 0) * 1024 + (ic * 128) * 1024 + (oc % 8) * 128
        return _apd(T["ag1_out"], off, [[1024, 128], [1, 128]])

    def up_tile_ap(oc, ic):
        s = oc // 8
        off = (s * 12288 + 2048) * 1024 + (ic * 128) * 1024 + (oc % 8) * 128
        return _apd(T["ag1_out"], off, [[1024, 128], [1, 128]])

    def down_tile_ap(od, ic):
        s = od // 2
        off = (s * 12288 + 4096) * 1024 + (ic * 128) * 256 + (od % 2) * 128
        return _apd(T["ag1_out"], off, [[256, 128], [1, 128]])

    fo_ve0 = [0, 0]
    for slab in range(NSLAB):
        csl = slice(slab * TFF, (slab + 1) * TFF)
        VE.wait(nc.gpsimd)
        cx.g(nc.gpsimd.memset(hmx[:, :], 0.0))
        GP.wait(nc.vector)
        gu_mm = [0, 0]
        gu_ve = [fo_ve0[0], fo_ve0[1]]
        for oc in range(64):
            buf = oc % 2
            MM.wait(nc.sync, gu_mm[buf])
            LD.inc(nc.sync.dma_start(
                out=_apd(wg, buf * 16 * 128, [[2 * 16 * 128, 128], [128, 16], [1, 128]]),
                in_=_apd(T["ag1_out"], (oc // 8 * 12288) * 1024 + (oc % 8) * 128,
                         [[1024, 128], [128 * 1024, 16], [1, 128]])), 16)
            LD.inc(nc.sync.dma_start(
                out=_apd(wu, buf * 16 * 128, [[2 * 16 * 128, 128], [128, 16], [1, 128]]),
                in_=_apd(T["ag1_out"], (oc // 8 * 12288 + 2048) * 1024 + (oc % 8) * 128,
                         [[1024, 128], [128 * 1024, 16], [1, 128]])), 16)
            LD.wait(nc.tensor)
            psA, psB = P[oc % 2], P[2 + oc % 2]
            VE.wait(nc.tensor, gu_ve[oc % 2])
            for ic in range(NCH):
                MM.inc(nc.tensor.matmul(psA[:, 0:TFF], wg[:, buf, ic, :],
                                        _apd(xq, ic * TOK + slab * TFF, [[NCH * TOK, 128], [1, TFF]]),
                                        start=(ic == 0), stop=(ic == NCH - 1)))
            for ic in range(NCH):
                MM.inc(nc.tensor.matmul(psB[:, 0:TFF], wu[:, buf, ic, :],
                                        _apd(xq, ic * TOK + slab * TFF, [[NCH * TOK, 128], [1, TFF]]),
                                        start=(ic == 0), stop=(ic == NCH - 1)))
            MM.wait(nc.vector)
            MM.wait(nc.scalar)
            cx.v(nc.vector.scalar_tensor_tensor(tg[:, :], psA[:, 0:TFF], 1.0, u1g[:, csl],
                                                ALU.mult, ALU.mult))
            VE.wait(nc.scalar)
            cx.s(nc.scalar.activation(gs[:, :], tg[:, :], AF.Silu, bias=0.0, scale=1.0))
            gu_ve[oc % 2] = cx.v(nc.vector.scalar_tensor_tensor(
                tu[:, :], psB[:, 0:TFF], 1.0, u1u[:, csl], ALU.mult, ALU.mult))
            gu_mm[buf] = MM.n
            SC.wait(nc.vector)
            cx.v(nc.vector.tensor_tensor(hbuf[:, oc, :], tu[:, :], gs[:, :], ALU.mult))
            VE.wait(nc.scalar)
            cx.s(nc.scalar.activation(tg[:, :], hbuf[:, oc, :], AF.Abs, bias=0.0, scale=1.0))
            SC.wait(nc.vector)
            cx.v(nc.vector.tensor_tensor(hmx[:, :], hmx[:, :], tg[:, :], ALU.max))
        if "d_h" in dumps and slab == 0:
            VE.wait(nc.sync)
            LD.inc(nc.sync.dma_start(
                out=_apd(dump_t["d_h"], 0, [[TFF, 128], [128 * TFF, 64], [1, TFF]]),
                in_=_apd(hbuf, 0, [[64 * TFF, 128], [TFF, 64], [1, TFF]])), 16)
            LD.wait(nc.sync)
        GP.wait(nc.vector)
        _part_max_tree(cx, hmx, TFF, P[5])
        s2b = tg
        cx.v(nc.vector.tensor_scalar(s2b[:, :], hmx[:, :], 1.0 / 127.0, 1e-5,
                                     ALU.mult, ALU.add))
        cx.v(nc.vector.reciprocal(r2b[:, :], s2b[:, :]))
        cx.v(nc.vector.tensor_tensor(hmx[:, :], s2b[:, :], r2b[:, :], ALU.mult))
        cx.v(nc.vector.tensor_scalar(hmx[:, :], hmx[:, :], -1.0, 2.0, ALU.mult, ALU.add))
        cx.v(nc.vector.tensor_tensor(r2b[:, :], r2b[:, :], hmx[:, :], ALU.mult))
        cx.v(nc.vector.tensor_scalar(u2b[:, :], s2b[:, :], cx.sb["s_wscb"][:, 2:3],
                                     None, ALU.mult))
        ve_hq = []
        for oc in range(64):
            cx.v(nc.vector.tensor_tensor(tg[:, :], hbuf[:, oc, :], r2b[:, :], ALU.mult))
            ve_hq.append(cx.v(nc.vector.tensor_scalar(hq[:, oc, :], tg[:, :], MAGIC, MAGIC,
                                         ALU.add, ALU.subtract)))
        if "d_hq" in dumps and slab == 0:
            VE.wait(nc.sync)
            LD.inc(nc.sync.dma_start(
                out=_apd(dump_t["d_hq"], 0, [[TFF, 128], [128 * TFF, 64], [1, TFF]]),
                in_=_apd(hq, 0, [[64 * TFF, 128], [TFF, 64], [1, TFF]])), 16)
            LD.wait(nc.sync)
        SLD = cx.sem["SLD"]
        for od in range(NCH):
            MM.wait(nc.sync)
            MM.wait(nc.scalar)
            LD.inc(nc.sync.dma_start(
                out=wd[:, 0:32, :],
                in_=_apd(T["ag1_out"],
                         (od // 2 * 12288 + 4096) * 1024 + (od % 2) * 128,
                         [[256, 128], [128 * 256, 32], [1, 128]])), 16)
            ldh0 = LD.n
            SLD.inc(nc.scalar.dma_start(
                out=wd[:, 32:64, :],
                in_=_apd(T["ag1_out"],
                         (od // 2 * 12288 + 4096) * 1024 + 32 * 128 * 256 + (od % 2) * 128,
                         [[256, 128], [128 * 256, 32], [1, 128]])), 16)
            ldh1 = SLD.n
            ps = P[od % 2]
            VE.wait(nc.tensor, fo_ve0[od % 2])
            for ic in range(64):
                if od == 0 and ic % 8 == 0:
                    VE.wait(nc.tensor, ve_hq[ic + 7])
                if ic == 0:
                    LD.wait(nc.tensor, ldh0)
                if ic == 32:
                    SLD.wait(nc.tensor, ldh1)
                MM.inc(nc.tensor.matmul(ps[:, 0:TFF], wd[:, ic, :], hq[:, ic, :],
                                        start=(ic == 0), stop=(ic == 63)))
            MM.wait(nc.vector)
            GLD.wait(nc.vector)
            fo_ve0[od % 2] = cx.v(nc.vector.scalar_tensor_tensor(fout[:, od % 2, :], ps[:, 0:TFF], 1.0,
                                                u2b[:, :], ALU.mult, ALU.mult))
            VE.wait(nc.gpsimd)
            GLD.inc(nc.gpsimd.dma_start(
                out=_apd(T["x_res"], od * 128 * TOK + slab * TFF, [[TOK, 128], [1, TFF]]),
                in_=fout[:, od % 2, :], accum_op=ALU.add), 16)
    cx.pop_scope()


def _emit_attn0(cx, T, attn_sb, dumps, dump_t):
    nc = cx.nc
    cx.push_scope()
    LD, GLD, MM, VE, SC, GP, CC = (cx.sem[k] for k in
                                   ("LD", "GLD", "MM", "VE", "SC", "GP", "CC"))
    P = [cx.ps[f"p{i}"] for i in range(6)]
    kb = cx.sbuf("s_kb", [128, 2, 2, 32, 128], BF16)
    vb = cx.sbuf("s_vb", [128, 2, 2, 32, 128], BF16)
    qblk = cx.sbuf("s_qblk", [128, 2, 2, 4, 128], BF16)
    pf = cx.sbuf("s_pf", [128, 2, 512], F32)
    ph = cx.sbuf("s_ph", [128, 2, 512], BF16)
    pl = cx.sbuf("s_pl", [128, 2, 512], BF16)
    ts_ = cx.sbuf("s_ts", [128, 2, 512], F32)
    rrow = cx.sbuf("s_rrow", [1, 512], F32)
    rbt = cx.sbuf("s_rbt", [128, 512], F32)
    ones20 = cx.sbuf("s_ones20", [128, 1], F32)
    cx.g(nc.gpsimd.memset(ones20[:, :], 1048576.0))
    bmask = cx.sb["s_bmask"]

    SLD = cx.sem["SLD"]
    CC.wait(nc.sync)
    CC.wait(nc.scalar)
    for kvh in range(2):
        for half in range(2):
            for gp_ in range(4):
                LD.inc(nc.sync.dma_start(
                    out=_apd(kb, (kvh * 2 + half) * 32 * 128 + gp_ * 128,
                             [[2 * 2 * 32 * 128, 128], [4 * 128, 8], [1, 128]]),
                    in_=_apd(T["ag2_out"], (gp_ * 1024 + kvh * 256 + half * 128) * 1024,
                             [[1024, 128], [128, 8], [1, 128]])), 16)
                SLD.inc(nc.scalar.dma_start(
                    out=_apd(vb, (kvh * 2 + half) * 32 * 128 + gp_ * 128,
                             [[2 * 2 * 32 * 128, 128], [4 * 128, 8], [1, 128]]),
                    in_=_apd(T["ag2_out"], (gp_ * 1024 + 512 + (kvh * 2 + half) * 128) * 1024,
                             [[128, 128], [128 * 128, 8], [1, 128]])), 16)
    LD.wait(nc.tensor)
    SLD.wait(nc.tensor)

    qb_mm = [0, 0]
    bank_tanh = [0, 0]
    ld_qb = [0, 0]
    blocks = [(jj, hh) for jj in range(NSLOT) for hh in range(4)]

    def _qload(b):
        jj, hh = blocks[b]
        sl = b % 2
        MM.wait(nc.sync, qb_mm[sl])
        for half in range(2):
            LD.inc(nc.sync.dma_start(
                out=qblk[:, sl, half, :, :],
                in_=_apd(T["q_spill"], (half * 2048 + hh * 512) * TOK + jj * 128,
                         [[TOK, 128], [128 * TOK, 4], [1, 128]]), ), 16)
        ld_qb[sl] = LD.n

    blk_i = 0
    for j in range(NSLOT):
        for hg in range(4):
            kvh = hg // 2
            sl_q = blk_i % 2
            if blk_i == 0:
                _qload(0)
            if blk_i + 1 < len(blocks):
                _qload(blk_i + 1)
            LD.wait(nc.tensor, ld_qb[sl_q])
            nk = 4 * j + 4
            sc_tanh = {}
            ve_split = {}
            mm_pv = {}

            def scores0(k, sl_q=sl_q, kvh=kvh, j=j):
                ps_s = P[k % 2]
                SC.wait(nc.tensor, sc_tanh[k - 2] if k >= 2 else bank_tanh[k % 2])
                MM.inc(nc.tensor.matmul(ps_s[:, :], kb[:, kvh, 0, k, :],
                                        qblk[:, sl_q, 0, :, :], start=True, stop=False))
                MM.inc(nc.tensor.matmul(ps_s[:, :], kb[:, kvh, 0, k, :],
                                        qblk[:, sl_q, 1, :, :], start=False, stop=False))
                mm_sc = MM.inc(nc.tensor.matmul(ps_s[:, :], kb[:, kvh, 1, k, :],
                                                qblk[:, sl_q, 0, :, :], start=False, stop=True))
                MM.wait(nc.scalar, mm_sc)
                if k >= 2:
                    VE.wait(nc.scalar, ve_split[k - 2])
                    MM.wait(nc.scalar, mm_pv[k - 2])
                    MM.wait(nc.vector, mm_pv[k - 2])
                sc_tanh[k] = cx.s(nc.scalar.activation(
                    ts_[:, k % 2, :], ps_s[:, :], AF.Tanh, bias=0.0, scale=SCALE0 / CAP))
                bank_tanh[k % 2] = sc_tanh[k]
                cx.s(nc.scalar.activation(
                    pf[:, k % 2, :], ts_[:, k % 2, :], AF.Exp, bias=0.0, scale=CAP))
                SC.wait(nc.vector)
                b = k - 4 * j
                if b >= 0:
                    cx.v(nc.vector.tensor_tensor(
                        pf[:, k % 2, :], pf[:, k % 2, :],
                        _apd(bmask, b * 128, [[512, 128], [0, 4], [1, 128]]), ALU.mult))
                cx.v(nc.vector.tensor_copy(ph[:, k % 2, :], pf[:, k % 2, :]))
                ve_split[k] = cx.v(nc.vector.tensor_tensor(
                    pl[:, k % 2, :], pf[:, k % 2, :], ph[:, k % 2, :], ALU.subtract))

            def pvrow(k, kvh=kvh, nk=nk):
                VE.wait(nc.tensor, ve_split[k])
                MM.inc(nc.tensor.matmul(P[3][0:1, :], ones20[:, :], pf[:, k % 2, :],
                                        start=(k == 0), stop=(k == nk - 1)))
                MM.inc(nc.tensor.matmul(P[2][:, :], vb[:, kvh, 0, k, :], ph[:, k % 2, :],
                                        start=(k == 0), stop=False))
                MM.inc(nc.tensor.matmul(P[2][:, :], vb[:, kvh, 0, k, :], pl[:, k % 2, :],
                                        start=False, stop=False))
                mm_pv[k] = MM.inc(nc.tensor.matmul(P[2][:, :], vb[:, kvh, 1, k, :],
                                                   ph[:, k % 2, :],
                                                   start=False, stop=(k == nk - 1)))

            scores0(0)
            for k in range(1, nk):
                scores0(k)
                pvrow(k - 1)
            pvrow(nk - 1)
            qb_mm[sl_q] = MM.n
            blk_i += 1
            MM.wait(nc.vector)
            cx.v(nc.vector.reciprocal(rrow[0:1, :], P[3][0:1, :]))
            cx.v(nc.vector.tensor_scalar(rrow[0:1, :], rrow[0:1, :], 1048576.0, None, ALU.mult))
            VE.wait(nc.tensor)
            MM.inc(nc.tensor.matmul(P[4][:, :], cx.sb["s_onesr"][:, :], rrow[0:1, :],
                                    start=True, stop=True))
            MM.wait(nc.vector)
            cx.v(nc.vector.tensor_copy(rbt[:, :], P[4][:, :]))
            cx.v(nc.vector.scalar_tensor_tensor(
                _apd(attn_sb, (hg * 4) * TOK + j * 128, [[NCH * TOK, 128], [TOK, 4], [1, 128]]),
                P[2][:, :], 1.0, rbt[:, :], ALU.mult, ALU.mult))
            VE.wait(nc.tensor)
    if "d_attn0" in dumps:
        VE.wait(nc.sync)
        LD.inc(nc.sync.dma_start(
            out=_apd(dump_t["d_attn0"], 0, [[TOK, 128], [128 * TOK, NCH], [1, TOK]]),
            in_=_apd(attn_sb, 0, [[NCH * TOK, 128], [TOK, NCH], [1, TOK]])), 16)
        LD.wait(nc.sync)
    cx.pop_scope()


def _emit_wo(cx, T, attn_sb, w, bias, layer):
    nc = cx.nc
    cx.push_scope()
    LD, GLD, MM, VE, SC, GP = (cx.sem[k] for k in
                               ("LD", "GLD", "MM", "VE", "SC", "GP"))
    P = [cx.ps[f"p{i}"] for i in range(6)]
    abh = cx.sbuf("s_abh", [128, NCH, TOK], BF16)
    abl = cx.sbuf("s_abl", [128, NCH, TOK], BF16)
    whi = cx.sbuf("s_owhi", [128, 2, 16, 128], BF16)
    wlo = cx.sbuf("s_owlo", [128, 2, 16, 128], BF16)
    w32 = cx.sbuf("s_ow32", [128, 2, 16, 256], F32)
    fout = cx.sbuf("s_ofout", [128, 2, 512], F32)
    if bias is not None:
        bo = cx.sbuf("s_bo", [128, NCH], F32)
        LD.inc(nc.sync.dma_start(
            out=bo[:, :], in_=_apd(T[bias], 0, [[1, 128], [128, NCH]]),
            allow_slow_non_contiguous=True), 16)
        LD.wait(nc.vector)

    for ch in range(NCH):
        cx.v(nc.vector.tensor_copy(abh[:, ch, :], attn_sb[:, ch, :]))
        cx.v(nc.vector.tensor_tensor(abl[:, ch, :], attn_sb[:, ch, :], abh[:, ch, :],
                                     ALU.subtract))
    for oc in range(NCH):
        buf = oc % 2
        _load_w_hilo(cx, T, w, oc * 128, 128, whi, wlo, w32, buf, key=w)
        GP.wait(nc.tensor)
        for th in range(2):
            ps = P[(oc * 2 + th) % 2]
            VE.wait(nc.tensor)
            for ic in range(NCH):
                MM.inc(nc.tensor.matmul(ps[:, :], whi[:, buf, ic, :],
                                        _apd(abh, ic * TOK + th * 512, [[NCH * TOK, 128], [1, 512]]),
                                        start=(ic == 0), stop=False))
                MM.inc(nc.tensor.matmul(ps[:, :], whi[:, buf, ic, :],
                                        _apd(abl, ic * TOK + th * 512, [[NCH * TOK, 128], [1, 512]]),
                                        start=False, stop=False))
            for ic in range(NCH):
                MM.inc(nc.tensor.matmul(ps[:, :], wlo[:, buf, ic, :],
                                        _apd(abh, ic * TOK + th * 512, [[NCH * TOK, 128], [1, 512]]),
                                        start=False, stop=(ic == NCH - 1)))
            MM.wait(nc.vector)
            GLD.wait(nc.vector)
            if bias is not None:
                cx.v(nc.vector.tensor_scalar(fout[:, th, :], ps[:, :],
                                             bo[:, oc:oc + 1], None, ALU.add))
            else:
                cx.v(nc.vector.tensor_copy(fout[:, th, :], ps[:, :]))
            VE.wait(nc.gpsimd)
            GLD.inc(nc.gpsimd.dma_start(
                out=_apd(T["x_res"], oc * 128 * TOK + th * 512, [[TOK, 128], [1, 512]]),
                in_=fout[:, th, :], accum_op=ALU.add), 16)
        _wmark(cx, w, buf)
    cx.pop_scope()


def _emit_proj1(cx, T, xnh, xnl, dumps, dump_t):
    nc = cx.nc
    cx.push_scope()
    LD, GLD, MM, VE, SC, GP, CC = (cx.sem[k] for k in
                                   ("LD", "GLD", "MM", "VE", "SC", "GP", "CC"))
    P = [cx.ps[f"p{i}"] for i in range(6)]
    bias4 = cx.sbuf("s_bias4", [128, 4, NCH], F32)
    dec_b = cx.sbuf("s_decb", [128, NCH], F32)
    decn_b = cx.sbuf("s_decnb", [128, NCH], F32)
    lnsc = cx.sbuf("s_lnsc", [128, 1], F32)
    pos_b = cx.sbuf("s_posb", [128, TOK], F32)
    bv_b = cx.sbuf("s_bvb", [128, D], F32)
    posr = cx.sbuf("s_posr", [1, TOK], F32)
    decr = cx.sbuf("s_decr", [1, NH], F32)
    bvr = cx.sbuf("s_bvr", [1, D], F32)
    sg = cx.sbuf("s_sg", [128, NH, 8], F32)

    for i, nm in enumerate(("b_q1", "b_k1", "b_v1", "b_g1")):
        LD.inc(nc.sync.dma_start(
            out=bias4[:, i, :], in_=_apd(T[nm], 0, [[1, 128], [128, NCH]]),
            allow_slow_non_contiguous=True), 16)
    LD.inc(nc.sync.dma_start(out=posr[0:1, :], in_=T["pos2048"][:, :]), 16)
    LD.inc(nc.sync.dma_start(out=decr[0:1, :], in_=T["decay"][:, :]), 16)
    LD.inc(nc.sync.dma_start(out=bvr[0:1, :], in_=T["b_v1"][:, :]), 16)
    cx.g(nc.gpsimd.memset(lnsc[:, :], LNLOG_SCALE0))
    LD.wait(nc.tensor)
    MM.inc(nc.tensor.matmul(P[0][:, 0:NH], cx.sb["s_onesr"][:, :], decr[0:1, :],
                            start=True, stop=True))
    MM.wait(nc.vector)
    cx.v(nc.vector.tensor_copy(dec_b[:, :], P[0][:, 0:NH]))
    cx.v(nc.vector.tensor_scalar(decn_b[:, :], dec_b[:, :], -1.0, None, ALU.mult))
    _bcast_row(cx, lambda th: posr[0:1, th * 512:(th + 1) * 512], pos_b)
    for th in range(4):
        VE.wait(nc.tensor)
        MM.inc(nc.tensor.matmul(P[0][:, :], cx.sb["s_onesr"][:, :],
                                bvr[0:1, th * 512:(th + 1) * 512], start=True, stop=True))
        MM.wait(nc.vector)
        cx.v(nc.vector.tensor_copy(bv_b[:, th * 512:(th + 1) * 512], P[0][:, :]))
    VE.wait(nc.tensor)

    cx.push_scope()
    whi = cx.sbuf("s_pwhi1", [128, 2, 16, 128], BF16)
    wlo = cx.sbuf("s_pwlo1", [128, 2, 16, 128], BF16)
    w32 = cx.sbuf("s_pw321", [128, 2, 16, 256], F32)
    bt = cx.sbuf("s_bt", [128, TOK], F32)
    qrot = cx.sbuf("s_qrot1", [128, 512], F32)
    qh = cx.sbuf("s_qh1", [128, 512], BF16)
    ql = cx.sbuf("s_ql1", [128, 512], BF16)

    def xnh_sl(ic, th, n=512):
        return _apd(xnh, ic * TOK + th * 512, [[NCH * TOK, 128], [1, n]])

    def xnl_sl(ic, th, n=512):
        return _apd(xnl, ic * TOK + th * 512, [[NCH * TOK, 128], [1, n]])

    def proj_mms(ps, buf, th):
        for ic in range(NCH):
            MM.inc(nc.tensor.matmul(ps[:, :], whi[:, buf, ic, :], xnh_sl(ic, th),
                                    start=(ic == 0), stop=False))
            MM.inc(nc.tensor.matmul(ps[:, :], whi[:, buf, ic, :], xnl_sl(ic, th),
                                    start=False, stop=False))
        for ic in range(NCH):
            MM.inc(nc.tensor.matmul(ps[:, :], wlo[:, buf, ic, :], xnh_sl(ic, th),
                                    start=False, stop=(ic == NCH - 1)))

    for oc in range(NCH):
        buf = oc % 2
        _load_w_hilo(cx, T, "w_q1", oc * 128, 128, whi, wlo, w32, buf, key="w_q1")
        GP.wait(nc.tensor)
        VE.wait(nc.scalar)
        MM.wait(nc.scalar)
        cx.s(nc.scalar.activation(bt[:, :], pos_b[:, :], AF.Exp,
                                  bias=lnsc[:, 0:1], scale=dec_b[:, oc:oc + 1]))
        SC.wait(nc.vector)
        for th in range(2):
            ps = P[(oc * 2 + th) % 2]
            VE.wait(nc.tensor)
            proj_mms(ps, buf, th)
            MM.wait(nc.vector)
            LD.wait(nc.vector)
            cx.v(nc.vector.scalar_tensor_tensor(
                qrot[:, :], ps[:, :], bias4[:, 0, oc:oc + 1],
                bt[:, th * 512:(th + 1) * 512], ALU.add, ALU.mult))
            cx.v(nc.vector.tensor_copy(qh[:, :], qrot[:, :]))
            cx.v(nc.vector.tensor_tensor(ql[:, :], qrot[:, :], qh[:, :], ALU.subtract))
            VE.wait(nc.sync)
            for half, t in ((0, qh), (1, ql)):
                LD.inc(nc.sync.dma_start(
                    out=_apd(T["q_spill"], (half * 2048 + oc * 128) * TOK + th * 512,
                             [[TOK, 128], [1, 512]]),
                    in_=t[:, :]), 16)
        _wmark(cx, "w_q1", buf)

    for oc in range(NCH):
        buf = oc % 2
        _load_w_hilo(cx, T, "w_k1", oc * 128, 128, whi, wlo, w32, buf, key="w_k1")
        GP.wait(nc.tensor)
        VE.wait(nc.scalar)
        MM.wait(nc.scalar)
        cx.s(nc.scalar.activation(bt[:, :], pos_b[:, :], AF.Exp,
                                  bias=0.0, scale=decn_b[:, oc:oc + 1]))
        SC.wait(nc.vector)
        for th in range(2):
            ps = P[(oc * 2 + th) % 2]
            VE.wait(nc.tensor)
            proj_mms(ps, buf, th)
            MM.wait(nc.vector)
            LD.wait(nc.vector)
            cx.v(nc.vector.scalar_tensor_tensor(
                qrot[:, :], ps[:, :], bias4[:, 1, oc:oc + 1],
                bt[:, th * 512:(th + 1) * 512], ALU.add, ALU.mult))
            for sl in range(4):
                cx.v(nc.vector.tensor_reduce(sg[:, oc, th * 4 + sl:th * 4 + sl + 1],
                                             qrot[:, sl * 128:(sl + 1) * 128],
                                             AX.X, ALU.add))
            cx.v(nc.vector.tensor_copy(qh[:, :], qrot[:, :]))
            cx.v(nc.vector.tensor_tensor(ql[:, :], qrot[:, :], qh[:, :], ALU.subtract))
            VE.wait(nc.sync)
            for half, t in ((0, qh), (1, ql)):
                LD.inc(nc.sync.dma_start(
                    out=_apd(T["ag3_in"], (oc * 256 + half * 128) * 1024 + th * 512,
                             [[1024, 128], [1, 512]]),
                    in_=t[:, :]), 16)
        _wmark(cx, "w_k1", buf)
    VE.wait(nc.gpsimd)
    GLD.inc(nc.gpsimd.dma_start(
        out=T["ag3s_in"][:, :],
        in_=_apd(sg, 0, [[NH * 8, 128], [1, NH * 8]])), 16)
    GLD.wait(nc.gpsimd)
    CC.inc(nc.gpsimd.collective_compute(
        "AllGather", ALU.bypass, replica_groups=[[0, 1, 2, 3], [4, 5, 6, 7]],
        ins=[T["ag3s_in"].ap().opt()], outs=[T["ag3s_out"].ap().opt()]))

    for oc in range(NCH):
        buf = oc % 2
        _load_w_hilo(cx, T, "w_g1", oc * 128, 128, whi, wlo, w32, buf, key="w_g1")
        GP.wait(nc.tensor)
        for th in range(2):
            ps = P[(oc * 2 + th) % 2]
            SC.wait(nc.tensor)
            proj_mms(ps, buf, th)
            MM.wait(nc.scalar)
            LD.wait(nc.scalar)
            cx.s(nc.scalar.activation(qrot[:, :], ps[:, :], AF.Silu,
                                      bias=bias4[:, 3, oc:oc + 1], scale=1.0))
            SC.wait(nc.sync)
            LD.inc(nc.sync.dma_start(
                out=_apd(T["gate1_h"], oc * 128 * TOK + th * 512, [[TOK, 128], [1, 512]]),
                in_=qrot[:, :]), 16)
        _wmark(cx, "w_g1", buf)

    cx.pop_scope()
    cx.push_scope()
    wvhi = cx.sbuf("s_wvhi1", [128, 16, 512], BF16)
    wvlo = cx.sbuf("s_wvlo1", [128, 16, 512], BF16)
    wv32 = cx.sbuf("s_wv321", [128, 16, 512], F32)
    qrot = cx.sbuf("s_qrot1", [128, 512], F32)
    qh = cx.sbuf("s_qh1", [128, 512], BF16)
    ql = cx.sbuf("s_ql1", [128, 512], BF16)
    for os_ in range(4):
        _load_w_hilo2(cx, T, "w_v1", os_ * 512, 512, wvhi, wvlo, wv32)
        GP.wait(nc.tensor)
        for tt in range(NSLOT):
            ps = P[4 + tt % 2]
            VE.wait(nc.tensor)
            for ic in range(NCH):
                xh = _apd(xnh, ic * TOK + tt * 128, [[NCH * TOK, 128], [1, 128]])
                xl = _apd(xnl, ic * TOK + tt * 128, [[NCH * TOK, 128], [1, 128]])
                MM.inc(nc.tensor.matmul(ps[:, :], xh, wvhi[:, ic, :],
                                        start=(ic == 0), stop=False))
                MM.inc(nc.tensor.matmul(ps[:, :], xl, wvhi[:, ic, :],
                                        start=False, stop=False))
                MM.inc(nc.tensor.matmul(ps[:, :], xh, wvlo[:, ic, :],
                                        start=False, stop=(ic == NCH - 1)))
            MM.wait(nc.vector)
            LD.wait(nc.vector)
            cx.v(nc.vector.scalar_tensor_tensor(
                qrot[:, :], ps[:, :], 1.0, bv_b[:, os_ * 512:(os_ + 1) * 512],
                ALU.mult, ALU.add))
            cx.v(nc.vector.tensor_copy(qh[:, :], qrot[:, :]))
            cx.v(nc.vector.tensor_tensor(ql[:, :], qrot[:, :], qh[:, :], ALU.subtract))
            VE.wait(nc.sync)
            for hh in range(4):
                h = os_ * 4 + hh
                for half, t in ((0, qh), (1, ql)):
                    LD.inc(nc.sync.dma_start(
                        out=_apd(T["ag3_in"],
                                 (4096 + (h * 2 + half) * 128) * 1024 + tt * 128 * 128,
                                 [[128, 128], [1, 128]]),
                        in_=t[:, hh * 128:(hh + 1) * 128]), 16)

    cx.pop_scope()
    LD.wait(nc.gpsimd)
    CC.inc(nc.gpsimd.collective_compute(
        "AllGather", ALU.bypass, replica_groups=[[0, 1, 2, 3], [4, 5, 6, 7]],
        ins=[T["ag3_in"].ap().opt()], outs=[T["ag3_out"].ap().opt()]))
    if "d_q1" in dumps:
        LD.wait(nc.sync)
        LD.inc(nc.sync.dma_start(out=dump_t["d_q1"][:, :], in_=T["q_spill"][:, :]), 16)
        LD.wait(nc.sync)
    if "d_ag3" in dumps:
        CC.wait(nc.sync)
        LD.inc(nc.sync.dma_start(out=dump_t["d_ag3"][:, :], in_=T["ag3_out"][0:8192, :]), 16)
        LD.wait(nc.sync)
    if "d_sig" in dumps:
        CC.wait(nc.sync)
        LD.inc(nc.sync.dma_start(out=dump_t["d_sig"][:, :], in_=T["ag3s_in"][:, :]), 16)
        LD.wait(nc.sync)
    if "d_gate1" in dumps:
        LD.wait(nc.sync)
        LD.inc(nc.sync.dma_start(out=dump_t["d_gate1"][:, :], in_=T["gate1_h"][:, :]), 16)
        LD.wait(nc.sync)
    cx.pop_scope()


def _emit_ffn1_gu(cx, T, xnh, xnl, invrow, dumps, dump_t):
    nc = cx.nc
    cx.push_scope()
    LD, GLD, MM, VE, SC, GP, CC = (cx.sem[k] for k in
                                   ("LD", "GLD", "MM", "VE", "SC", "GP", "CC"))
    P = [cx.ps[f"p{i}"] for i in range(6)]
    wt = cx.sbuf("s_guw", [128, 2, 2, 16, 128], BF16)
    gsb = cx.sbuf("s_gsb", [128, 512], F32)
    hsb = cx.sbuf("s_hsb", [128, 2, 512], F32)
    sqs = cx.sbuf("s_sqs", [128, 512], F32)

    CC.wait(nc.sync)
    CC.wait(nc.tensor)
    CC.wait(nc.gpsimd)

    def gu_tile_ap(og, ic):
        s = og // 16
        off = (s * 12288 + 6144) * 1024 + (ic * 128) * 2048 + (og % 16) * 128
        return _apd(T["ag1_out"], off, [[2048, 128], [1, 128]])

    gu1_mm = [0, 0]
    for og in range(64):
        buf = og % 2
        MM.wait(nc.sync, gu1_mm[buf])
        for gi, ogx in enumerate((og, og + 64)):
            LD.inc(nc.sync.dma_start(
                out=wt[:, buf, gi, :, :],
                in_=_apd(T["ag1_out"], (ogx // 16 * 12288 + 6144) * 1024 + (ogx % 16) * 128,
                         [[2048, 128], [128 * 2048, 16], [1, 128]])), 16)
        LD.wait(nc.tensor)
        for th in range(2):
            psA, psB = P[th], P[2 + th]
            VE.wait(nc.tensor)
            SC.wait(nc.tensor)
            for ic in range(NCH):
                MM.inc(nc.tensor.matmul(psA[:, :], wt[:, buf, 0, ic, :],
                                        _apd(xnh, ic * TOK + th * 512, [[NCH * TOK, 128], [1, 512]]),
                                        start=(ic == 0), stop=False))
                MM.inc(nc.tensor.matmul(psA[:, :], wt[:, buf, 0, ic, :],
                                        _apd(xnl, ic * TOK + th * 512, [[NCH * TOK, 128], [1, 512]]),
                                        start=False, stop=(ic == NCH - 1)))
            for ic in range(NCH):
                MM.inc(nc.tensor.matmul(psB[:, :], wt[:, buf, 1, ic, :],
                                        _apd(xnh, ic * TOK + th * 512, [[NCH * TOK, 128], [1, 512]]),
                                        start=(ic == 0), stop=False))
                MM.inc(nc.tensor.matmul(psB[:, :], wt[:, buf, 1, ic, :],
                                        _apd(xnl, ic * TOK + th * 512, [[NCH * TOK, 128], [1, 512]]),
                                        start=False, stop=(ic == NCH - 1)))
            MM.wait(nc.scalar)
            VE.wait(nc.scalar)
            cx.s(nc.scalar.activation(gsb[:, :], psA[:, :], AF.Silu, bias=0.0, scale=1.0))
            SC.wait(nc.vector)
            GLD.wait(nc.vector)
            cx.v(nc.vector.tensor_tensor(hsb[:, th, :], gsb[:, :], psB[:, :], ALU.mult))
            VE.wait(nc.scalar)
            MM.wait(nc.scalar)
            cx.s(nc.scalar.activation(sqs[:, :], hsb[:, th, :], AF.Square, bias=0.0, scale=1.0))
            SC.wait(nc.tensor)
            MM.inc(nc.tensor.matmul(P[4][0:1, :] if th == 0 else P[5][0:1, :],
                                    cx.sb["s_ones"][:, :], sqs[:, :],
                                    start=(og == 0), stop=(og == 63)))
            VE.wait(nc.gpsimd)
            GLD.inc(nc.gpsimd.dma_start(
                out=_apd(T["h_hbm"], og * 128 * TOK + th * 512, [[TOK, 128], [1, 512]]),
                in_=hsb[:, th, :]), 16)
        gu1_mm[buf] = MM.n
    MM.wait(nc.vector)
    eps32 = float(np.finfo(np.float32).eps)
    for th, pp in ((0, P[4]), (1, P[5])):
        sl = slice(th * 512, (th + 1) * 512)
        cx.v(nc.vector.tensor_scalar(invrow[0:1, sl], pp[0:1, :], 1.0 / FFH, eps32,
                                     ALU.mult, ALU.add))
    VE.wait(nc.scalar)
    cx.s(nc.scalar.activation(invrow[0:1, :], invrow[0:1, :], AF.Sqrt, bias=0.0, scale=1.0))
    SC.wait(nc.vector)
    cx.v(nc.vector.reciprocal(invrow[0:1, :], invrow[0:1, :]))
    cx.pop_scope()


def _emit_ffn1_proj(cx, T, invrow, dumps, dump_t):
    nc = cx.nc
    cx.push_scope()
    LD, GLD, MM, VE, SC, GP, CC = (cx.sem[k] for k in
                                   ("LD", "GLD", "MM", "VE", "SC", "GP", "CC"))
    P = [cx.ps[f"p{i}"] for i in range(6)]
    hsl = cx.sbuf("s_hsl", [128, 64, TFF], F32)
    hnh = cx.sbuf("s_hnh", [128, 64, TFF], BF16)
    hnl = cx.sbuf("s_hnl", [128, 64, TFF], BF16)
    wp = cx.sbuf("s_wp", [128, 64, 128], BF16)
    invb = cx.sbuf("s_invb1", [128, TFF], F32)
    fout = cx.sbuf("s_fout1", [128, 2, TFF], F32)
    tsc = cx.sbuf("s_tsc1", [128, TFF], F32)

    SLDq = cx.sem["SLD"]
    fo_ve = [0, 0]
    for slab in range(NSLAB):
        VE.wait(nc.sync)
        VE.wait(nc.scalar)
        MM.wait(nc.sync)
        MM.wait(nc.scalar)
        ldq = []
        for q in range(4):
            sem_, eng_ = (LD, nc.sync) if q % 2 == 0 else (SLDq, nc.scalar)
            sem_.inc(eng_.dma_start(
                out=hsl[:, q * 16:(q + 1) * 16, :],
                in_=_apd(T["h_hbm"], slab * TFF + q * 16 * 128 * TOK,
                         [[TOK, 128], [128 * TOK, 16], [1, TFF]])), 16)
            ldq.append((sem_, sem_.n))
        VE.wait(nc.tensor)
        MM.inc(nc.tensor.matmul(P[4][:, 0:TFF], cx.sb["s_onesr"][:, :],
                                invrow[0:1, slab * TFF:(slab + 1) * TFF],
                                start=True, stop=True))
        MM.wait(nc.vector)
        cx.v(nc.vector.tensor_copy(invb[:, :], P[4][:, 0:TFF]))
        if "d_h" in dumps and slab == 0:
            LD.wait(nc.sync)
            LD.inc(nc.sync.dma_start(
                out=_apd(dump_t["d_h"], 0, [[TFF, 128], [128 * TFF, 64], [1, TFF]]),
                in_=_apd(T["h_hbm"], 0, [[TOK, 128], [128 * TOK, 64], [1, TFF]])), 16)
            LD.wait(nc.sync)
        MM.wait(nc.vector)
        ve_q = []
        for q in range(4):
            ldq[q][0].wait(nc.vector, ldq[q][1])
            for oc in range(q * 16, (q + 1) * 16):
                cx.v(nc.vector.tensor_tensor(tsc[:, :], hsl[:, oc, :], invb[:, :], ALU.mult))
                cx.v(nc.vector.tensor_copy(hnh[:, oc, :], tsc[:, :]))
                cx.v(nc.vector.tensor_tensor(hnl[:, oc, :], tsc[:, :], hnh[:, oc, :],
                                             ALU.subtract))
            ve_q.append(cx.sem["VE"].n)
        if "d_hn1" in dumps and slab == 0:
            VE.wait(nc.sync)
            LD.inc(nc.sync.dma_start(
                out=_apd(dump_t["d_hn1"], 0, [[TFF, 128], [128 * TFF, 64], [1, TFF]]),
                in_=_apd(hnh, 0, [[64 * TFF, 128], [TFF, 64], [1, TFF]])), 16)
            LD.wait(nc.sync)
        SLD = cx.sem["SLD"]
        for od in range(NCH):
            MM.wait(nc.sync)
            MM.wait(nc.scalar)
            LD.inc(nc.sync.dma_start(
                out=wp[:, 0:32, :],
                in_=_apd(T["ag1_out"],
                         (od // 2 * 12288 + 10240) * 1024 + (od % 2) * 128,
                         [[256, 128], [128 * 256, 32], [1, 128]])), 16)
            ldh0 = LD.n
            SLD.inc(nc.scalar.dma_start(
                out=wp[:, 32:64, :],
                in_=_apd(T["ag1_out"],
                         (od // 2 * 12288 + 10240) * 1024 + 32 * 128 * 256 + (od % 2) * 128,
                         [[256, 128], [128 * 256, 32], [1, 128]])), 16)
            ldh1 = SLD.n
            ps = P[od % 2]
            VE.wait(nc.tensor, fo_ve[od % 2])
            LD.wait(nc.tensor, ldh0)
            for ic in range(32):
                if od == 0 and ic % 16 == 0:
                    VE.wait(nc.tensor, ve_q[ic // 16])
                MM.inc(nc.tensor.matmul(ps[:, 0:TFF], wp[:, ic, :], hnh[:, ic, :],
                                        start=(ic == 0), stop=False))
                MM.inc(nc.tensor.matmul(ps[:, 0:TFF], wp[:, ic, :], hnl[:, ic, :],
                                        start=False, stop=False))
            SLD.wait(nc.tensor, ldh1)
            for ic in range(32, 64):
                if od == 0 and ic % 16 == 0:
                    VE.wait(nc.tensor, ve_q[ic // 16])
                MM.inc(nc.tensor.matmul(ps[:, 0:TFF], wp[:, ic, :], hnh[:, ic, :],
                                        start=False, stop=False))
                MM.inc(nc.tensor.matmul(ps[:, 0:TFF], wp[:, ic, :], hnl[:, ic, :],
                                        start=False, stop=(ic == 63)))
            MM.wait(nc.vector)
            GLD.wait(nc.vector)
            fo_ve[od % 2] = cx.v(nc.vector.tensor_copy(fout[:, od % 2, :], ps[:, 0:TFF]))
            VE.wait(nc.gpsimd)
            GLD.inc(nc.gpsimd.dma_start(
                out=_apd(T["x_res"], od * 128 * TOK + slab * TFF, [[TOK, 128], [1, TFF]]),
                in_=fout[:, od % 2, :], accum_op=ALU.add), 16)
    cx.pop_scope()


def _emit_attn1(cx, T, attn_sb, dumps, dump_t):
    nc = cx.nc
    cx.push_scope()
    LD, GLD, MM, VE, SC, GP, CC = (cx.sem[k] for k in
                                   ("LD", "GLD", "MM", "VE", "SC", "GP", "CC"))
    P = [cx.ps[f"p{i}"] for i in range(6)]
    kb1 = cx.sbuf("s_kb1", [128, 2, 2, 32, 128], BF16)
    vb1 = cx.sbuf("s_vb1", [128, 2, 2, 32, 128], BF16)
    qb1 = cx.sbuf("s_qb1", [128, 2, 2, TOK], BF16)
    sg1 = cx.sbuf("s_sg1", [128, 32], F32)
    sgz = cx.sbuf("s_sgz", [128, 32], F32)
    spref = cx.sbuf("s_spref", [128, 32], F32)
    sadj = cx.sbuf("s_sadj", [128, 8], F32)
    sh = cx.sbuf("s_sh", [128, 8], BF16)
    sl_ = cx.sbuf("s_sl", [128, 8], BF16)
    corr = cx.sbuf("s_corr", [128, 1], F32)
    cm1 = cx.sbuf("s_cm1", [128, 1], F32)
    dec_b = cx.sbuf("s_decb1", [128, NH], F32)
    decr = cx.sbuf("s_decr1", [1, NH], F32)
    pf = cx.sbuf("s_pf1", [128, 2, 512], F32)
    ph = cx.sbuf("s_ph1", [128, 2, 512], BF16)
    pl = cx.sbuf("s_pl1", [128, 2, 512], BF16)
    rrow = cx.sbuf("s_rrow1", [1, 512], F32)
    rabs = cx.sbuf("s_rabs1", [1, 512], F32)
    rbt = cx.sbuf("s_rbt1", [128, 512], F32)
    onesb = cx.sbuf("s_onesb", [128, 1], BF16)
    sgall = cx.sbuf("s_sgall", [128, 4, 128], F32)
    bmask = cx.sb["s_bmask"]
    SLD = cx.sem["SLD"]
    cx.g(nc.gpsimd.memset(onesb[:, :], 1.0))

    cx.g(nc.gpsimd.memset(sgz[:, :], 0.0))
    GP.wait(nc.tensor)
    LD.inc(nc.sync.dma_start(out=decr[0:1, :], in_=T["decay"][:, :]), 16)
    LD.wait(nc.tensor)
    MM.inc(nc.tensor.matmul(P[0][:, 0:NH], cx.sb["s_onesr"][:, :], decr[0:1, :],
                            start=True, stop=True))
    MM.wait(nc.vector)
    cx.v(nc.vector.tensor_copy(dec_b[:, :], P[0][:, 0:NH]))
    CC.wait(nc.sync)
    CC.wait(nc.tensor)
    CC.wait(nc.scalar)
    CC.wait(nc.vector)
    for gp_ in range(4):
        LD.inc(nc.sync.dma_start(
            out=sgall[:, gp_, :],
            in_=_apd(T["ag3s_out"], gp_ * 128 * 128, [[128, 128], [1, 128]])), 16)
    ld_sg = LD.n

    hb_mm = [0, 0]
    bank_ev = [0, 0]
    ld_k = [0, 0]
    ld_v = [0, 0]
    for h in range(NH):
        hs = h % 2
        MM.wait(nc.sync, hb_mm[hs])
        MM.wait(nc.scalar, hb_mm[hs])
        for half in range(2):
            for gp_ in range(4):
                LD.inc(nc.sync.dma_start(
                    out=_apd(kb1, hs * 8192 + half * 4096 + gp_ * 128,
                             [[16384, 128], [4 * 128, 8], [1, 128]]),
                    in_=_apd(T["ag3_out"],
                             (gp_ * 8192 + h * 256 + half * 128) * 1024,
                             [[1024, 128], [128, 8], [1, 128]])), 16)
                SLD.inc(nc.scalar.dma_start(
                    out=_apd(vb1, hs * 8192 + half * 4096 + gp_ * 128,
                             [[16384, 128], [4 * 128, 8], [1, 128]]),
                    in_=_apd(T["ag3_out"],
                             (gp_ * 8192 + 4096 + (h * 2 + half) * 128) * 1024,
                             [[128, 128], [128 * 128, 8], [1, 128]])), 16)
        LD.inc(nc.sync.dma_start(
            out=qb1[:, hs, :, :],
            in_=_apd(T["q_spill"], h * 128 * TOK, [[TOK, 128], [2048 * TOK, 2], [1, TOK]])), 16)
        ld_k[hs] = LD.n
        ld_v[hs] = SLD.n
        if h == 0:
            LD.wait(nc.vector, ld_sg)
        for gp_ in range(4):
            cx.v(nc.vector.tensor_copy(
                _apd(sg1, gp_, [[32, 128], [4, 8]]),
                _apd(sgall, gp_ * 128 + h * 8, [[512, 128], [1, 8]])))
        cx.v(nc.vector.tensor_tensor_scan(spref[:, :], sg1[:, :], sgz[:, :], 0.0,
                                          ALU.add, ALU.add))
        VE.wait(nc.scalar)
        MM.wait(nc.scalar)
        cx.s(nc.scalar.activation(corr[:, 0:1], dec_b[:, h:h + 1], AF.Exp,
                                  bias=0.0, scale=2048.0))
        SC.wait(nc.vector)
        cx.v(nc.vector.tensor_scalar(cm1[:, 0:1], corr[:, 0:1], -1.0, None, ALU.add))
        for j in range(NSLOT):
            if j == 0:
                cx.g(nc.gpsimd.memset(sadj[:, 0:1], 0.0))
                GP.wait(nc.vector)
            elif j < 4:
                cx.v(nc.vector.tensor_copy(sadj[:, j:j + 1], spref[:, 4 * j - 1:4 * j]))
            else:
                cx.v(nc.vector.scalar_tensor_tensor(
                    sadj[:, j:j + 1], spref[:, 15:16], cm1[:, 0:1],
                    spref[:, 4 * j - 1:4 * j], ALU.mult, ALU.add))
        cx.v(nc.vector.tensor_copy(sh[:, :], sadj[:, :]))
        cx.v(nc.vector.tensor_tensor(sl_[:, :], sadj[:, :], sh[:, :], ALU.subtract))

        for bl in range(2):
            nk = 16 * bl + 16
            VE.wait(nc.tensor)
            if bl == 0:
                LD.wait(nc.tensor, ld_k[hs])
                SLD.wait(nc.tensor, ld_v[hs])
            first = True
            for j in range(4 * bl, 4 * bl + 4):
                if j == 0:
                    continue
                co = (j - 4 * bl) * 128
                MM.inc(nc.tensor.matmul(P[3][0:1, co:co + 128], sh[:, j:j + 1],
                                        qb1[:, hs, 0, j * 128:(j + 1) * 128],
                                        start=first, stop=False))
                MM.inc(nc.tensor.matmul(P[3][0:1, co:co + 128], sl_[:, j:j + 1],
                                        qb1[:, hs, 0, j * 128:(j + 1) * 128],
                                        start=False, stop=False))
                MM.inc(nc.tensor.matmul(P[3][0:1, co:co + 128], sh[:, j:j + 1],
                                        qb1[:, hs, 1, j * 128:(j + 1) * 128],
                                        start=False, stop=False))
                first = False
            sc_evac = {}
            mm_pv = {}

            def scores1(k, hs=hs, bl=bl):
                j0 = max(4 * bl, -(-(k - 3) // 4) if k > 3 else 4 * bl)
                co = (j0 - 4 * bl) * 128
                n = (4 * bl + 4 - j0) * 128
                ps_s = P[k % 2]
                VE.wait(nc.tensor, sc_evac[k - 2] if k >= 2 else bank_ev[k % 2])
                MM.inc(nc.tensor.matmul(ps_s[:, co:co + n], kb1[:, hs, 0, k, :],
                                        qb1[:, hs, 0, j0 * 128:(4 * bl + 4) * 128],
                                        start=True, stop=False))
                MM.inc(nc.tensor.matmul(ps_s[:, co:co + n], kb1[:, hs, 0, k, :],
                                        qb1[:, hs, 1, j0 * 128:(4 * bl + 4) * 128],
                                        start=False, stop=False))
                mm_sc = MM.inc(nc.tensor.matmul(ps_s[:, co:co + n], kb1[:, hs, 1, k, :],
                                                qb1[:, hs, 0, j0 * 128:(4 * bl + 4) * 128],
                                                start=False, stop=True))
                MM.wait(nc.vector, mm_sc)
                if k >= 2:
                    MM.wait(nc.vector, mm_pv[k - 2])
                if k >= 16 * bl:
                    b = k % 4
                    cx.v(nc.vector.tensor_tensor(
                        ps_s[:, co:co + 128], ps_s[:, co:co + 128],
                        _apd(bmask, b * 128, [[512, 128], [1, 128]]), ALU.mult))
                crossing = (bl == 1 and k < 16)
                if crossing:
                    cx.v(nc.vector.tensor_scalar(pf[:, k % 2, co:co + n],
                                                 ps_s[:, co:co + n],
                                                 corr[:, 0:1], None, ALU.mult))
                else:
                    cx.v(nc.vector.tensor_copy(pf[:, k % 2, co:co + n], ps_s[:, co:co + n]))
                cx.v(nc.vector.tensor_copy(ph[:, k % 2, co:co + n], pf[:, k % 2, co:co + n]))
                sc_evac[k] = cx.v(nc.vector.tensor_tensor(
                    pl[:, k % 2, co:co + n], pf[:, k % 2, co:co + n],
                    ph[:, k % 2, co:co + n], ALU.subtract))
                bank_ev[k % 2] = sc_evac[k]

            def pvrow1(k, hs=hs, bl=bl, nk=nk):
                j0 = max(4 * bl, -(-(k - 3) // 4) if k > 3 else 4 * bl)
                co = (j0 - 4 * bl) * 128
                n = (4 * bl + 4 - j0) * 128
                VE.wait(nc.tensor, sc_evac[k])
                if k >= 16 * bl:
                    MM.inc(nc.tensor.matmul(P[3][0:1, co:co + 128], onesb[:, :],
                                            ph[:, k % 2, co:co + 128],
                                            start=False, stop=False))
                    MM.inc(nc.tensor.matmul(P[3][0:1, co:co + 128], onesb[:, :],
                                            pl[:, k % 2, co:co + 128],
                                            start=False,
                                            stop=(k == nk - 1)))
                MM.inc(nc.tensor.matmul(P[2][:, co:co + n], vb1[:, hs, 0, k, :],
                                        ph[:, k % 2, co:co + n],
                                        start=(k == 0), stop=False))
                MM.inc(nc.tensor.matmul(P[2][:, co:co + n], vb1[:, hs, 0, k, :],
                                        pl[:, k % 2, co:co + n],
                                        start=False, stop=False))
                mm_pv[k] = MM.inc(nc.tensor.matmul(P[2][:, co:co + n], vb1[:, hs, 1, k, :],
                                                   ph[:, k % 2, co:co + n],
                                                   start=False, stop=(k == nk - 1)))

            scores1(0)
            for k in range(1, nk):
                scores1(k)
                pvrow1(k - 1)
            pvrow1(nk - 1)
            MM.wait(nc.scalar)
            cx.s(nc.scalar.activation(rabs[0:1, :], P[3][0:1, :], AF.Abs,
                                      bias=0.0, scale=1.0))
            SC.wait(nc.vector)
            cx.v(nc.vector.tensor_scalar_max(rrow[0:1, :], rabs[0:1, :], 1.0))
            cx.v(nc.vector.reciprocal(rrow[0:1, :], rrow[0:1, :]))
            VE.wait(nc.tensor)
            MM.inc(nc.tensor.matmul(P[4][:, :], cx.sb["s_onesr"][:, :], rrow[0:1, :],
                                    start=True, stop=True))
            MM.wait(nc.vector)
            cx.v(nc.vector.tensor_copy(rbt[:, :], P[4][:, :]))
            cx.v(nc.vector.scalar_tensor_tensor(
                _apd(attn_sb, h * TOK + bl * 512, [[NCH * TOK, 128], [1, 512]]),
                P[2][:, :], 1.0, rbt[:, :], ALU.mult, ALU.mult))
            VE.wait(nc.tensor)
            hb_mm[hs] = MM.n
    if "d_attn1" in dumps:
        VE.wait(nc.sync)
        LD.inc(nc.sync.dma_start(
            out=_apd(dump_t["d_attn1"], 0, [[TOK, 128], [128 * TOK, NCH], [1, TOK]]),
            in_=_apd(attn_sb, 0, [[NCH * TOK, 128], [TOK, NCH], [1, TOK]])), 16)
        LD.wait(nc.sync)
    cx.pop_scope()


def _emit_asm1(cx, T, attn_sb, dumps, dump_t):
    nc = cx.nc
    cx.push_scope()
    cx.push_scope()
    LD, GLD, MM, VE, SC, GP = (cx.sem[k] for k in
                               ("LD", "GLD", "MM", "VE", "SC", "GP"))
    P = [cx.ps[f"p{i}"] for i in range(6)]
    gnw = cx.sbuf("s_gnw", [128, NCH], F32)
    gnb = cx.sbuf("s_gnb", [128, NCH], F32)
    sq = cx.sbuf("s_gsq", [128, 512], F32)
    gtc = cx.sbuf("s_gtc", [128, 512], F32)
    row = cx.sbuf("s_grow", [1, 4 * 512], F32)
    ivb = cx.sbuf("s_givb", [128, 512], F32)
    m2b = cx.sbuf("s_gm2b", [128, 512], F32)
    LD.inc(nc.sync.dma_start(out=gnw[:, :], in_=_apd(T["gn_w"], 0, [[1, 128], [128, NCH]]),
                             allow_slow_non_contiguous=True), 16)
    LD.inc(nc.sync.dma_start(out=gnb[:, :], in_=_apd(T["gn_b"], 0, [[1, 128], [128, NCH]]),
                             allow_slow_non_contiguous=True), 16)
    LD.wait(nc.vector)
    mu = row[0:1, 0:512]
    var = row[0:1, 512:1024]
    m2r = row[0:1, 1024:1536]

    for ch in range(NCH):
        for th in range(2):
            asl = attn_sb[:, ch, th * 512:(th + 1) * 512]
            VE.wait(nc.scalar)
            MM.wait(nc.scalar)
            cx.s(nc.scalar.activation(sq[:, :], asl, AF.Square, bias=0.0, scale=1.0))
            SC.wait(nc.tensor)
            VE.wait(nc.tensor)
            MM.inc(nc.tensor.matmul(P[0][0:1, :], cx.sb["s_ones"][:, :], asl,
                                    start=True, stop=True))
            MM.inc(nc.tensor.matmul(P[1][0:1, :], cx.sb["s_ones"][:, :], sq[:, :],
                                    start=True, stop=True))
            MM.wait(nc.vector)
            cx.v(nc.vector.tensor_scalar(mu[:, :], P[0][0:1, :], 1.0 / 128.0, None, ALU.mult))
            cx.v(nc.vector.tensor_scalar(var[:, :], P[1][0:1, :], 1.0 / 128.0, None, ALU.mult))
            cx.v(nc.vector.tensor_tensor(m2r[:, :], mu[:, :], mu[:, :], ALU.mult))
            cx.v(nc.vector.tensor_tensor(var[:, :], var[:, :], m2r[:, :], ALU.subtract))
            cx.v(nc.vector.tensor_scalar(var[:, :], var[:, :], 1e-5, None, ALU.add))
            VE.wait(nc.scalar)
            cx.s(nc.scalar.activation(var[:, :], var[:, :], AF.Sqrt, bias=0.0, scale=1.0))
            SC.wait(nc.vector)
            cx.v(nc.vector.reciprocal(var[:, :], var[:, :]))
            cx.v(nc.vector.tensor_tensor(m2r[:, :], mu[:, :], var[:, :], ALU.mult))
            VE.wait(nc.tensor)
            MM.inc(nc.tensor.matmul(P[2][:, :], cx.sb["s_onesr"][:, :], var[:, :],
                                    start=True, stop=True))
            MM.inc(nc.tensor.matmul(P[3][:, :], cx.sb["s_onesr"][:, :], m2r[:, :],
                                    start=True, stop=True))
            MM.wait(nc.vector)
            cx.v(nc.vector.tensor_copy(ivb[:, :], P[2][:, :]))
            cx.v(nc.vector.tensor_copy(m2b[:, :], P[3][:, :]))
            VE.wait(nc.sync)
            LD.inc(nc.sync.dma_start(
                out=gtc[:, :],
                in_=_apd(T["gate1_h"], ch * 128 * TOK + th * 512, [[TOK, 128], [1, 512]])), 16)
            cx.v(nc.vector.scalar_tensor_tensor(sq[:, :], asl, 1.0, ivb[:, :],
                                                ALU.mult, ALU.mult))
            cx.v(nc.vector.tensor_tensor(sq[:, :], sq[:, :], m2b[:, :], ALU.subtract))
            cx.v(nc.vector.tensor_scalar(sq[:, :], sq[:, :], gnw[:, ch:ch + 1],
                                         gnb[:, ch:ch + 1], ALU.mult, ALU.add))
            LD.wait(nc.vector)
            cx.v(nc.vector.tensor_tensor(asl, sq[:, :], gtc[:, :], ALU.mult))
    cx.pop_scope()
    _emit_wo(cx, T, attn_sb, w="w_o1", bias="b_o1", layer=1)
    cx.pop_scope()



def _host_inputs(inputs):
    x0 = np.ascontiguousarray(np.asarray(inputs["L0_attn_hidden_states"], np.float32))
    maps = []
    inv = 1.0 / 10000.0 ** (np.arange(0, HD, 2, dtype=np.float32) / HD)
    eye32 = np.eye(128, dtype=np.float32)
    eyebf = np.eye(128).astype(BF)
    tri = np.tril(np.ones((128, 128), np.float32))
    swap = np.zeros((128, 128), np.float32)
    swap[np.arange(128), np.arange(128) ^ 64] = 1.0
    rotm = np.zeros((7, 128, 128), np.float32)
    for j in range(7):
        s = 1 << (6 - j)
        rotm[j, (np.arange(128) + s) % 128, np.arange(128)] = 1.0
    for c in range(NCORE):
        b, g = c // 4, c % 4
        tiles = [g + 4 * j for j in range(NSLOT)]
        rows = np.concatenate([np.arange(t * 128, (t + 1) * 128) for t in tiles])
        pos = rows.astype(np.float32)
        f = np.outer(pos, inv)
        bm = np.zeros((4, 128, 128), np.float32)
        for bi in range(4):
            if bi < g:
                bm[bi] = 1.0
            elif bi == g:
                bm[bi] = tri.T
        m = {
            "x0": np.ascontiguousarray(x0[b, rows, :]),
            "w_qkv": np.asarray(inputs["L0_attn_Wqkv"], np.float32),
            "w_o0": np.asarray(inputs["L0_attn_Wo"], np.float32),
            "wg_sl": np.ascontiguousarray(np.asarray(inputs["L0_ffn_w_gate"], np.float32)[c * 1024:(c + 1) * 1024]),
            "wu_sl": np.ascontiguousarray(np.asarray(inputs["L0_ffn_w_up"], np.float32)[c * 1024:(c + 1) * 1024]),
            "wd_sl": np.ascontiguousarray(np.asarray(inputs["L0_ffn_w_down"], np.float32)[c * 256:(c + 1) * 256]),
            "w_q1": np.asarray(inputs["L1_attn_Wq"], np.float32),
            "w_k1": np.asarray(inputs["L1_attn_Wk"], np.float32),
            "w_v1": np.asarray(inputs["L1_attn_Wv"], np.float32),
            "w_g1": np.asarray(inputs["L1_attn_Wg"], np.float32),
            "w_o1": np.asarray(inputs["L1_attn_Wo"], np.float32),
            "b_q1": np.asarray(inputs["L1_attn_bq"], np.float32).reshape(1, D),
            "b_k1": np.asarray(inputs["L1_attn_bk"], np.float32).reshape(1, D),
            "b_v1": np.asarray(inputs["L1_attn_bv"], np.float32).reshape(1, D),
            "b_g1": np.asarray(inputs["L1_attn_bg"], np.float32).reshape(1, D),
            "b_o1": np.asarray(inputs["L1_attn_bo"], np.float32).reshape(1, D),
            "gn_w": np.asarray(inputs["L1_attn_gn_w"], np.float32).reshape(1, D),
            "gn_b": np.asarray(inputs["L1_attn_gn_b"], np.float32).reshape(1, D),
            "decay": np.asarray(inputs["L1_attn_decay"], np.float32).reshape(1, NH),
            "wgu_sl": np.ascontiguousarray(np.asarray(inputs["L1_ffn_w_gate_up"], np.float32)[c * 2048:(c + 1) * 2048]),
            "wp_sl": np.ascontiguousarray(np.asarray(inputs["L1_ffn_w_proj"], np.float32)[c * 256:(c + 1) * 256]),
            "pos2048": np.mod(pos, 2048.0).reshape(1, TOK).astype(np.float32),
            "cos_t": np.ascontiguousarray(np.tile(np.cos(f).T.astype(np.float32), (2, 1))),
            "sin_t": np.ascontiguousarray(np.concatenate(
                [-np.sin(f).T, np.sin(f).T]).astype(np.float32)),
            "swap128": swap,
            "rotm": rotm.reshape(7 * 128, 128),
            "bmask": bm.reshape(4 * 128, 128),
            "ident32": eye32,
            "identbf": eyebf,
        }
        maps.append(m)
    return maps


def kernel(**inputs):
    global _LAST_RESULTS
    nc = _build(dumps=_DUMPS, stage=_STAGE)
    in_maps = _host_inputs(inputs)
    res = run_bass_kernel_spmd(nc, in_maps, core_ids=list(range(NCORE)))
    _LAST_RESULTS = res
    out = np.zeros((B, S, D), np.float32)
    for c in range(NCORE):
        b, g = c // 4, c % 4
        tiles = [g + 4 * j for j in range(NSLOT)]
        rows = np.concatenate([np.arange(t * 128, (t + 1) * 128) for t in tiles])
        out[b, rows, :] = np.asarray(res.results[c]["out_tok"], np.float32)
    return out


# version 14
# speedup vs baseline: 1.0366x; 1.0366x over previous
import numpy as np
import ml_dtypes
from contextlib import ExitStack

import concourse.bass as bass
import concourse.mybir as mybir
from concourse.bass_utils import run_bass_kernel_spmd

F32 = mybir.dt.float32
BF16 = mybir.dt.bfloat16
AF = mybir.ActivationFunctionType
ALU = mybir.AluOpType
AX = mybir.AxisListType
BF = ml_dtypes.bfloat16

B, S, D = 2, 4096, 2048
NCORE = 8
TOK = 1024
NSLOT = 8
NCH = 16
NH, HD = 16, 128
QKV_O = 2560
FFH = 8192
GUH = 16384
MAGIC = 12582912.0
SCALE0 = 0.08838834764831845
CAP = 30.0
LNLOG_SCALE0 = float(np.log(np.float64(SCALE0)))
TFF = 256
NSLAB = TOK // TFF

_STAGE = 99
_DUMPS = ()
_LAST_RESULTS = None
_WPREP = True
_WPARTS = {'ag0', 'l0', 'l1', 'ag1'}
_FFN0 = True
_ATTN0 = True


def _apd(t, off, dims):
    return bass.AP(t, off, [list(d) for d in dims])


class Sem:
    def __init__(self, h):
        self.h = h
        self.n = 0

    def inc(self, inst, k=1):
        inst.then_inc(self.h, k)
        self.n += k
        return self.n

    def wait(self, eng, val=None):
        eng.wait_ge(self.h, self.n if val is None else val)


class Ctx:

    def __init__(self, nc, stack):
        self.nc = nc
        self.stack = stack
        en = stack.enter_context
        self.sem = {}
        for name in ("LD", "GLD", "MM", "VE", "SC", "GP", "CC", "SLD",
                     "ATT", "ATV", "ATS", "PRJ", "FFN", "TRP", "TEV"):
            self.sem[name] = Sem(en(nc.semaphore(name.lower())))
        self.sb = {}
        self.ps = {}

    def sbuf(self, name, shape, dt):
        scope = self.scopes[-1] if getattr(self, "scopes", None) else self.stack
        if not hasattr(self, "_used_names"):
            self._used_names = {}
        k = self._used_names.get(name, 0)
        self._used_names[name] = k + 1
        uname = name if k == 0 else f"{name}__{k}"
        self.sb[name] = scope.enter_context(self.nc.sbuf_tensor(uname, shape, dt))
        if getattr(self, "scopes", None) and len(self.scopes) > 1:
            self.scope_names[-1].append(name)
        return self.sb[name]

    def push_scope(self):
        if not hasattr(self, "scopes"):
            self.scopes = [self.stack]
            self.scope_names = [[]]
        st = ExitStack()
        st.__enter__()
        self.scopes.append(st)
        self.scope_names.append([])
        return st

    def pop_scope(self):
        self.fence()
        st = self.scopes.pop()
        for n in self.scope_names.pop():
            self.sb.pop(n, None)
        st.__exit__(None, None, None)

    def fence(self):
        for ename in ("sync", "gpsimd", "vector", "scalar", "tensor"):
            eng = getattr(self.nc, ename)
            for s in ("VE", "SC", "GP", "MM", "LD", "GLD", "SLD", "TRP", "TEV"):
                self.sem[s].wait(eng)

    def psum(self, name, shape, dt=F32):
        self.ps[name] = self.stack.enter_context(self.nc.psum_tensor(name, shape, dt))
        return self.ps[name]

    def v(self, inst):
        s = self.sem["VE"]
        n = s.inc(inst)
        self.nc.vector.wait_ge(s.h, n)
        return n

    def s(self, inst):
        s = self.sem["SC"]
        n = s.inc(inst)
        self.nc.scalar.wait_ge(s.h, n)
        return n

    def g(self, inst):
        s = self.sem["GP"]
        n = s.inc(inst)
        self.nc.gpsimd.wait_ge(s.h, n)
        return n


def _build(dumps=(), stage=99):
    nc = bass.Bass("TRN2", target_bir_lowering=False, debug=False)
    dumps = set(dumps)

    def di(name, shape, dt=F32):
        return nc.dram_tensor(name, shape, dt, kind="ExternalInput")

    x0 = di("x0", [TOK, D])
    w_qkv = di("w_qkv", [D, QKV_O])
    w_o0 = di("w_o0", [D, D])
    wg_sl = di("wg_sl", [FFH // 8, D])
    wu_sl = di("wu_sl", [FFH // 8, D])
    wd_sl = di("wd_sl", [D // 8, FFH])
    w_q1 = di("w_q1", [D, D])
    w_k1 = di("w_k1", [D, D])
    w_v1 = di("w_v1", [D, D])
    w_g1 = di("w_g1", [D, D])
    w_o1 = di("w_o1", [D, D])
    b_q1 = di("b_q1", [1, D])
    b_k1 = di("b_k1", [1, D])
    b_v1 = di("b_v1", [1, D])
    b_g1 = di("b_g1", [1, D])
    b_o1 = di("b_o1", [1, D])
    gn_w = di("gn_w", [1, D])
    gn_b = di("gn_b", [1, D])
    decay = di("decay", [1, NH])
    wgu_sl = di("wgu_sl", [GUH // 8, D])
    wp_sl = di("wp_sl", [D // 8, FFH])
    pos2048 = di("pos2048", [1, TOK])
    cos_t = di("cos_t", [128, TOK])
    sin_t = di("sin_t", [128, TOK])
    swap128 = di("swap128", [128, 128], F32)
    rotm = di("rotm", [7 * 128, 128], F32)
    bmask = di("bmask", [4 * 128, 128])
    ident32 = di("ident32", [128, 128], F32)
    identbf = di("identbf", [128, 128], BF16)

    out_tok = nc.dram_tensor("out_tok", [TOK, D], F32, kind="ExternalOutput")

    dump_t = {}
    for dn, shape, dt in [
        ("d_xn", [D, TOK], F32),
        ("d_xres", [D, TOK], F32),
        ("d_q0", [NH * 2 * 128, TOK], BF16),
        ("d_ag2", [4096, 1024], BF16),
        ("d_attn0", [D, TOK], F32),
        ("d_x1", [D, TOK], F32),
        ("d_r", [1, TOK], F32),
        ("d_xq", [D, TOK], BF16),
        ("d_wsc", [1, 4], F32),
        ("d_h", [FFH, TFF], F32),
        ("d_hq", [FFH, TFF], BF16),
        ("d_ag1", [2048, 1024], BF16),
        ("d_ag1d", [2048, 1024], BF16),
        ("d_xn1", [D, TOK], F32),
        ("d_q1", [NH * 2 * 128, TOK], BF16),
        ("d_ag3", [8192, 1024], BF16),
        ("d_sig", [128, 128], F32),
        ("d_attn1", [D, TOK], F32),
        ("d_gate1", [D, TOK], F32),
        ("d_hn1", [FFH, TFF], BF16),
    ]:
        if dn in dumps:
            dump_t[dn] = nc.dram_tensor(dn, shape, dt, kind="ExternalOutput")

    st_mu = nc.dram_tensor("st_mu", [1, TOK], F32)
    st_sq = nc.dram_tensor("st_sq", [1, TOK], F32)
    st_mx = nc.dram_tensor("st_mx", [1, TOK], F32)
    x_res = nc.dram_tensor("x_res", [D, TOK], F32)
    xn_hbm = nc.dram_tensor("xn_hbm", [D, TOK], F32)
    q_spill = nc.dram_tensor("q_spill", [2 * NH * 128, TOK], BF16)
    gate1_h = nc.dram_tensor("gate1_h", [D, TOK], F32)
    h_hbm = nc.dram_tensor("h_hbm", [FFH, TOK], F32)
    ag0_in = nc.dram_tensor("ag0_in", [1, 4], F32)
    ag0_out = nc.dram_tensor("ag0_out", [8, 4], F32)
    ag1_in = nc.dram_tensor("ag1_in", [12288, 1024], BF16)
    ag1_out = nc.dram_tensor("ag1_out", [98304, 1024], BF16)
    ag2_in = nc.dram_tensor("ag2_in", [1024, 1024], BF16)
    ag2_out = nc.dram_tensor("ag2_out", [4096, 1024], BF16)
    ag3_in = nc.dram_tensor("ag3_in", [8192, 1024], BF16)
    ag3_out = nc.dram_tensor("ag3_out", [32768, 1024], BF16)
    ag3s_in = nc.dram_tensor("ag3s_in", [128, 128], F32)
    ag3s_out = nc.dram_tensor("ag3s_out", [512, 128], F32)

    with ExitStack() as stack:
        cx = Ctx(nc, stack)
        en = stack.enter_context
        blk = en(nc.Block())
        _emit(cx, locals(), dumps, dump_t, stage)
    return nc



def _emit(cx, T, dumps, dump_t, stage):
    nc = cx.nc
    LD, GLD, MM, VE, SC, GP, CC = (cx.sem[k] for k in
                                   ("LD", "GLD", "MM", "VE", "SC", "GP", "CC"))

    ident32 = cx.sbuf("s_id32", [128, 128], F32)
    identbf = cx.sbuf("s_idbf", [128, 128], BF16)
    ones_col = cx.sbuf("s_ones", [128, 1], F32)
    ones_row = cx.sbuf("s_onesr", [1, 128], F32)
    bmask_sb = cx.sbuf("s_bmask", [128, 4, 128], F32)
    cos_sb = cx.sbuf("s_cos", [128, TOK], F32)
    sin_sb = cx.sbuf("s_sin", [128, TOK], F32)
    swap_sb = cx.sbuf("s_swap", [128, 128], F32)
    rotm_sb = cx.sbuf("s_rotm", [128, 7, 128], F32)

    P = [cx.psum(f"p{i}", [128, 512], F32) for i in range(6)]
    PTB = [cx.psum(f"ptb{i}", [128, 128], BF16) for i in range(2)]

    nc.sync.dma_start(out=ident32[:, :], in_=T["ident32"][:, :]).then_inc(LD.h, 16); LD.n += 16
    nc.sync.dma_start(out=identbf[:, :], in_=T["identbf"][:, :]).then_inc(LD.h, 16); LD.n += 16
    nc.sync.dma_start(out=bmask_sb[:, :, :],
                      in_=_apd(T["bmask"], 0, [[128, 128], [128 * 128, 4], [1, 128]])).then_inc(LD.h, 16); LD.n += 16
    nc.sync.dma_start(out=cos_sb[:, :], in_=T["cos_t"][:, :]).then_inc(LD.h, 16); LD.n += 16
    nc.sync.dma_start(out=sin_sb[:, :], in_=T["sin_t"][:, :]).then_inc(LD.h, 16); LD.n += 16
    nc.sync.dma_start(out=swap_sb[:, :], in_=T["swap128"][:, :]).then_inc(LD.h, 16); LD.n += 16
    nc.sync.dma_start(out=rotm_sb[:, :, :],
                      in_=_apd(T["rotm"], 0, [[128, 128], [128 * 128, 7], [1, 128]])).then_inc(LD.h, 16); LD.n += 16
    cx.g(nc.gpsimd.memset(ones_col[:, :], 1.0))
    cx.g(nc.gpsimd.memset(ones_row[:, :], 1.0))

    cx.push_scope()
    xin = cx.sbuf("s_xin", [128, D], F32)
    tstg = cx.sbuf("s_tstg", [128, NCH * 128], F32)
    xsc = cx.sbuf("s_xsc", [128, D], F32)
    stt3 = cx.sbuf("s_stt3", [128, 3], F32)

    def _stats_hook(tt, buf):
        LD.wait(nc.scalar)
        cx.sem["SLD"].wait(nc.scalar)
        VE.wait(nc.scalar)
        cx.s(nc.scalar.activation(xsc[:, :], buf[:, :], AF.Square, bias=0.0, scale=1.0))
        cx.sem["TRP"].wait(nc.vector)
        SC.wait(nc.vector)
        cx.v(nc.vector.tensor_reduce(stt3[:, 0:1], buf[:, :], AX.X, ALU.add))
        cx.v(nc.vector.tensor_reduce(stt3[:, 1:2], xsc[:, :], AX.X, ALU.add))
        cx.v(nc.vector.tensor_scalar(stt3[:, 0:1], stt3[:, 0:1], 1.0 / D, None, ALU.mult))
        cx.v(nc.vector.tensor_scalar(stt3[:, 1:2], stt3[:, 1:2], 1.0 / D, None, ALU.mult))
        cx.v(nc.vector.tensor_scalar(xsc[:, :], buf[:, :], stt3[:, 0:1], None, ALU.subtract))
        cx.v(nc.vector.tensor_reduce(stt3[:, 2:3], xsc[:, :], AX.X, ALU.max,
                                       apply_absolute_value=True))
        VE.wait(nc.sync)
        for k, dst in enumerate(("st_mu", "st_sq", "st_mx")):
            cx.sem["TRP"].inc(nc.sync.dma_start(
                out=_apd(T[dst], tt * 128, [[1, 128], [1, 1]]),
                in_=stt3[:, k:k + 1]), 16)

    _transpose_128xD(cx, src_load=lambda tt: nc.sync.dma_start(
        out=xin[:, 0:1024], in_=T["x0"][tt * 128:(tt + 1) * 128, 0:1024]),
        src_load2=lambda tt: nc.scalar.dma_start(
            out=xin[:, 1024:2048], in_=T["x0"][tt * 128:(tt + 1) * 128, 1024:2048]),
        xin=xin, tstg=tstg, ident=ident32,
        dst_ap=lambda tt: _apd(T["x_res"], tt * 128,
                               [[TOK, 128], [128 * TOK, NCH], [1, 128]]),
        tstg_ap=_apd(tstg, 0, [[NCH * 128, 128], [128, NCH], [1, 128]]),
        per_tile=_stats_hook)
    cx.pop_scope()

    if stage < 1:
        _finish_stub(cx, T)
        return

    cx.push_scope()
    xq = cx.sbuf("s_xq", [128, NCH, TOK], BF16)
    u1g = cx.sbuf("s_u1g", [128, TOK], F32)
    u1u = cx.sbuf("s_u1u", [128, TOK], F32)
    cx.sbuf("s_wsc", [1, 4], F32)
    cx.sbuf("s_wscb", [128, 4], F32)

    cx.push_scope()
    xnh = cx.sbuf("s_xnh", [128, NCH, TOK], BF16)
    xnl = cx.sbuf("s_xnl", [128, NCH, TOK], BF16)
    rb = cx.sbuf("s_rb", [128, TOK], F32)
    sb_s = cx.sbuf("s_qs", [1, TOK], F32)

    _emit_ln(cx, T, layer=0, xnh=xnh, xnl=xnl, rb=rb, sb_s=sb_s)
    if "d_xn" in dumps:
        _dump_xn(cx, T, dump_t["d_xn"], xnh, xnl)
    if "d_r" in dumps:
        VE.wait(nc.sync)
        nc.sync.dma_start(out=dump_t["d_r"][:, :], in_=sb_s[:, :]).then_inc(LD.h, 16); LD.n += 16
    if _WPREP:
        _emit_wprep(cx, T, dumps, dump_t)
    if stage >= 2:
        _emit_proj0(cx, T, xnh, xnl, dumps, dump_t)
    if stage >= 3 and _FFN0:
        _emit_ffn0_xq(cx, T, xnh, xnl, rb, sb_s, xq, u1g, u1u, dumps, dump_t)
    cx.pop_scope()

    if stage < 2:
        cx.pop_scope()
        _finish_stub(cx, T)
        return

    if stage >= 3 and _FFN0:
        _emit_ffn0_main(cx, T, xq, u1g, u1u, dumps, dump_t)
    cx.pop_scope()
    if stage < 4:
        _finish_stub(cx, T)
        return

    cx.push_scope()
    attn_sb = cx.sbuf("s_attn", [128, NCH, TOK], F32)
    _emit_attn0(cx, T, attn_sb, dumps, dump_t)
    if stage >= 5:
        _emit_wo(cx, T, attn_sb, w="w_o0", bias=None, layer=0)
    cx.pop_scope()
    if "d_x1" in dumps:
        _dump_xres(cx, T, dump_t["d_x1"])
    if stage < 6:
        _finish_stub(cx, T)
        return

    cx.push_scope()
    invrow = cx.sbuf("s_invrow", [1, TOK], F32)

    cx.push_scope()
    xnh = cx.sbuf("s_xnh", [128, NCH, TOK], BF16)
    xnl = cx.sbuf("s_xnl", [128, NCH, TOK], BF16)
    _emit_ln(cx, T, layer=1, xnh=xnh, xnl=xnl, rb=None, sb_s=None)
    if "d_xn1" in dumps:
        _dump_xn(cx, T, dump_t["d_xn1"], xnh, xnl)
    _emit_proj1(cx, T, xnh, xnl, dumps, dump_t)
    if stage >= 7:
        _emit_ffn1_gu(cx, T, xnh, xnl, invrow, dumps, dump_t)
    cx.pop_scope()
    if stage < 7:
        cx.pop_scope()
        _finish_stub(cx, T)
        return

    _emit_ffn1_proj(cx, T, invrow, dumps, dump_t)
    cx.pop_scope()
    if stage < 8:
        _finish_stub(cx, T)
        return

    cx.push_scope()
    attn_sb = cx.sbuf("s_attn", [128, NCH, TOK], F32)
    _emit_attn1(cx, T, attn_sb, dumps, dump_t)
    if stage >= 9:
        _emit_asm1(cx, T, attn_sb, dumps, dump_t)
    cx.pop_scope()

    _emit_output(cx, T)


def _tev_copy(cx, inst):
    s = cx.sem["TEV"]
    n = s.inc(inst)
    cx.nc.vector.wait_ge(s.h, n)
    return n


def _transpose_128xD(cx, src_load, xin, tstg, ident, dst_ap, tstg_ap=None,
                     per_tile=None, src_load2=None):
    nc = cx.nc
    LD, MM, TEV, TRP = cx.sem["LD"], cx.sem["MM"], cx.sem["TEV"], cx.sem["TRP"]
    SLD = cx.sem["SLD"]
    PT = [cx.ps["p4"], cx.ps["p5"]]
    for tt in range(NSLOT):
        LD.wait(nc.sync)
        if tt > 0:
            MM.wait(nc.sync)
            if src_load2 is not None:
                MM.wait(nc.scalar)
            if per_tile is not None:
                cx.sem["VE"].wait(nc.sync)
                cx.sem["SC"].wait(nc.sync)
                if src_load2 is not None:
                    cx.sem["VE"].wait(nc.scalar)
                    cx.sem["SC"].wait(nc.scalar)
        src_load(tt).then_inc(LD.h, 16); LD.n += 16
        if src_load2 is not None:
            src_load2(tt).then_inc(SLD.h, 16); SLD.n += 16
        if per_tile is not None:
            per_tile(tt, xin)
        LD.wait(nc.tensor)
        if tt > 0:
            TRP.wait(nc.tensor)
        for ch in range(NCH):
            if ch == 8 and src_load2 is not None:
                SLD.wait(nc.tensor)
            pt = PT[ch % 2]
            if ch >= 2:
                TEV.wait(nc.tensor, TEV.n - 1)
            MM.inc(nc.tensor.transpose(pt[:, 0:128], xin[:, ch * 128:(ch + 1) * 128], ident[:, :]))
            MM.wait(nc.vector)
            _tev_copy(cx, nc.vector.tensor_copy(tstg[:, ch * 128:(ch + 1) * 128], pt[:, 0:128]))
        TEV.wait(nc.sync)
        TRP.inc(nc.sync.dma_start(
            out=dst_ap(tt),
            in_=(tstg_ap if tstg_ap is not None else tstg[:, :])), 16)
    TRP.wait(nc.sync)


def _finish_stub(cx, T):
    _emit_output(cx, T)


def _emit_output(cx, T):
    nc = cx.nc
    cx.sem["GLD"].wait(nc.sync)
    cx.sem["GP"].wait(nc.sync)
    cx.sem["TRP"].wait(nc.sync)
    cx.push_scope()
    xf = cx.sbuf("s_xin", [128, D], F32)
    tstg = cx.sbuf("s_tstg", [128, NCH * 128], F32)
    _transpose_128xD(cx, src_load=lambda tt: nc.sync.dma_start(
        out=_apd(xf, 0, [[D, 128], [128, NCH], [1, 128]]),
        in_=_apd(T["x_res"], tt * 128, [[TOK, 128], [128 * TOK, NCH], [1, 128]])),
        xin=xf, tstg=tstg, ident=cx.sb["s_id32"],
        dst_ap=lambda tt: T["out_tok"][tt * 128:(tt + 1) * 128, :])
    cx.sem["LD"].wait(nc.sync)
    cx.pop_scope()


def _dump_xres(cx, T, dst):
    nc = cx.nc
    GLD, LD = cx.sem["GLD"], cx.sem["LD"]
    GLD.wait(nc.sync)
    LD.inc(nc.sync.dma_start(out=dst[:, :], in_=T["x_res"][:, :]), 16)
    LD.wait(nc.sync)


def _dump_xn(cx, T, dst, xnh, xnl):
    nc = cx.nc
    VE, LD = cx.sem["VE"], cx.sem["LD"]
    tmp = cx.sb.get("s_dmp") or cx.sbuf("s_dmp", [128, TOK], F32)
    for ch in range(NCH):
        LD.wait(nc.vector)
        cx.v(nc.vector.tensor_tensor(tmp[:, :], xnh[:, ch, :], xnl[:, ch, :], ALU.add))
        VE.wait(nc.sync)
        LD.inc(nc.sync.dma_start(out=dst[ch * 128:(ch + 1) * 128, :], in_=tmp[:, :]), 16)
        LD.wait(nc.sync)


def _emit_wprep(cx, T, dumps, dump_t):
    nc = cx.nc
    wsc = cx.sb["s_wsc"]
    wscb = cx.sb["s_wscb"]
    cx.push_scope()
    LD, GLD, MM, VE, SC, GP, CC = (cx.sem[k] for k in
                                   ("LD", "GLD", "MM", "VE", "SC", "GP", "CC"))
    identbf = cx.sb["s_idbf"]

    wsl = cx.sbuf("s_wsl", [128, 2, 2048], F32)
    wbs = cx.sbuf("s_wbs", [128, 2048], BF16)
    wq = cx.sbuf("s_wq", [128, 2048], BF16)
    wq32 = cx.sbuf("s_wq32", [128, 2048], F32)
    wqt = cx.sbuf("s_wqt", [128, 128], BF16)
    scl = cx.sbuf("s_wscl", [128, 32], F32)
    psums = cx.sbuf("s_wps", [128, 1], F32)
    acc128 = cx.sbuf("s_acc128", [128, 4], F32)
    acc4 = cx.sbuf("s_acc4", [1, 4], F32)
    g8 = cx.sbuf("s_g8", [8, 4], F32)

    do = _WPARTS
    cx.g(nc.gpsimd.memset(acc128[:, :], 0.0))
    GP.wait(nc.vector)
    specs0 = [("wg_sl", FFH // 8, D, 0), ("wu_sl", FFH // 8, D, 1), ("wd_sl", D // 8, FFH, 2)]
    SLD = cx.sem["SLD"]
    ve_wm = [0, 0]
    bufi = [0]

    def _wsl_load(name, rt, seg):
        buf = bufi[0] % 2
        bufi[0] += 1
        sem_, eng_ = (LD, nc.sync) if buf == 0 else (SLD, nc.scalar)
        VE.wait(eng_, ve_wm[buf])
        sem_.inc(eng_.dma_start(
            out=wsl[:, buf, :],
            in_=T[name][rt * 128:(rt + 1) * 128, seg * 2048:(seg + 1) * 2048]), 16)
        sem_.wait(nc.vector)
        return buf

    for name, rows, cols, slot in specs0:
        for rt in range(rows // 128):
            for seg in range(cols // 2048):
                buf = _wsl_load(name, rt, seg)
                ve_wm[buf] = cx.v(nc.vector.tensor_reduce(psums[:, :], wsl[:, buf, :],
                                               AX.X, ALU.add,
                                               apply_absolute_value=True))
                cx.v(nc.vector.tensor_tensor(acc128[:, slot:slot + 1], acc128[:, slot:slot + 1],
                                               psums[:, :], ALU.add))
    VE.wait(nc.tensor)
    MM.inc(nc.tensor.matmul(cx.ps["p0"][0:1, 0:4], cx.sb["s_ones"][:, :], acc128[:, :],
                            start=True, stop=True))
    MM.wait(nc.vector)
    cx.v(nc.vector.tensor_copy(acc4[:, :], cx.ps["p0"][0:1, 0:4]))
    VE.wait(nc.gpsimd)
    if 'ag0' in do:
        GLD.inc(nc.gpsimd.dma_start(out=T["ag0_in"][:, :], in_=acc4[:, :]), 16)
        GLD.wait(nc.gpsimd)
        CC.inc(nc.gpsimd.collective_compute(
            "AllGather", ALU.bypass, replica_groups=[list(range(8))],
            ins=[T["ag0_in"].ap().opt()], outs=[T["ag0_out"].ap().opt()]))
        CC.wait(nc.sync)
        LD.inc(nc.sync.dma_start(out=g8[:, :], in_=T["ag0_out"][:, :]), 16)
        LD.wait(nc.tensor)
        MM.inc(nc.tensor.matmul(cx.ps["p0"][0:1, 0:4], cx.sb["s_ones"][0:8, :], g8[:, :],
                                start=True, stop=True))
    else:
        VE.wait(nc.tensor)
        MM.inc(nc.tensor.matmul(cx.ps["p0"][0:1, 0:4], cx.sb["s_ones"][:, :], acc128[:, :],
                                start=True, stop=True))
    MM.wait(nc.vector)
    n0 = float(FFH * D) / (1.0 if 'ag0' in do else 8.0)
    cx.v(nc.vector.tensor_scalar(wsc[0:1, 0:4], cx.ps["p0"][0:1, 0:4], 1.0 / n0, 1e-5,
                                   ALU.mult, ALU.add))
    wrec = cx.sbuf("s_wrec", [1, 4], F32)
    cx.v(nc.vector.reciprocal(wrec[0:1, 0:4], wsc[0:1, 0:4]))
    if "d_wsc" in dumps:
        VE.wait(nc.sync)
        LD.inc(nc.sync.dma_start(out=dump_t["d_wsc"][:, :], in_=wsc[:, :]), 16)

    wrec_b = cx.sbuf("s_wrecb", [128, 4], F32)
    VE.wait(nc.tensor)
    MM.inc(nc.tensor.matmul(cx.ps["p0"][:, 0:4], cx.sb["s_onesr"][:, :], wrec[0:1, 0:4],
                            start=True, stop=True))
    MM.wait(nc.vector)
    cx.v(nc.vector.tensor_copy(wrec_b[:, :], cx.ps["p0"][:, 0:4]))
    VE.wait(nc.tensor)
    MM.inc(nc.tensor.matmul(cx.ps["p0"][:, 0:4], cx.sb["s_onesr"][:, :], wsc[0:1, 0:4],
                            start=True, stop=True))
    MM.wait(nc.vector)
    cx.v(nc.vector.tensor_copy(wscb[:, :], cx.ps["p0"][:, 0:4]))

    def quant_l0(name, rows, cols, slot, row_base):
        for rt in range(rows // 128):
            for seg in range(cols // 2048):
                buf = _wsl_load(name, rt, seg)
                cx.sem["TRP"].wait(nc.vector)
                GP.wait(nc.vector)
                ve_wm[buf] = cx.v(nc.vector.tensor_scalar(wq32[:, :], wsl[:, buf, :],
                                               wrec_b[:, slot:slot + 1], None, ALU.mult))
                cx.v(nc.vector.tensor_scalar(wbs[:, :], wq32[:, :], 0.5, None, ALU.is_gt))
                cx.v(nc.vector.tensor_scalar(wq32[:, :], wq32[:, :], -0.5, None, ALU.is_lt))
                VE.wait(nc.gpsimd)
                MM.wait(nc.gpsimd)
                cx.g(nc.gpsimd.tensor_tensor(wq[:, :], wbs[:, :], wq32[:, :],
                                               ALU.subtract))
                GP.wait(nc.tensor)
                _wq_transpose(cx, T, wq, 2048, row_base, rt, seg, cols)

    if 'l0' in do:
        quant_l0("wg_sl", FFH // 8, D, 0, 0)
        quant_l0("wu_sl", FFH // 8, D, 1, 2048)
        quant_l0("wd_sl", D // 8, FFH, 2, 4096)

    def quant_l1(name, rows, cols, row_base):
        ntile = rows // 128
        nseg = cols // 2048
        for rt in range(ntile):
            for seg in range(nseg):
                buf = _wsl_load(name, rt, seg)
                cx.sem["TRP"].wait(nc.vector)
                GP.wait(nc.vector)
                ve_wm[buf] = cx.v(nc.vector.tensor_copy(wbs[:, :2048], wsl[:, buf, :]))
                cx.v(nc.vector.tensor_reduce(
                    scl[:, :], _apd(wbs, 0, [[2048, 128], [64, 32], [1, 64]]),
                    AX.X, ALU.add, apply_absolute_value=True))
                sclb = cx.sb.get("s_sclb") or cx.sbuf("s_sclb", [128, 32], BF16)
                cx.v(nc.vector.tensor_scalar(sclb[:, :], scl[:, :], 1.0 / 64.0, None, ALU.mult))
                cx.v(nc.vector.tensor_scalar_max(scl[:, :], sclb[:, :], 1e-8))
                recg = cx.sb.get("s_recg") or cx.sbuf("s_recg", [128, 32], F32)
                cx.v(nc.vector.reciprocal(recg[:, :], scl[:, :]))
                qv = cx.sb.get("s_qv") or cx.sbuf("s_qv", [128, 2048], BF16)
                cx.v(nc.vector.tensor_tensor(
                    _apd(qv, 0, [[2048, 128], [64, 32], [1, 64]]),
                    _apd(wbs, 0, [[2048, 128], [64, 32], [1, 64]]),
                    _apd(recg, 0, [[32, 128], [1, 32], [0, 64]]), ALU.mult))
                t1 = cx.sb.get("s_t1w") or cx.sbuf("s_t1w", [128, 2048], F32)
                cx.v(nc.vector.tensor_scalar(t1[:, :2048], qv[:, :2048], 0.5, None, ALU.is_gt))
                MM.wait(nc.vector)
                cx.v(nc.vector.tensor_scalar(wq[:, :2048], qv[:, :2048], -0.5, None, ALU.is_lt))
                VE.wait(nc.gpsimd)
                cx.g(nc.gpsimd.tensor_tensor(t1[:, :2048], t1[:, :2048], wq[:, :2048],
                                               ALU.subtract))
                GP.wait(nc.vector)
                cx.v(nc.vector.tensor_tensor(
                    _apd(wq, 0, [[2048, 128], [64, 32], [1, 64]]),
                    _apd(t1, 0, [[2048, 128], [64, 32], [1, 64]]),
                    _apd(scl, 0, [[32, 128], [1, 32], [0, 64]]), ALU.mult))
                _wq_transpose(cx, T, wq, 2048, row_base, rt, seg, cols)

    if 'l1' in do:
        quant_l1("wgu_sl", GUH // 8, D, 6144)
        quant_l1("wp_sl", D // 8, FFH, 10240)

    if 'ag1' in do:
        cx.sem["TRP"].wait(nc.gpsimd)
        CC.inc(nc.gpsimd.collective_compute(
            "AllGather", ALU.bypass, replica_groups=[list(range(8))],
            ins=[T["ag1_in"].ap().opt()], outs=[T["ag1_out"].ap().opt()]))
        if "d_ag1" in dumps:
            CC.wait(nc.sync)
            LD.inc(nc.sync.dma_start(out=dump_t["d_ag1"][:, :],
                                     in_=T["ag1_out"][0:2048, :]), 16)
            LD.wait(nc.sync)
        if "d_ag1d" in dumps:
            CC.wait(nc.sync)
            LD.inc(nc.sync.dma_start(out=dump_t["d_ag1d"][:, :],
                                     in_=T["ag1_out"][6144:8192, :]), 16)
            LD.wait(nc.sync)
    cx.pop_scope()


def _wq_transpose(cx, T, wq, cols, row_base, rt, seg, full_cols):
    nc = cx.nc
    MM, VE, TRP = cx.sem["MM"], cx.sem["VE"], cx.sem["TRP"]
    identbf = cx.sb["s_idbf"]
    wqt = cx.sb["s_wqt"]
    slice_rows = {2048: 1024, 8192: 256, 16384: 2048}
    srows = {0: 1024, 2048: 1024, 4096: 256, 6144: 2048, 10240: 256}[row_base]
    TEV = cx.sem["TEV"]
    cx.sem["VE"].wait(nc.tensor)
    cx.sem["GP"].wait(nc.tensor)
    for c in range(cols // 128):
        pt = cx.ps["ptb%d" % (c % 2)]
        if TEV.n >= 1:
            TEV.wait(nc.tensor, TEV.n - 1)
        MM.inc(nc.tensor.transpose(pt[:, :], wq[:, c * 128:(c + 1) * 128], identbf[:, :]))
        MM.wait(nc.vector)
        TRP.wait(nc.vector)
        _tev_copy(cx, nc.vector.tensor_copy(wqt[:, :], pt[:, :]))
        TEV.wait(nc.sync)
        i0 = seg * 2048 + c * 128
        flat = row_base * 1024 + i0 * srows + rt * 128
        TRP.inc(nc.sync.dma_start(
            out=_apd(T["ag1_in"], flat, [[srows, 128], [1, 128]]),
            in_=wqt[:, :]), 16)


def _bcast_row(cx, row_ap_fn, dst, n=TOK):
    nc = cx.nc
    MM, VE = cx.sem["MM"], cx.sem["VE"]
    P0 = cx.ps["p0"]
    VE.wait(nc.tensor)
    for th in range(n // 512):
        MM.inc(nc.tensor.matmul(P0[:, :], cx.sb["s_onesr"][:, :],
                                row_ap_fn(th), start=True, stop=True))
        MM.wait(nc.vector)
        cx.v(nc.vector.tensor_copy(dst[:, th * 512:(th + 1) * 512], P0[:, :]))
        VE.wait(nc.tensor)


def _emit_ln(cx, T, layer, xnh, xnl, rb, sb_s):
    nc = cx.nc
    cx.push_scope()
    LD, GLD, MM, VE, SC, GP = (cx.sem[k] for k in ("LD", "GLD", "MM", "VE", "SC", "GP"))
    P0, P1, P2, P3 = (cx.ps[f"p{i}"] for i in range(4))
    xch = [cx.sbuf("s_lnx0", [128, TOK], F32),
           cx.sbuf("s_lnx1", [128, TOK], F32)]
    xsq = cx.sbuf("s_lnsq", [128, TOK], F32)
    stat = cx.sbuf("s_stat", [1, 3 * TOK], F32)
    lns2 = cx.sbuf("s_lns2", [1, TOK], F32)
    invb = cx.sbuf("s_invb", [128, TOK], F32)
    m2b = cx.sbuf("s_m2b", [128, TOK], F32)

    cx.sem["TRP"].wait(nc.sync)
    GLD.wait(nc.sync)
    GLD.wait(nc.tensor)

    mu = stat[0:1, 0:TOK]
    m2 = stat[0:1, 1 * TOK:2 * TOK]
    var = stat[0:1, 2 * TOK:3 * TOK]
    inv = var

    if layer == 0:
        LD.inc(nc.sync.dma_start(out=mu[:, :], in_=T["st_mu"][:, :]), 16)
        LD.inc(nc.sync.dma_start(out=var[:, :], in_=T["st_sq"][:, :]), 16)
        LD.inc(nc.sync.dma_start(out=lns2[0:1, :], in_=T["st_mx"][:, :]), 16)
        LD.wait(nc.vector)
        cx.v(nc.vector.tensor_tensor(m2[:, :], mu[:, :], mu[:, :], ALU.mult))
        cx.v(nc.vector.tensor_tensor(var[:, :], var[:, :], m2[:, :], ALU.subtract))
    else:
        for ch in range(NCH):
            buf = xch[ch % 2]
            if ch >= 2:
                MM.wait(nc.sync)
                SC.wait(nc.sync)
            LD.inc(nc.sync.dma_start(out=buf[:, :],
                                     in_=_apd(T["x_res"], ch * 128 * TOK,
                                              [[TOK, 128], [1, TOK]])), 16)
            LD.wait(nc.scalar)
            cx.s(nc.scalar.activation(xsq[:, :], buf[:, :], AF.Square, bias=0.0, scale=1.0))
            LD.wait(nc.tensor)
            SC.wait(nc.tensor)
            for th in range(2):
                MM.inc(nc.tensor.matmul(P0[0:1, :] if th == 0 else P1[0:1, :],
                                        cx.sb["s_ones"][:, :], buf[:, th * 512:(th + 1) * 512],
                                        start=(ch == 0), stop=(ch == NCH - 1)))
                MM.inc(nc.tensor.matmul(P2[0:1, :] if th == 0 else P3[0:1, :],
                                        cx.sb["s_ones"][:, :], xsq[:, th * 512:(th + 1) * 512],
                                        start=(ch == 0), stop=(ch == NCH - 1)))
        MM.wait(nc.vector)
        for th, (ps_s, ps_q) in enumerate([(P0, P2), (P1, P3)]):
            sl = slice(th * 512, (th + 1) * 512)
            cx.v(nc.vector.tensor_scalar(mu[:, sl], ps_s[0:1, :], 1.0 / D, None, ALU.mult))
            cx.v(nc.vector.tensor_scalar(var[:, sl], ps_q[0:1, :], 1.0 / D, None, ALU.mult))
        cx.v(nc.vector.tensor_tensor(m2[:, :], mu[:, :], mu[:, :], ALU.mult))
        cx.v(nc.vector.tensor_tensor(var[:, :], var[:, :], m2[:, :], ALU.subtract))

    cx.v(nc.vector.tensor_scalar(var[:, :], var[:, :], 1e-5, None, ALU.add))
    VE.wait(nc.scalar)
    cx.s(nc.scalar.activation(inv[:, :], var[:, :], AF.Sqrt, bias=0.0, scale=1.0))
    SC.wait(nc.vector)
    cx.v(nc.vector.reciprocal(inv[:, :], inv[:, :]))
    cx.v(nc.vector.tensor_tensor(m2[:, :], mu[:, :], inv[:, :], ALU.mult))
    _bcast_row(cx, lambda th: inv[:, th * 512:(th + 1) * 512], invb)
    _bcast_row(cx, lambda th: m2[:, th * 512:(th + 1) * 512], m2b)

    for ch in range(NCH):
        buf = xch[ch % 2]
        VE.wait(nc.sync)
        MM.wait(nc.sync)
        SC.wait(nc.sync)
        LD.inc(nc.sync.dma_start(out=buf[:, :],
                                 in_=_apd(T["x_res"], ch * 128 * TOK, [[TOK, 128], [1, TOK]])), 16)
        LD.wait(nc.vector)
        xn = xsq
        cx.v(nc.vector.scalar_tensor_tensor(xn[:, :], buf[:, :], 1.0, invb[:, :],
                                              ALU.mult, ALU.mult))
        cx.v(nc.vector.tensor_tensor(xn[:, :], xn[:, :], m2b[:, :], ALU.subtract))
        if layer == 0:
            VE.wait(nc.sync)
            cx.sem["TRP"].inc(nc.sync.dma_start(
                out=_apd(T["xn_hbm"], ch * 128 * TOK, [[TOK, 128], [1, TOK]]),
                in_=xn[:, :]), 16)
        cx.v(nc.vector.tensor_copy(xnh[:, ch, :], xn[:, :]))
        cx.v(nc.vector.tensor_tensor(xnl[:, ch, :], xn[:, :], xnh[:, ch, :], ALU.subtract))

    if layer == 0:
        amx = lns2[0:1, :]
        t = mu
        cx.v(nc.vector.tensor_tensor(t[:, :], amx[:, :], inv[:, :], ALU.mult))
        cx.v(nc.vector.tensor_scalar(sb_s[0:1, :], t[:, :], 1.0 / 127.0, 1e-5,
                                       ALU.mult, ALU.add))
        r0 = amx
        cx.v(nc.vector.reciprocal(r0[:, :], sb_s[0:1, :]))
        cx.v(nc.vector.tensor_tensor(t[:, :], sb_s[0:1, :], r0[:, :], ALU.mult))
        cx.v(nc.vector.tensor_scalar(t[:, :], t[:, :], -1.0, 2.0, ALU.mult, ALU.add))
        cx.v(nc.vector.tensor_tensor(r0[:, :], r0[:, :], t[:, :], ALU.mult))
        _bcast_row(cx, lambda th: r0[:, th * 512:(th + 1) * 512], rb)
    cx.pop_scope()


def _load_w_hilo(cx, T, name, col0, ncols, whi, wlo, w32, buf, key=None):
    nc = cx.nc
    LD, GLD, GP, MM = cx.sem["LD"], cx.sem["GLD"], cx.sem["GP"], cx.sem["MM"]
    if not hasattr(cx, "wmk"):
        cx.wmk = {}
        cx.wmk_gp = {}
    wm = cx.wmk.setdefault(key or name, [0, 0])
    wmg = cx.wmk_gp.setdefault(key or name, [0, 0])
    SLD = cx.sem["SLD"]
    wt = T[name]
    wcols = wt.shape[1]
    src = _apd(wt, col0, [[wcols, 128], [wcols * 128, 16], [1, ncols]])
    srcA = _apd(wt, col0, [[wcols, 128], [wcols * 128, 8], [1, ncols]])
    srcB = _apd(wt, col0 + 8 * wcols * 128, [[wcols, 128], [wcols * 128, 8], [1, ncols]])
    MM.wait(nc.gpsimd, wm[buf])
    MM.wait(nc.sync, wm[buf])
    MM.wait(nc.scalar, wm[buf])
    GLD.inc(nc.gpsimd.dma_start(out=whi[:, buf, :, :], in_=src), 16)
    GP.wait(nc.sync, wmg[buf])
    GP.wait(nc.scalar, wmg[buf])
    LD.inc(nc.sync.dma_start(out=w32[:, buf, 0:8, 0:ncols], in_=srcA), 16)
    SLD.inc(nc.scalar.dma_start(out=w32[:, buf, 8:16, 0:ncols], in_=srcB), 16)
    GLD.wait(nc.gpsimd)
    LD.wait(nc.gpsimd)
    SLD.wait(nc.gpsimd)
    wmg[buf] = cx.g(nc.gpsimd.tensor_tensor(wlo[:, buf, :, :],
                                            w32[:, buf, :, 0:ncols],
                                            whi[:, buf, :, :], ALU.subtract))


def _wmark(cx, key, buf):
    cx.wmk.setdefault(key, [0, 0])[buf] = cx.sem["MM"].n



def _load_w_hilo2(cx, T, name, col0, ncols, whi, wlo, w32):
    nc = cx.nc
    LD, GLD, GP, MM = cx.sem["LD"], cx.sem["GLD"], cx.sem["GP"], cx.sem["MM"]
    SLD = cx.sem["SLD"]
    wt = T[name]
    wcols = wt.shape[1]
    src = _apd(wt, col0, [[wcols, 128], [wcols * 128, 16], [1, ncols]])
    srcA = _apd(wt, col0, [[wcols, 128], [wcols * 128, 8], [1, ncols]])
    srcB = _apd(wt, col0 + 8 * wcols * 128, [[wcols, 128], [wcols * 128, 8], [1, ncols]])
    MM.wait(nc.gpsimd)
    MM.wait(nc.sync)
    MM.wait(nc.scalar)
    GLD.inc(nc.gpsimd.dma_start(out=whi[:, :, :], in_=src), 16)
    GP.wait(nc.sync)
    GP.wait(nc.scalar)
    LD.inc(nc.sync.dma_start(out=w32[:, 0:8, 0:ncols], in_=srcA), 16)
    SLD.inc(nc.scalar.dma_start(out=w32[:, 8:16, 0:ncols], in_=srcB), 16)
    GLD.wait(nc.gpsimd)
    LD.wait(nc.gpsimd)
    SLD.wait(nc.gpsimd)
    cx.g(nc.gpsimd.tensor_tensor(wlo[:, :, :],
                                 w32[:, :, 0:ncols],
                                 whi[:, :, :], ALU.subtract))


def _rotary_evac(cx, ps, psB, th, qrot, q_sb):
    nc = cx.nc
    MM, VE = cx.sem["MM"], cx.sem["VE"]
    cos = cx.sb["s_cos"]
    sin = cx.sb["s_sin"]
    ta = cx.sb["s_rota"]
    sl = slice(th * 512, (th + 1) * 512)
    MM.wait(nc.vector)
    cx.v(nc.vector.tensor_copy(q_sb[:, :], ps[:, :]))
    VE.wait(nc.tensor)
    MM.inc(nc.tensor.matmul(psB[:, :], cx.sb["s_swap"][:, :], q_sb[:, :],
                            start=True, stop=True))
    cx.v(nc.vector.tensor_tensor(qrot[:, :], q_sb[:, :], cos[:, sl], ALU.mult))
    MM.wait(nc.vector)
    cx.v(nc.vector.scalar_tensor_tensor(ta[:, :], psB[:, :], 1.0, sin[:, sl],
                                        ALU.mult, ALU.mult))
    cx.v(nc.vector.tensor_tensor(qrot[:, :], qrot[:, :], ta[:, :], ALU.add))


def _emit_proj0(cx, T, xnh, xnl, dumps, dump_t):
    nc = cx.nc
    cx.push_scope()
    LD, GLD, MM, VE, SC, GP, CC = (cx.sem[k] for k in
                                   ("LD", "GLD", "MM", "VE", "SC", "GP", "CC"))
    P = [cx.ps[f"p{i}"] for i in range(6)]
    cx.push_scope()
    whi = cx.sbuf("s_pwhi", [128, 2, 16, 128], BF16)
    wlo = cx.sbuf("s_pwlo", [128, 2, 16, 128], BF16)
    w32 = cx.sbuf("s_pw32", [128, 2, 16, 256], F32)
    qrot = cx.sb.get("s_qrot") or cx.sbuf("s_qrot", [128, 512], F32)
    qh = cx.sb.get("s_qh") or cx.sbuf("s_qh", [128, 512], BF16)
    ql = cx.sb.get("s_ql") or cx.sbuf("s_ql", [128, 512], BF16)
    cx.sbuf("s_rota", [128, 512], F32) if "s_rota" not in cx.sb else None
    cx.sbuf("s_qsb", [128, 512], F32) if "s_qsb" not in cx.sb else None

    def xnh_sl(ic, th):
        return _apd(xnh, ic * TOK + th * 512, [[NCH * TOK, 128], [1, 512]])

    def xnl_sl(ic, th):
        return _apd(xnl, ic * TOK + th * 512, [[NCH * TOK, 128], [1, 512]])

    VE.wait(nc.tensor)
    for oc in range(18):
        buf = oc % 2
        _load_w_hilo(cx, T, "w_qkv", oc * 128, 128, whi, wlo, w32, buf, key="qkv")
        GP.wait(nc.tensor)
        for th in range(2):
            ps = P[(oc * 2 + th) % 2]
            psB = P[2 + (oc * 2 + th) % 2]
            VE.wait(nc.tensor)
            for ic in range(NCH):
                MM.inc(nc.tensor.matmul(ps[:, :], whi[:, buf, ic, :], xnh_sl(ic, th),
                                        start=(ic == 0), stop=False))
                MM.inc(nc.tensor.matmul(ps[:, :], whi[:, buf, ic, :], xnl_sl(ic, th),
                                        start=False, stop=False))
            for ic in range(NCH):
                MM.inc(nc.tensor.matmul(ps[:, :], wlo[:, buf, ic, :], xnh_sl(ic, th),
                                        start=False, stop=(ic == NCH - 1)))
            MM.wait(nc.vector)
            _rotary_evac(cx, ps, psB, th, qrot, cx.sb["s_qsb"])
            LD.wait(nc.vector)
            cx.v(nc.vector.tensor_copy(qh[:, :], qrot[:, :]))
            cx.v(nc.vector.tensor_tensor(ql[:, :], qrot[:, :], qh[:, :], ALU.subtract))
            VE.wait(nc.sync)
            if oc < 16:
                for half, t in ((0, qh), (1, ql)):
                    LD.inc(nc.sync.dma_start(
                        out=_apd(T["q_spill"], (half * 2048 + oc * 128) * TOK + th * 512,
                                 [[TOK, 128], [1, 512]]),
                        in_=t[:, :]), 16)
            else:
                kvh = oc - 16
                for half, t in ((0, qh), (1, ql)):
                    LD.inc(nc.sync.dma_start(
                        out=_apd(T["ag2_in"], (kvh * 256 + half * 128) * 1024 + th * 512,
                                 [[1024, 128], [1, 512]]),
                        in_=t[:, :]), 16)
        _wmark(cx, "qkv", buf)

    cx.pop_scope()
    wvhi = cx.sb.get("s_wvhi") or cx.sbuf("s_wvhi", [128, 16, 256], BF16)
    wvlo = cx.sb.get("s_wvlo") or cx.sbuf("s_wvlo", [128, 16, 256], BF16)
    wv32 = cx.sb.get("s_wv32") or cx.sbuf("s_wv32", [128, 16, 256], F32)
    _load_w_hilo2(cx, T, "w_qkv", 2304, 256, wvhi, wvlo, wv32)
    GP.wait(nc.tensor)
    for tt in range(NSLOT):
        ps = P[4 + tt % 2]
        VE.wait(nc.tensor)
        for ic in range(NCH):
            xh = _apd(xnh, ic * TOK + tt * 128, [[NCH * TOK, 128], [1, 128]])
            xl = _apd(xnl, ic * TOK + tt * 128, [[NCH * TOK, 128], [1, 128]])
            MM.inc(nc.tensor.matmul(ps[:, 0:256], xh, wvhi[:, ic, :],
                                    start=(ic == 0), stop=False))
            MM.inc(nc.tensor.matmul(ps[:, 0:256], xl, wvhi[:, ic, :],
                                    start=False, stop=False))
            MM.inc(nc.tensor.matmul(ps[:, 0:256], xh, wvlo[:, ic, :],
                                    start=False, stop=(ic == NCH - 1)))
        MM.wait(nc.vector)
        LD.wait(nc.vector)
        cx.v(nc.vector.tensor_copy(qh[:, 0:256], ps[:, 0:256]))
        cx.v(nc.vector.tensor_tensor(ql[:, 0:256], ps[:, 0:256], qh[:, 0:256], ALU.subtract))
        VE.wait(nc.sync)
        for kvh in range(2):
            for half, t in ((0, qh), (1, ql)):
                LD.inc(nc.sync.dma_start(
                    out=_apd(T["ag2_in"], (512 + (kvh * 2 + half) * 128) * 1024 + tt * 128 * 128,
                             [[128, 128], [1, 128]]),
                    in_=t[:, kvh * 128:(kvh + 1) * 128]), 16)

    LD.wait(nc.gpsimd)
    CC.inc(nc.gpsimd.collective_compute(
        "AllGather", ALU.bypass,
        replica_groups=[[0, 1, 2, 3], [4, 5, 6, 7]],
        ins=[T["ag2_in"].ap().opt()], outs=[T["ag2_out"].ap().opt()]))
    if "d_q0" in dumps:
        LD.wait(nc.sync)
        LD.inc(nc.sync.dma_start(out=dump_t["d_q0"][:, :], in_=T["q_spill"][:, :]), 16)
        LD.wait(nc.sync)
    if "d_ag2" in dumps:
        CC.wait(nc.sync)
        LD.inc(nc.sync.dma_start(out=dump_t["d_ag2"][:, :], in_=T["ag2_out"][:, :]), 16)
        LD.wait(nc.sync)
    cx.pop_scope()


def _part_max_tree(cx, hmx, n, ps):
    nc = cx.nc
    MM, VE = cx.sem["MM"], cx.sem["VE"]
    rotm = cx.sb["s_rotm"]
    for j in range(7):
        VE.wait(nc.tensor)
        MM.inc(nc.tensor.matmul(ps[:, 0:n], rotm[:, j, :], hmx[:, 0:n],
                                start=True, stop=True))
        MM.wait(nc.vector)
        cx.v(nc.vector.tensor_tensor(hmx[:, 0:n], hmx[:, 0:n], ps[:, 0:n], ALU.max))


def _emit_ffn0_xq(cx, T, xnh, xnl, rb, sb_s, xq, u1g, u1u, dumps, dump_t):
    nc = cx.nc
    LD, VE = cx.sem["LD"], cx.sem["VE"]
    wsc = cx.sb["s_wsc"]
    xsc = cx.sbuf("s_xqsc", [128, TOK], F32)
    urow = cx.sbuf("s_urow", [1, TOK], F32)
    cx.sem["TRP"].wait(nc.sync)
    for ch in range(NCH):
        VE.wait(nc.sync)
        LD.inc(nc.sync.dma_start(
            out=xsc[:, :],
            in_=_apd(T["xn_hbm"], ch * 128 * TOK, [[TOK, 128], [1, TOK]])), 16)
        LD.wait(nc.vector)
        cx.v(nc.vector.tensor_tensor(xsc[:, :], xsc[:, :], rb[:, :], ALU.mult))
        cx.v(nc.vector.tensor_scalar(xq[:, ch, :], xsc[:, :], MAGIC, MAGIC, ALU.add, ALU.subtract))
    if "d_xq" in dumps:
        VE.wait(nc.sync)
        LD.inc(nc.sync.dma_start(
            out=_apd(dump_t["d_xq"], 0, [[TOK, 128], [128 * TOK, NCH], [1, TOK]]),
            in_=_apd(xq, 0, [[NCH * TOK, 128], [TOK, NCH], [1, TOK]])), 16)
        LD.wait(nc.sync)
    cx.v(nc.vector.tensor_scalar(urow[0:1, :], sb_s[0:1, :],
                                 wsc[0:1, 0:1], None, ALU.mult))
    _bcast_row(cx, lambda th: urow[0:1, th * 512:(th + 1) * 512], u1g)
    cx.v(nc.vector.tensor_scalar(urow[0:1, :], sb_s[0:1, :],
                                 wsc[0:1, 1:2], None, ALU.mult))
    _bcast_row(cx, lambda th: urow[0:1, th * 512:(th + 1) * 512], u1u)


def _emit_ffn0_main(cx, T, xq, u1g, u1u, dumps, dump_t):
    nc = cx.nc
    cx.push_scope()
    LD, GLD, MM, VE, SC, GP, CC = (cx.sem[k] for k in
                                   ("LD", "GLD", "MM", "VE", "SC", "GP", "CC"))
    P = [cx.ps[f"p{i}"] for i in range(6)]
    hbuf = cx.sbuf("s_h", [128, 64, TFF], F32)
    hq = cx.sbuf("s_hq", [128, 64, TFF], BF16)
    wg = cx.sbuf("s_fwg", [128, 2, 16, 128], BF16)
    wu = cx.sbuf("s_fwu", [128, 2, 16, 128], BF16)
    wd = cx.sbuf("s_fwd", [128, 64, 128], BF16)
    u2b = cx.sbuf("s_u2b", [128, TFF], F32)
    r2b = cx.sbuf("s_r2b", [128, TFF], F32)
    hmx = cx.sbuf("s_hmx", [128, TFF], F32)
    tg = cx.sbuf("s_tg", [128, TFF], F32)
    gs = cx.sbuf("s_gs", [128, TFF], F32)
    tu = cx.sbuf("s_tu", [128, TFF], F32)
    fout = cx.sbuf("s_fout", [128, 2, TFF], F32)
    wsc = cx.sb["s_wsc"]

    CC.wait(nc.sync)
    CC.wait(nc.gpsimd)
    CC.wait(nc.tensor)

    def gate_tile_ap(oc, ic):
        s = oc // 8
        off = (s * 12288 + 0) * 1024 + (ic * 128) * 1024 + (oc % 8) * 128
        return _apd(T["ag1_out"], off, [[1024, 128], [1, 128]])

    def up_tile_ap(oc, ic):
        s = oc // 8
        off = (s * 12288 + 2048) * 1024 + (ic * 128) * 1024 + (oc % 8) * 128
        return _apd(T["ag1_out"], off, [[1024, 128], [1, 128]])

    def down_tile_ap(od, ic):
        s = od // 2
        off = (s * 12288 + 4096) * 1024 + (ic * 128) * 256 + (od % 2) * 128
        return _apd(T["ag1_out"], off, [[256, 128], [1, 128]])

    fo_ve0 = [0, 0]
    for slab in range(NSLAB):
        csl = slice(slab * TFF, (slab + 1) * TFF)
        VE.wait(nc.gpsimd)
        cx.g(nc.gpsimd.memset(hmx[:, :], 0.0))
        GP.wait(nc.vector)
        gu_mm = [0, 0]
        gu_ve = [fo_ve0[0], fo_ve0[1]]
        for oc in range(64):
            buf = oc % 2
            MM.wait(nc.sync, gu_mm[buf])
            LD.inc(nc.sync.dma_start(
                out=_apd(wg, buf * 16 * 128, [[2 * 16 * 128, 128], [128, 16], [1, 128]]),
                in_=_apd(T["ag1_out"], (oc // 8 * 12288) * 1024 + (oc % 8) * 128,
                         [[1024, 128], [128 * 1024, 16], [1, 128]])), 16)
            LD.inc(nc.sync.dma_start(
                out=_apd(wu, buf * 16 * 128, [[2 * 16 * 128, 128], [128, 16], [1, 128]]),
                in_=_apd(T["ag1_out"], (oc // 8 * 12288 + 2048) * 1024 + (oc % 8) * 128,
                         [[1024, 128], [128 * 1024, 16], [1, 128]])), 16)
            LD.wait(nc.tensor)
            psA, psB = P[oc % 2], P[2 + oc % 2]
            VE.wait(nc.tensor, gu_ve[oc % 2])
            for ic in range(NCH):
                MM.inc(nc.tensor.matmul(psA[:, 0:TFF], wg[:, buf, ic, :],
                                        _apd(xq, ic * TOK + slab * TFF, [[NCH * TOK, 128], [1, TFF]]),
                                        start=(ic == 0), stop=(ic == NCH - 1)))
            for ic in range(NCH):
                MM.inc(nc.tensor.matmul(psB[:, 0:TFF], wu[:, buf, ic, :],
                                        _apd(xq, ic * TOK + slab * TFF, [[NCH * TOK, 128], [1, TFF]]),
                                        start=(ic == 0), stop=(ic == NCH - 1)))
            MM.wait(nc.vector)
            MM.wait(nc.scalar)
            cx.v(nc.vector.scalar_tensor_tensor(tg[:, :], psA[:, 0:TFF], 1.0, u1g[:, csl],
                                                ALU.mult, ALU.mult))
            VE.wait(nc.scalar)
            cx.s(nc.scalar.activation(gs[:, :], tg[:, :], AF.Silu, bias=0.0, scale=1.0))
            gu_ve[oc % 2] = cx.v(nc.vector.scalar_tensor_tensor(
                tu[:, :], psB[:, 0:TFF], 1.0, u1u[:, csl], ALU.mult, ALU.mult))
            gu_mm[buf] = MM.n
            SC.wait(nc.vector)
            cx.v(nc.vector.tensor_tensor(hbuf[:, oc, :], tu[:, :], gs[:, :], ALU.mult))
            VE.wait(nc.scalar)
            cx.s(nc.scalar.activation(tg[:, :], hbuf[:, oc, :], AF.Abs, bias=0.0, scale=1.0))
            SC.wait(nc.vector)
            cx.v(nc.vector.tensor_tensor(hmx[:, :], hmx[:, :], tg[:, :], ALU.max))
        if "d_h" in dumps and slab == 0:
            VE.wait(nc.sync)
            LD.inc(nc.sync.dma_start(
                out=_apd(dump_t["d_h"], 0, [[TFF, 128], [128 * TFF, 64], [1, TFF]]),
                in_=_apd(hbuf, 0, [[64 * TFF, 128], [TFF, 64], [1, TFF]])), 16)
            LD.wait(nc.sync)
        GP.wait(nc.vector)
        _part_max_tree(cx, hmx, TFF, P[5])
        s2b = tg
        cx.v(nc.vector.tensor_scalar(s2b[:, :], hmx[:, :], 1.0 / 127.0, 1e-5,
                                     ALU.mult, ALU.add))
        cx.v(nc.vector.reciprocal(r2b[:, :], s2b[:, :]))
        cx.v(nc.vector.tensor_tensor(hmx[:, :], s2b[:, :], r2b[:, :], ALU.mult))
        cx.v(nc.vector.tensor_scalar(hmx[:, :], hmx[:, :], -1.0, 2.0, ALU.mult, ALU.add))
        cx.v(nc.vector.tensor_tensor(r2b[:, :], r2b[:, :], hmx[:, :], ALU.mult))
        cx.v(nc.vector.tensor_scalar(u2b[:, :], s2b[:, :], cx.sb["s_wscb"][:, 2:3],
                                     None, ALU.mult))
        ve_hq = []
        for oc in range(64):
            cx.v(nc.vector.tensor_tensor(tg[:, :], hbuf[:, oc, :], r2b[:, :], ALU.mult))
            ve_hq.append(cx.v(nc.vector.tensor_scalar(hq[:, oc, :], tg[:, :], MAGIC, MAGIC,
                                         ALU.add, ALU.subtract)))
        if "d_hq" in dumps and slab == 0:
            VE.wait(nc.sync)
            LD.inc(nc.sync.dma_start(
                out=_apd(dump_t["d_hq"], 0, [[TFF, 128], [128 * TFF, 64], [1, TFF]]),
                in_=_apd(hq, 0, [[64 * TFF, 128], [TFF, 64], [1, TFF]])), 16)
            LD.wait(nc.sync)
        SLD = cx.sem["SLD"]
        for od in range(NCH):
            MM.wait(nc.sync)
            MM.wait(nc.scalar)
            LD.inc(nc.sync.dma_start(
                out=wd[:, 0:32, :],
                in_=_apd(T["ag1_out"],
                         (od // 2 * 12288 + 4096) * 1024 + (od % 2) * 128,
                         [[256, 128], [128 * 256, 32], [1, 128]])), 16)
            ldh0 = LD.n
            SLD.inc(nc.scalar.dma_start(
                out=wd[:, 32:64, :],
                in_=_apd(T["ag1_out"],
                         (od // 2 * 12288 + 4096) * 1024 + 32 * 128 * 256 + (od % 2) * 128,
                         [[256, 128], [128 * 256, 32], [1, 128]])), 16)
            ldh1 = SLD.n
            ps = P[od % 2]
            VE.wait(nc.tensor, fo_ve0[od % 2])
            for ic in range(64):
                if od == 0 and ic % 8 == 0:
                    VE.wait(nc.tensor, ve_hq[ic + 7])
                if ic == 0:
                    LD.wait(nc.tensor, ldh0)
                if ic == 32:
                    SLD.wait(nc.tensor, ldh1)
                MM.inc(nc.tensor.matmul(ps[:, 0:TFF], wd[:, ic, :], hq[:, ic, :],
                                        start=(ic == 0), stop=(ic == 63)))
            MM.wait(nc.vector)
            GLD.wait(nc.vector)
            fo_ve0[od % 2] = cx.v(nc.vector.scalar_tensor_tensor(fout[:, od % 2, :], ps[:, 0:TFF], 1.0,
                                                u2b[:, :], ALU.mult, ALU.mult))
            VE.wait(nc.gpsimd)
            GLD.inc(nc.gpsimd.dma_start(
                out=_apd(T["x_res"], od * 128 * TOK + slab * TFF, [[TOK, 128], [1, TFF]]),
                in_=fout[:, od % 2, :], accum_op=ALU.add), 16)
    cx.pop_scope()


def _emit_attn0(cx, T, attn_sb, dumps, dump_t):
    nc = cx.nc
    cx.push_scope()
    LD, GLD, MM, VE, SC, GP, CC = (cx.sem[k] for k in
                                   ("LD", "GLD", "MM", "VE", "SC", "GP", "CC"))
    P = [cx.ps[f"p{i}"] for i in range(6)]
    kb = cx.sbuf("s_kb", [128, 2, 2, 32, 128], BF16)
    vb = cx.sbuf("s_vb", [128, 2, 2, 32, 128], BF16)
    qblk = cx.sbuf("s_qblk", [128, 2, 2, 4, 128], BF16)
    pf = cx.sbuf("s_pf", [128, 2, 512], F32)
    ph = cx.sbuf("s_ph", [128, 2, 512], BF16)
    pl = cx.sbuf("s_pl", [128, 2, 512], BF16)
    ts_ = cx.sbuf("s_ts", [128, 2, 512], F32)
    rrow = cx.sbuf("s_rrow", [1, 512], F32)
    rbt = cx.sbuf("s_rbt", [128, 512], F32)
    ones20 = cx.sbuf("s_ones20", [128, 1], F32)
    cx.g(nc.gpsimd.memset(ones20[:, :], 1048576.0))
    bmask = cx.sb["s_bmask"]

    SLD = cx.sem["SLD"]
    CC.wait(nc.sync)
    CC.wait(nc.scalar)
    for kvh in range(2):
        for half in range(2):
            for gp_ in range(4):
                LD.inc(nc.sync.dma_start(
                    out=_apd(kb, (kvh * 2 + half) * 32 * 128 + gp_ * 128,
                             [[2 * 2 * 32 * 128, 128], [4 * 128, 8], [1, 128]]),
                    in_=_apd(T["ag2_out"], (gp_ * 1024 + kvh * 256 + half * 128) * 1024,
                             [[1024, 128], [128, 8], [1, 128]])), 16)
                SLD.inc(nc.scalar.dma_start(
                    out=_apd(vb, (kvh * 2 + half) * 32 * 128 + gp_ * 128,
                             [[2 * 2 * 32 * 128, 128], [4 * 128, 8], [1, 128]]),
                    in_=_apd(T["ag2_out"], (gp_ * 1024 + 512 + (kvh * 2 + half) * 128) * 1024,
                             [[128, 128], [128 * 128, 8], [1, 128]])), 16)
    LD.wait(nc.tensor)
    SLD.wait(nc.tensor)

    qb_mm = [0, 0]
    bank_tanh = [0, 0]
    ld_qb = [0, 0]
    blocks = [(jj, hh) for jj in range(NSLOT) for hh in range(4)]

    def _qload(b):
        jj, hh = blocks[b]
        sl = b % 2
        MM.wait(nc.sync, qb_mm[sl])
        for half in range(2):
            LD.inc(nc.sync.dma_start(
                out=qblk[:, sl, half, :, :],
                in_=_apd(T["q_spill"], (half * 2048 + hh * 512) * TOK + jj * 128,
                         [[TOK, 128], [128 * TOK, 4], [1, 128]]), ), 16)
        ld_qb[sl] = LD.n

    blk_i = 0
    for j in range(NSLOT):
        for hg in range(4):
            kvh = hg // 2
            sl_q = blk_i % 2
            if blk_i == 0:
                _qload(0)
            if blk_i + 1 < len(blocks):
                _qload(blk_i + 1)
            LD.wait(nc.tensor, ld_qb[sl_q])
            nk = 4 * j + 4
            sc_tanh = {}
            ve_split = {}
            mm_pv = {}

            def scores0(k, sl_q=sl_q, kvh=kvh, j=j):
                ps_s = P[k % 2]
                SC.wait(nc.tensor, sc_tanh[k - 2] if k >= 2 else bank_tanh[k % 2])
                MM.inc(nc.tensor.matmul(ps_s[:, :], kb[:, kvh, 0, k, :],
                                        qblk[:, sl_q, 0, :, :], start=True, stop=False))
                MM.inc(nc.tensor.matmul(ps_s[:, :], kb[:, kvh, 0, k, :],
                                        qblk[:, sl_q, 1, :, :], start=False, stop=False))
                mm_sc = MM.inc(nc.tensor.matmul(ps_s[:, :], kb[:, kvh, 1, k, :],
                                                qblk[:, sl_q, 0, :, :], start=False, stop=True))
                MM.wait(nc.scalar, mm_sc)
                if k >= 2:
                    VE.wait(nc.scalar, ve_split[k - 2])
                    MM.wait(nc.scalar, mm_pv[k - 2])
                    MM.wait(nc.vector, mm_pv[k - 2])
                sc_tanh[k] = cx.s(nc.scalar.activation(
                    ts_[:, k % 2, :], ps_s[:, :], AF.Tanh, bias=0.0, scale=SCALE0 / CAP))
                bank_tanh[k % 2] = sc_tanh[k]
                cx.s(nc.scalar.activation(
                    pf[:, k % 2, :], ts_[:, k % 2, :], AF.Exp, bias=0.0, scale=CAP))
                SC.wait(nc.vector)
                b = k - 4 * j
                if b >= 0:
                    cx.v(nc.vector.tensor_tensor(
                        pf[:, k % 2, :], pf[:, k % 2, :],
                        _apd(bmask, b * 128, [[512, 128], [0, 4], [1, 128]]), ALU.mult))
                cx.v(nc.vector.tensor_copy(ph[:, k % 2, :], pf[:, k % 2, :]))
                ve_split[k] = cx.v(nc.vector.tensor_tensor(
                    pl[:, k % 2, :], pf[:, k % 2, :], ph[:, k % 2, :], ALU.subtract))

            def pvrow(k, kvh=kvh, nk=nk):
                VE.wait(nc.tensor, ve_split[k])
                MM.inc(nc.tensor.matmul(P[3][0:1, :], ones20[:, :], pf[:, k % 2, :],
                                        start=(k == 0), stop=(k == nk - 1)))
                MM.inc(nc.tensor.matmul(P[2][:, :], vb[:, kvh, 0, k, :], ph[:, k % 2, :],
                                        start=(k == 0), stop=False))
                MM.inc(nc.tensor.matmul(P[2][:, :], vb[:, kvh, 0, k, :], pl[:, k % 2, :],
                                        start=False, stop=False))
                mm_pv[k] = MM.inc(nc.tensor.matmul(P[2][:, :], vb[:, kvh, 1, k, :],
                                                   ph[:, k % 2, :],
                                                   start=False, stop=(k == nk - 1)))

            scores0(0)
            for k in range(1, nk):
                scores0(k)
                pvrow(k - 1)
            pvrow(nk - 1)
            qb_mm[sl_q] = MM.n
            blk_i += 1
            MM.wait(nc.vector)
            cx.v(nc.vector.reciprocal(rrow[0:1, :], P[3][0:1, :]))
            cx.v(nc.vector.tensor_scalar(rrow[0:1, :], rrow[0:1, :], 1048576.0, None, ALU.mult))
            VE.wait(nc.tensor)
            MM.inc(nc.tensor.matmul(P[4][:, :], cx.sb["s_onesr"][:, :], rrow[0:1, :],
                                    start=True, stop=True))
            MM.wait(nc.vector)
            cx.v(nc.vector.tensor_copy(rbt[:, :], P[4][:, :]))
            cx.v(nc.vector.scalar_tensor_tensor(
                _apd(attn_sb, (hg * 4) * TOK + j * 128, [[NCH * TOK, 128], [TOK, 4], [1, 128]]),
                P[2][:, :], 1.0, rbt[:, :], ALU.mult, ALU.mult))
            VE.wait(nc.tensor)
    if "d_attn0" in dumps:
        VE.wait(nc.sync)
        LD.inc(nc.sync.dma_start(
            out=_apd(dump_t["d_attn0"], 0, [[TOK, 128], [128 * TOK, NCH], [1, TOK]]),
            in_=_apd(attn_sb, 0, [[NCH * TOK, 128], [TOK, NCH], [1, TOK]])), 16)
        LD.wait(nc.sync)
    cx.pop_scope()


def _emit_wo(cx, T, attn_sb, w, bias, layer):
    nc = cx.nc
    cx.push_scope()
    LD, GLD, MM, VE, SC, GP = (cx.sem[k] for k in
                               ("LD", "GLD", "MM", "VE", "SC", "GP"))
    P = [cx.ps[f"p{i}"] for i in range(6)]
    abh = cx.sbuf("s_abh", [128, NCH, TOK], BF16)
    abl = cx.sbuf("s_abl", [128, NCH, TOK], BF16)
    whi = cx.sbuf("s_owhi", [128, 2, 16, 128], BF16)
    wlo = cx.sbuf("s_owlo", [128, 2, 16, 128], BF16)
    w32 = cx.sbuf("s_ow32", [128, 2, 16, 256], F32)
    fout = cx.sbuf("s_ofout", [128, 2, 512], F32)
    if bias is not None:
        bo = cx.sbuf("s_bo", [128, NCH], F32)
        LD.inc(nc.sync.dma_start(
            out=bo[:, :], in_=_apd(T[bias], 0, [[1, 128], [128, NCH]]),
            allow_slow_non_contiguous=True), 16)
        LD.wait(nc.vector)

    for ch in range(NCH):
        cx.v(nc.vector.tensor_copy(abh[:, ch, :], attn_sb[:, ch, :]))
        cx.v(nc.vector.tensor_tensor(abl[:, ch, :], attn_sb[:, ch, :], abh[:, ch, :],
                                     ALU.subtract))
    for oc in range(NCH):
        buf = oc % 2
        _load_w_hilo(cx, T, w, oc * 128, 128, whi, wlo, w32, buf, key=w)
        GP.wait(nc.tensor)
        for th in range(2):
            ps = P[(oc * 2 + th) % 2]
            VE.wait(nc.tensor)
            for ic in range(NCH):
                MM.inc(nc.tensor.matmul(ps[:, :], whi[:, buf, ic, :],
                                        _apd(abh, ic * TOK + th * 512, [[NCH * TOK, 128], [1, 512]]),
                                        start=(ic == 0), stop=False))
                MM.inc(nc.tensor.matmul(ps[:, :], whi[:, buf, ic, :],
                                        _apd(abl, ic * TOK + th * 512, [[NCH * TOK, 128], [1, 512]]),
                                        start=False, stop=False))
            for ic in range(NCH):
                MM.inc(nc.tensor.matmul(ps[:, :], wlo[:, buf, ic, :],
                                        _apd(abh, ic * TOK + th * 512, [[NCH * TOK, 128], [1, 512]]),
                                        start=False, stop=(ic == NCH - 1)))
            MM.wait(nc.vector)
            GLD.wait(nc.vector)
            if bias is not None:
                cx.v(nc.vector.tensor_scalar(fout[:, th, :], ps[:, :],
                                             bo[:, oc:oc + 1], None, ALU.add))
            else:
                cx.v(nc.vector.tensor_copy(fout[:, th, :], ps[:, :]))
            VE.wait(nc.gpsimd)
            GLD.inc(nc.gpsimd.dma_start(
                out=_apd(T["x_res"], oc * 128 * TOK + th * 512, [[TOK, 128], [1, 512]]),
                in_=fout[:, th, :], accum_op=ALU.add), 16)
        _wmark(cx, w, buf)
    cx.pop_scope()


def _emit_proj1(cx, T, xnh, xnl, dumps, dump_t):
    nc = cx.nc
    cx.push_scope()
    LD, GLD, MM, VE, SC, GP, CC = (cx.sem[k] for k in
                                   ("LD", "GLD", "MM", "VE", "SC", "GP", "CC"))
    P = [cx.ps[f"p{i}"] for i in range(6)]
    bias4 = cx.sbuf("s_bias4", [128, 4, NCH], F32)
    dec_b = cx.sbuf("s_decb", [128, NCH], F32)
    decn_b = cx.sbuf("s_decnb", [128, NCH], F32)
    lnsc = cx.sbuf("s_lnsc", [128, 1], F32)
    pos_b = cx.sbuf("s_posb", [128, TOK], F32)
    bv_b = cx.sbuf("s_bvb", [128, D], F32)
    posr = cx.sbuf("s_posr", [1, TOK], F32)
    decr = cx.sbuf("s_decr", [1, NH], F32)
    bvr = cx.sbuf("s_bvr", [1, D], F32)
    sg = cx.sbuf("s_sg", [128, NH, 8], F32)

    for i, nm in enumerate(("b_q1", "b_k1", "b_v1", "b_g1")):
        LD.inc(nc.sync.dma_start(
            out=bias4[:, i, :], in_=_apd(T[nm], 0, [[1, 128], [128, NCH]]),
            allow_slow_non_contiguous=True), 16)
    LD.inc(nc.sync.dma_start(out=posr[0:1, :], in_=T["pos2048"][:, :]), 16)
    LD.inc(nc.sync.dma_start(out=decr[0:1, :], in_=T["decay"][:, :]), 16)
    LD.inc(nc.sync.dma_start(out=bvr[0:1, :], in_=T["b_v1"][:, :]), 16)
    cx.g(nc.gpsimd.memset(lnsc[:, :], LNLOG_SCALE0))
    LD.wait(nc.tensor)
    MM.inc(nc.tensor.matmul(P[0][:, 0:NH], cx.sb["s_onesr"][:, :], decr[0:1, :],
                            start=True, stop=True))
    MM.wait(nc.vector)
    cx.v(nc.vector.tensor_copy(dec_b[:, :], P[0][:, 0:NH]))
    cx.v(nc.vector.tensor_scalar(decn_b[:, :], dec_b[:, :], -1.0, None, ALU.mult))
    _bcast_row(cx, lambda th: posr[0:1, th * 512:(th + 1) * 512], pos_b)
    for th in range(4):
        VE.wait(nc.tensor)
        MM.inc(nc.tensor.matmul(P[0][:, :], cx.sb["s_onesr"][:, :],
                                bvr[0:1, th * 512:(th + 1) * 512], start=True, stop=True))
        MM.wait(nc.vector)
        cx.v(nc.vector.tensor_copy(bv_b[:, th * 512:(th + 1) * 512], P[0][:, :]))
    VE.wait(nc.tensor)

    cx.push_scope()
    whi = cx.sbuf("s_pwhi1", [128, 2, 16, 128], BF16)
    wlo = cx.sbuf("s_pwlo1", [128, 2, 16, 128], BF16)
    w32 = cx.sbuf("s_pw321", [128, 2, 16, 256], F32)
    bt = cx.sbuf("s_bt", [128, TOK], F32)
    qrot = cx.sbuf("s_qrot1", [128, 512], F32)
    qh = cx.sbuf("s_qh1", [128, 512], BF16)
    ql = cx.sbuf("s_ql1", [128, 512], BF16)

    def xnh_sl(ic, th, n=512):
        return _apd(xnh, ic * TOK + th * 512, [[NCH * TOK, 128], [1, n]])

    def xnl_sl(ic, th, n=512):
        return _apd(xnl, ic * TOK + th * 512, [[NCH * TOK, 128], [1, n]])

    def proj_mms(ps, buf, th):
        for ic in range(NCH):
            MM.inc(nc.tensor.matmul(ps[:, :], whi[:, buf, ic, :], xnh_sl(ic, th),
                                    start=(ic == 0), stop=False))
            MM.inc(nc.tensor.matmul(ps[:, :], whi[:, buf, ic, :], xnl_sl(ic, th),
                                    start=False, stop=False))
        for ic in range(NCH):
            MM.inc(nc.tensor.matmul(ps[:, :], wlo[:, buf, ic, :], xnh_sl(ic, th),
                                    start=False, stop=(ic == NCH - 1)))

    for oc in range(NCH):
        buf = oc % 2
        _load_w_hilo(cx, T, "w_q1", oc * 128, 128, whi, wlo, w32, buf, key="w_q1")
        GP.wait(nc.tensor)
        VE.wait(nc.scalar)
        MM.wait(nc.scalar)
        cx.s(nc.scalar.activation(bt[:, :], pos_b[:, :], AF.Exp,
                                  bias=lnsc[:, 0:1], scale=dec_b[:, oc:oc + 1]))
        SC.wait(nc.vector)
        for th in range(2):
            ps = P[(oc * 2 + th) % 2]
            VE.wait(nc.tensor)
            proj_mms(ps, buf, th)
            MM.wait(nc.vector)
            LD.wait(nc.vector)
            cx.v(nc.vector.scalar_tensor_tensor(
                qrot[:, :], ps[:, :], bias4[:, 0, oc:oc + 1],
                bt[:, th * 512:(th + 1) * 512], ALU.add, ALU.mult))
            cx.v(nc.vector.tensor_copy(qh[:, :], qrot[:, :]))
            cx.v(nc.vector.tensor_tensor(ql[:, :], qrot[:, :], qh[:, :], ALU.subtract))
            VE.wait(nc.sync)
            for half, t in ((0, qh), (1, ql)):
                LD.inc(nc.sync.dma_start(
                    out=_apd(T["q_spill"], (half * 2048 + oc * 128) * TOK + th * 512,
                             [[TOK, 128], [1, 512]]),
                    in_=t[:, :]), 16)
        _wmark(cx, "w_q1", buf)

    for oc in range(NCH):
        buf = oc % 2
        _load_w_hilo(cx, T, "w_k1", oc * 128, 128, whi, wlo, w32, buf, key="w_k1")
        GP.wait(nc.tensor)
        VE.wait(nc.scalar)
        MM.wait(nc.scalar)
        cx.s(nc.scalar.activation(bt[:, :], pos_b[:, :], AF.Exp,
                                  bias=0.0, scale=decn_b[:, oc:oc + 1]))
        SC.wait(nc.vector)
        for th in range(2):
            ps = P[(oc * 2 + th) % 2]
            VE.wait(nc.tensor)
            proj_mms(ps, buf, th)
            MM.wait(nc.vector)
            LD.wait(nc.vector)
            cx.v(nc.vector.scalar_tensor_tensor(
                qrot[:, :], ps[:, :], bias4[:, 1, oc:oc + 1],
                bt[:, th * 512:(th + 1) * 512], ALU.add, ALU.mult))
            for sl in range(4):
                cx.v(nc.vector.tensor_reduce(sg[:, oc, th * 4 + sl:th * 4 + sl + 1],
                                             qrot[:, sl * 128:(sl + 1) * 128],
                                             AX.X, ALU.add))
            cx.v(nc.vector.tensor_copy(qh[:, :], qrot[:, :]))
            cx.v(nc.vector.tensor_tensor(ql[:, :], qrot[:, :], qh[:, :], ALU.subtract))
            VE.wait(nc.sync)
            for half, t in ((0, qh), (1, ql)):
                LD.inc(nc.sync.dma_start(
                    out=_apd(T["ag3_in"], (oc * 256 + half * 128) * 1024 + th * 512,
                             [[1024, 128], [1, 512]]),
                    in_=t[:, :]), 16)
        _wmark(cx, "w_k1", buf)
    VE.wait(nc.gpsimd)
    GLD.inc(nc.gpsimd.dma_start(
        out=T["ag3s_in"][:, :],
        in_=_apd(sg, 0, [[NH * 8, 128], [1, NH * 8]])), 16)
    GLD.wait(nc.gpsimd)
    CC.inc(nc.gpsimd.collective_compute(
        "AllGather", ALU.bypass, replica_groups=[[0, 1, 2, 3], [4, 5, 6, 7]],
        ins=[T["ag3s_in"].ap().opt()], outs=[T["ag3s_out"].ap().opt()]))

    for oc in range(NCH):
        buf = oc % 2
        _load_w_hilo(cx, T, "w_g1", oc * 128, 128, whi, wlo, w32, buf, key="w_g1")
        GP.wait(nc.tensor)
        for th in range(2):
            ps = P[(oc * 2 + th) % 2]
            SC.wait(nc.tensor)
            proj_mms(ps, buf, th)
            MM.wait(nc.scalar)
            LD.wait(nc.scalar)
            cx.s(nc.scalar.activation(qrot[:, :], ps[:, :], AF.Silu,
                                      bias=bias4[:, 3, oc:oc + 1], scale=1.0))
            SC.wait(nc.sync)
            LD.inc(nc.sync.dma_start(
                out=_apd(T["gate1_h"], oc * 128 * TOK + th * 512, [[TOK, 128], [1, 512]]),
                in_=qrot[:, :]), 16)
        _wmark(cx, "w_g1", buf)

    cx.pop_scope()
    cx.push_scope()
    wvhi = cx.sbuf("s_wvhi1", [128, 16, 512], BF16)
    wvlo = cx.sbuf("s_wvlo1", [128, 16, 512], BF16)
    wv32 = cx.sbuf("s_wv321", [128, 16, 512], F32)
    qrot = cx.sbuf("s_qrot1", [128, 512], F32)
    qh = cx.sbuf("s_qh1", [128, 512], BF16)
    ql = cx.sbuf("s_ql1", [128, 512], BF16)
    for os_ in range(4):
        _load_w_hilo2(cx, T, "w_v1", os_ * 512, 512, wvhi, wvlo, wv32)
        GP.wait(nc.tensor)
        for tt in range(NSLOT):
            ps = P[4 + tt % 2]
            VE.wait(nc.tensor)
            for ic in range(NCH):
                xh = _apd(xnh, ic * TOK + tt * 128, [[NCH * TOK, 128], [1, 128]])
                xl = _apd(xnl, ic * TOK + tt * 128, [[NCH * TOK, 128], [1, 128]])
                MM.inc(nc.tensor.matmul(ps[:, :], xh, wvhi[:, ic, :],
                                        start=(ic == 0), stop=False))
                MM.inc(nc.tensor.matmul(ps[:, :], xl, wvhi[:, ic, :],
                                        start=False, stop=False))
                MM.inc(nc.tensor.matmul(ps[:, :], xh, wvlo[:, ic, :],
                                        start=False, stop=(ic == NCH - 1)))
            MM.wait(nc.vector)
            LD.wait(nc.vector)
            cx.v(nc.vector.scalar_tensor_tensor(
                qrot[:, :], ps[:, :], 1.0, bv_b[:, os_ * 512:(os_ + 1) * 512],
                ALU.mult, ALU.add))
            cx.v(nc.vector.tensor_copy(qh[:, :], qrot[:, :]))
            cx.v(nc.vector.tensor_tensor(ql[:, :], qrot[:, :], qh[:, :], ALU.subtract))
            VE.wait(nc.sync)
            for hh in range(4):
                h = os_ * 4 + hh
                for half, t in ((0, qh), (1, ql)):
                    LD.inc(nc.sync.dma_start(
                        out=_apd(T["ag3_in"],
                                 (4096 + (h * 2 + half) * 128) * 1024 + tt * 128 * 128,
                                 [[128, 128], [1, 128]]),
                        in_=t[:, hh * 128:(hh + 1) * 128]), 16)

    cx.pop_scope()
    LD.wait(nc.gpsimd)
    CC.inc(nc.gpsimd.collective_compute(
        "AllGather", ALU.bypass, replica_groups=[[0, 1, 2, 3], [4, 5, 6, 7]],
        ins=[T["ag3_in"].ap().opt()], outs=[T["ag3_out"].ap().opt()]))
    if "d_q1" in dumps:
        LD.wait(nc.sync)
        LD.inc(nc.sync.dma_start(out=dump_t["d_q1"][:, :], in_=T["q_spill"][:, :]), 16)
        LD.wait(nc.sync)
    if "d_ag3" in dumps:
        CC.wait(nc.sync)
        LD.inc(nc.sync.dma_start(out=dump_t["d_ag3"][:, :], in_=T["ag3_out"][0:8192, :]), 16)
        LD.wait(nc.sync)
    if "d_sig" in dumps:
        CC.wait(nc.sync)
        LD.inc(nc.sync.dma_start(out=dump_t["d_sig"][:, :], in_=T["ag3s_in"][:, :]), 16)
        LD.wait(nc.sync)
    if "d_gate1" in dumps:
        LD.wait(nc.sync)
        LD.inc(nc.sync.dma_start(out=dump_t["d_gate1"][:, :], in_=T["gate1_h"][:, :]), 16)
        LD.wait(nc.sync)
    cx.pop_scope()


def _emit_ffn1_gu(cx, T, xnh, xnl, invrow, dumps, dump_t):
    nc = cx.nc
    cx.push_scope()
    LD, GLD, MM, VE, SC, GP, CC = (cx.sem[k] for k in
                                   ("LD", "GLD", "MM", "VE", "SC", "GP", "CC"))
    P = [cx.ps[f"p{i}"] for i in range(6)]
    wt = cx.sbuf("s_guw", [128, 2, 2, 16, 128], BF16)
    gsb = cx.sbuf("s_gsb", [128, 512], F32)
    hsb = cx.sbuf("s_hsb", [128, 2, 512], F32)
    sqs = cx.sbuf("s_sqs", [128, 512], F32)

    CC.wait(nc.sync)
    CC.wait(nc.tensor)
    CC.wait(nc.gpsimd)

    def gu_tile_ap(og, ic):
        s = og // 16
        off = (s * 12288 + 6144) * 1024 + (ic * 128) * 2048 + (og % 16) * 128
        return _apd(T["ag1_out"], off, [[2048, 128], [1, 128]])

    gu1_mm = [0, 0]
    for og in range(64):
        buf = og % 2
        MM.wait(nc.sync, gu1_mm[buf])
        for gi, ogx in enumerate((og, og + 64)):
            LD.inc(nc.sync.dma_start(
                out=wt[:, buf, gi, :, :],
                in_=_apd(T["ag1_out"], (ogx // 16 * 12288 + 6144) * 1024 + (ogx % 16) * 128,
                         [[2048, 128], [128 * 2048, 16], [1, 128]])), 16)
        LD.wait(nc.tensor)
        for th in range(2):
            psA, psB = P[th], P[2 + th]
            VE.wait(nc.tensor)
            SC.wait(nc.tensor)
            for ic in range(NCH):
                MM.inc(nc.tensor.matmul(psA[:, :], wt[:, buf, 0, ic, :],
                                        _apd(xnh, ic * TOK + th * 512, [[NCH * TOK, 128], [1, 512]]),
                                        start=(ic == 0), stop=False))
                MM.inc(nc.tensor.matmul(psA[:, :], wt[:, buf, 0, ic, :],
                                        _apd(xnl, ic * TOK + th * 512, [[NCH * TOK, 128], [1, 512]]),
                                        start=False, stop=(ic == NCH - 1)))
            for ic in range(NCH):
                MM.inc(nc.tensor.matmul(psB[:, :], wt[:, buf, 1, ic, :],
                                        _apd(xnh, ic * TOK + th * 512, [[NCH * TOK, 128], [1, 512]]),
                                        start=(ic == 0), stop=False))
                MM.inc(nc.tensor.matmul(psB[:, :], wt[:, buf, 1, ic, :],
                                        _apd(xnl, ic * TOK + th * 512, [[NCH * TOK, 128], [1, 512]]),
                                        start=False, stop=(ic == NCH - 1)))
            MM.wait(nc.scalar)
            VE.wait(nc.scalar)
            cx.s(nc.scalar.activation(gsb[:, :], psA[:, :], AF.Silu, bias=0.0, scale=1.0))
            SC.wait(nc.vector)
            GLD.wait(nc.vector)
            cx.v(nc.vector.tensor_tensor(hsb[:, th, :], gsb[:, :], psB[:, :], ALU.mult))
            VE.wait(nc.scalar)
            MM.wait(nc.scalar)
            cx.s(nc.scalar.activation(sqs[:, :], hsb[:, th, :], AF.Square, bias=0.0, scale=1.0))
            SC.wait(nc.tensor)
            MM.inc(nc.tensor.matmul(P[4][0:1, :] if th == 0 else P[5][0:1, :],
                                    cx.sb["s_ones"][:, :], sqs[:, :],
                                    start=(og == 0), stop=(og == 63)))
            VE.wait(nc.gpsimd)
            GLD.inc(nc.gpsimd.dma_start(
                out=_apd(T["h_hbm"], og * 128 * TOK + th * 512, [[TOK, 128], [1, 512]]),
                in_=hsb[:, th, :]), 16)
        gu1_mm[buf] = MM.n
    MM.wait(nc.vector)
    eps32 = float(np.finfo(np.float32).eps)
    for th, pp in ((0, P[4]), (1, P[5])):
        sl = slice(th * 512, (th + 1) * 512)
        cx.v(nc.vector.tensor_scalar(invrow[0:1, sl], pp[0:1, :], 1.0 / FFH, eps32,
                                     ALU.mult, ALU.add))
    VE.wait(nc.scalar)
    cx.s(nc.scalar.activation(invrow[0:1, :], invrow[0:1, :], AF.Sqrt, bias=0.0, scale=1.0))
    SC.wait(nc.vector)
    cx.v(nc.vector.reciprocal(invrow[0:1, :], invrow[0:1, :]))
    cx.pop_scope()


def _emit_ffn1_proj(cx, T, invrow, dumps, dump_t):
    nc = cx.nc
    cx.push_scope()
    LD, GLD, MM, VE, SC, GP, CC = (cx.sem[k] for k in
                                   ("LD", "GLD", "MM", "VE", "SC", "GP", "CC"))
    P = [cx.ps[f"p{i}"] for i in range(6)]
    hsl = cx.sbuf("s_hsl", [128, 64, TFF], F32)
    hnh = cx.sbuf("s_hnh", [128, 64, TFF], BF16)
    hnl = cx.sbuf("s_hnl", [128, 64, TFF], BF16)
    wp = cx.sbuf("s_wp", [128, 64, 128], BF16)
    invb = cx.sbuf("s_invb1", [128, TFF], F32)
    fout = cx.sbuf("s_fout1", [128, 2, TFF], F32)
    tsc = cx.sbuf("s_tsc1", [128, TFF], F32)

    SLDq = cx.sem["SLD"]
    fo_ve = [0, 0]
    for slab in range(NSLAB):
        VE.wait(nc.sync)
        VE.wait(nc.scalar)
        MM.wait(nc.sync)
        MM.wait(nc.scalar)
        ldq = []
        for q in range(4):
            sem_, eng_ = (LD, nc.sync) if q % 2 == 0 else (SLDq, nc.scalar)
            sem_.inc(eng_.dma_start(
                out=hsl[:, q * 16:(q + 1) * 16, :],
                in_=_apd(T["h_hbm"], slab * TFF + q * 16 * 128 * TOK,
                         [[TOK, 128], [128 * TOK, 16], [1, TFF]])), 16)
            ldq.append((sem_, sem_.n))
        VE.wait(nc.tensor)
        MM.inc(nc.tensor.matmul(P[4][:, 0:TFF], cx.sb["s_onesr"][:, :],
                                invrow[0:1, slab * TFF:(slab + 1) * TFF],
                                start=True, stop=True))
        MM.wait(nc.vector)
        cx.v(nc.vector.tensor_copy(invb[:, :], P[4][:, 0:TFF]))
        if "d_h" in dumps and slab == 0:
            LD.wait(nc.sync)
            LD.inc(nc.sync.dma_start(
                out=_apd(dump_t["d_h"], 0, [[TFF, 128], [128 * TFF, 64], [1, TFF]]),
                in_=_apd(T["h_hbm"], 0, [[TOK, 128], [128 * TOK, 64], [1, TFF]])), 16)
            LD.wait(nc.sync)
        MM.wait(nc.vector)
        ve_q = []
        for q in range(4):
            ldq[q][0].wait(nc.vector, ldq[q][1])
            for oc in range(q * 16, (q + 1) * 16):
                cx.v(nc.vector.tensor_tensor(tsc[:, :], hsl[:, oc, :], invb[:, :], ALU.mult))
                cx.v(nc.vector.tensor_copy(hnh[:, oc, :], tsc[:, :]))
                cx.v(nc.vector.tensor_tensor(hnl[:, oc, :], tsc[:, :], hnh[:, oc, :],
                                             ALU.subtract))
            ve_q.append(cx.sem["VE"].n)
        if "d_hn1" in dumps and slab == 0:
            VE.wait(nc.sync)
            LD.inc(nc.sync.dma_start(
                out=_apd(dump_t["d_hn1"], 0, [[TFF, 128], [128 * TFF, 64], [1, TFF]]),
                in_=_apd(hnh, 0, [[64 * TFF, 128], [TFF, 64], [1, TFF]])), 16)
            LD.wait(nc.sync)
        SLD = cx.sem["SLD"]
        for od in range(NCH):
            MM.wait(nc.sync)
            MM.wait(nc.scalar)
            LD.inc(nc.sync.dma_start(
                out=wp[:, 0:32, :],
                in_=_apd(T["ag1_out"],
                         (od // 2 * 12288 + 10240) * 1024 + (od % 2) * 128,
                         [[256, 128], [128 * 256, 32], [1, 128]])), 16)
            ldh0 = LD.n
            SLD.inc(nc.scalar.dma_start(
                out=wp[:, 32:64, :],
                in_=_apd(T["ag1_out"],
                         (od // 2 * 12288 + 10240) * 1024 + 32 * 128 * 256 + (od % 2) * 128,
                         [[256, 128], [128 * 256, 32], [1, 128]])), 16)
            ldh1 = SLD.n
            ps = P[od % 2]
            VE.wait(nc.tensor, fo_ve[od % 2])
            LD.wait(nc.tensor, ldh0)
            for ic in range(32):
                if od == 0 and ic % 16 == 0:
                    VE.wait(nc.tensor, ve_q[ic // 16])
                MM.inc(nc.tensor.matmul(ps[:, 0:TFF], wp[:, ic, :], hnh[:, ic, :],
                                        start=(ic == 0), stop=False))
                MM.inc(nc.tensor.matmul(ps[:, 0:TFF], wp[:, ic, :], hnl[:, ic, :],
                                        start=False, stop=False))
            SLD.wait(nc.tensor, ldh1)
            for ic in range(32, 64):
                if od == 0 and ic % 16 == 0:
                    VE.wait(nc.tensor, ve_q[ic // 16])
                MM.inc(nc.tensor.matmul(ps[:, 0:TFF], wp[:, ic, :], hnh[:, ic, :],
                                        start=False, stop=False))
                MM.inc(nc.tensor.matmul(ps[:, 0:TFF], wp[:, ic, :], hnl[:, ic, :],
                                        start=False, stop=(ic == 63)))
            MM.wait(nc.vector)
            GLD.wait(nc.vector)
            fo_ve[od % 2] = cx.v(nc.vector.tensor_copy(fout[:, od % 2, :], ps[:, 0:TFF]))
            VE.wait(nc.gpsimd)
            GLD.inc(nc.gpsimd.dma_start(
                out=_apd(T["x_res"], od * 128 * TOK + slab * TFF, [[TOK, 128], [1, TFF]]),
                in_=fout[:, od % 2, :], accum_op=ALU.add), 16)
    cx.pop_scope()


def _emit_attn1(cx, T, attn_sb, dumps, dump_t):
    nc = cx.nc
    cx.push_scope()
    LD, GLD, MM, VE, SC, GP, CC = (cx.sem[k] for k in
                                   ("LD", "GLD", "MM", "VE", "SC", "GP", "CC"))
    P = [cx.ps[f"p{i}"] for i in range(6)]
    kb1 = cx.sbuf("s_kb1", [128, 2, 2, 32, 128], BF16)
    vb1 = cx.sbuf("s_vb1", [128, 2, 2, 32, 128], BF16)
    qb1 = cx.sbuf("s_qb1", [128, 2, 2, TOK], BF16)
    sg1 = cx.sbuf("s_sg1", [128, 32], F32)
    sgz = cx.sbuf("s_sgz", [128, 32], F32)
    spref = cx.sbuf("s_spref", [128, 32], F32)
    sadj = cx.sbuf("s_sadj", [128, 8], F32)
    sh = cx.sbuf("s_sh", [128, 8], BF16)
    sl_ = cx.sbuf("s_sl", [128, 8], BF16)
    corr = cx.sbuf("s_corr", [128, 1], F32)
    cm1 = cx.sbuf("s_cm1", [128, 1], F32)
    dec_b = cx.sbuf("s_decb1", [128, NH], F32)
    decr = cx.sbuf("s_decr1", [1, NH], F32)
    pf = cx.sbuf("s_pf1", [128, 2, 512], F32)
    ph = cx.sbuf("s_ph1", [128, 2, 512], BF16)
    pl = cx.sbuf("s_pl1", [128, 2, 512], BF16)
    rrow = cx.sbuf("s_rrow1", [1, 512], F32)
    rabs = cx.sbuf("s_rabs1", [1, 512], F32)
    rbt = cx.sbuf("s_rbt1", [128, 512], F32)
    onesb = cx.sbuf("s_onesb", [128, 1], BF16)
    sgall = cx.sbuf("s_sgall", [128, 4, 128], F32)
    bmask = cx.sb["s_bmask"]
    SLD = cx.sem["SLD"]
    cx.g(nc.gpsimd.memset(onesb[:, :], 1.0))

    cx.g(nc.gpsimd.memset(sgz[:, :], 0.0))
    GP.wait(nc.tensor)
    LD.inc(nc.sync.dma_start(out=decr[0:1, :], in_=T["decay"][:, :]), 16)
    LD.wait(nc.tensor)
    MM.inc(nc.tensor.matmul(P[0][:, 0:NH], cx.sb["s_onesr"][:, :], decr[0:1, :],
                            start=True, stop=True))
    MM.wait(nc.vector)
    cx.v(nc.vector.tensor_copy(dec_b[:, :], P[0][:, 0:NH]))
    CC.wait(nc.sync)
    CC.wait(nc.tensor)
    CC.wait(nc.scalar)
    CC.wait(nc.vector)
    for gp_ in range(4):
        LD.inc(nc.sync.dma_start(
            out=sgall[:, gp_, :],
            in_=_apd(T["ag3s_out"], gp_ * 128 * 128, [[128, 128], [1, 128]])), 16)
    ld_sg = LD.n

    hb_mm = [0, 0]
    bank_ev = [0, 0]
    ld_k = [0, 0]
    ld_v = [0, 0]
    for h in range(NH):
        hs = h % 2
        MM.wait(nc.sync, hb_mm[hs])
        MM.wait(nc.scalar, hb_mm[hs])
        for half in range(2):
            for gp_ in range(4):
                LD.inc(nc.sync.dma_start(
                    out=_apd(kb1, hs * 8192 + half * 4096 + gp_ * 128,
                             [[16384, 128], [4 * 128, 8], [1, 128]]),
                    in_=_apd(T["ag3_out"],
                             (gp_ * 8192 + h * 256 + half * 128) * 1024,
                             [[1024, 128], [128, 8], [1, 128]])), 16)
                SLD.inc(nc.scalar.dma_start(
                    out=_apd(vb1, hs * 8192 + half * 4096 + gp_ * 128,
                             [[16384, 128], [4 * 128, 8], [1, 128]]),
                    in_=_apd(T["ag3_out"],
                             (gp_ * 8192 + 4096 + (h * 2 + half) * 128) * 1024,
                             [[128, 128], [128 * 128, 8], [1, 128]])), 16)
        LD.inc(nc.sync.dma_start(
            out=qb1[:, hs, :, :],
            in_=_apd(T["q_spill"], h * 128 * TOK, [[TOK, 128], [2048 * TOK, 2], [1, TOK]])), 16)
        ld_k[hs] = LD.n
        ld_v[hs] = SLD.n
        if h == 0:
            LD.wait(nc.vector, ld_sg)
        for gp_ in range(4):
            cx.v(nc.vector.tensor_copy(
                _apd(sg1, gp_, [[32, 128], [4, 8]]),
                _apd(sgall, gp_ * 128 + h * 8, [[512, 128], [1, 8]])))
        cx.v(nc.vector.tensor_tensor_scan(spref[:, :], sg1[:, :], sgz[:, :], 0.0,
                                          ALU.add, ALU.add))
        VE.wait(nc.scalar)
        MM.wait(nc.scalar)
        cx.s(nc.scalar.activation(corr[:, 0:1], dec_b[:, h:h + 1], AF.Exp,
                                  bias=0.0, scale=2048.0))
        SC.wait(nc.vector)
        cx.v(nc.vector.tensor_scalar(cm1[:, 0:1], corr[:, 0:1], -1.0, None, ALU.add))
        for j in range(NSLOT):
            if j == 0:
                cx.g(nc.gpsimd.memset(sadj[:, 0:1], 0.0))
                GP.wait(nc.vector)
            elif j < 4:
                cx.v(nc.vector.tensor_copy(sadj[:, j:j + 1], spref[:, 4 * j - 1:4 * j]))
            else:
                cx.v(nc.vector.scalar_tensor_tensor(
                    sadj[:, j:j + 1], spref[:, 15:16], cm1[:, 0:1],
                    spref[:, 4 * j - 1:4 * j], ALU.mult, ALU.add))
        cx.v(nc.vector.tensor_copy(sh[:, :], sadj[:, :]))
        cx.v(nc.vector.tensor_tensor(sl_[:, :], sadj[:, :], sh[:, :], ALU.subtract))

        for bl in range(2):
            nk = 16 * bl + 16
            VE.wait(nc.tensor)
            if bl == 0:
                LD.wait(nc.tensor, ld_k[hs])
                SLD.wait(nc.tensor, ld_v[hs])
            first = True
            for j in range(4 * bl, 4 * bl + 4):
                if j == 0:
                    continue
                co = (j - 4 * bl) * 128
                MM.inc(nc.tensor.matmul(P[3][0:1, co:co + 128], sh[:, j:j + 1],
                                        qb1[:, hs, 0, j * 128:(j + 1) * 128],
                                        start=first, stop=False))
                MM.inc(nc.tensor.matmul(P[3][0:1, co:co + 128], sl_[:, j:j + 1],
                                        qb1[:, hs, 0, j * 128:(j + 1) * 128],
                                        start=False, stop=False))
                MM.inc(nc.tensor.matmul(P[3][0:1, co:co + 128], sh[:, j:j + 1],
                                        qb1[:, hs, 1, j * 128:(j + 1) * 128],
                                        start=False, stop=False))
                first = False
            sc_evac = {}
            mm_pv = {}

            def scores1(k, hs=hs, bl=bl):
                j0 = max(4 * bl, -(-(k - 3) // 4) if k > 3 else 4 * bl)
                co = (j0 - 4 * bl) * 128
                n = (4 * bl + 4 - j0) * 128
                ps_s = P[k % 2]
                VE.wait(nc.tensor, sc_evac[k - 2] if k >= 2 else bank_ev[k % 2])
                MM.inc(nc.tensor.matmul(ps_s[:, co:co + n], kb1[:, hs, 0, k, :],
                                        qb1[:, hs, 0, j0 * 128:(4 * bl + 4) * 128],
                                        start=True, stop=False))
                MM.inc(nc.tensor.matmul(ps_s[:, co:co + n], kb1[:, hs, 0, k, :],
                                        qb1[:, hs, 1, j0 * 128:(4 * bl + 4) * 128],
                                        start=False, stop=False))
                mm_sc = MM.inc(nc.tensor.matmul(ps_s[:, co:co + n], kb1[:, hs, 1, k, :],
                                                qb1[:, hs, 0, j0 * 128:(4 * bl + 4) * 128],
                                                start=False, stop=True))
                MM.wait(nc.vector, mm_sc)
                if k >= 2:
                    MM.wait(nc.vector, mm_pv[k - 2])
                if k >= 16 * bl:
                    b = k % 4
                    cx.v(nc.vector.tensor_tensor(
                        ps_s[:, co:co + 128], ps_s[:, co:co + 128],
                        _apd(bmask, b * 128, [[512, 128], [1, 128]]), ALU.mult))
                crossing = (bl == 1 and k < 16)
                if crossing:
                    cx.v(nc.vector.tensor_scalar(pf[:, k % 2, co:co + n],
                                                 ps_s[:, co:co + n],
                                                 corr[:, 0:1], None, ALU.mult))
                else:
                    cx.v(nc.vector.tensor_copy(pf[:, k % 2, co:co + n], ps_s[:, co:co + n]))
                cx.v(nc.vector.tensor_copy(ph[:, k % 2, co:co + n], pf[:, k % 2, co:co + n]))
                sc_evac[k] = cx.v(nc.vector.tensor_tensor(
                    pl[:, k % 2, co:co + n], pf[:, k % 2, co:co + n],
                    ph[:, k % 2, co:co + n], ALU.subtract))
                bank_ev[k % 2] = sc_evac[k]

            def pvrow1(k, hs=hs, bl=bl, nk=nk):
                j0 = max(4 * bl, -(-(k - 3) // 4) if k > 3 else 4 * bl)
                co = (j0 - 4 * bl) * 128
                n = (4 * bl + 4 - j0) * 128
                VE.wait(nc.tensor, sc_evac[k])
                if k >= 16 * bl:
                    MM.inc(nc.tensor.matmul(P[3][0:1, co:co + 128], onesb[:, :],
                                            ph[:, k % 2, co:co + 128],
                                            start=False, stop=False))
                    MM.inc(nc.tensor.matmul(P[3][0:1, co:co + 128], onesb[:, :],
                                            pl[:, k % 2, co:co + 128],
                                            start=False,
                                            stop=(k == nk - 1)))
                MM.inc(nc.tensor.matmul(P[2][:, co:co + n], vb1[:, hs, 0, k, :],
                                        ph[:, k % 2, co:co + n],
                                        start=(k == 0), stop=False))
                MM.inc(nc.tensor.matmul(P[2][:, co:co + n], vb1[:, hs, 0, k, :],
                                        pl[:, k % 2, co:co + n],
                                        start=False, stop=False))
                mm_pv[k] = MM.inc(nc.tensor.matmul(P[2][:, co:co + n], vb1[:, hs, 1, k, :],
                                                   ph[:, k % 2, co:co + n],
                                                   start=False, stop=(k == nk - 1)))

            scores1(0)
            for k in range(1, nk):
                scores1(k)
                pvrow1(k - 1)
            pvrow1(nk - 1)
            MM.wait(nc.scalar)
            cx.s(nc.scalar.activation(rabs[0:1, :], P[3][0:1, :], AF.Abs,
                                      bias=0.0, scale=1.0))
            SC.wait(nc.vector)
            cx.v(nc.vector.tensor_scalar_max(rrow[0:1, :], rabs[0:1, :], 1.0))
            cx.v(nc.vector.reciprocal(rrow[0:1, :], rrow[0:1, :]))
            VE.wait(nc.tensor)
            MM.inc(nc.tensor.matmul(P[4][:, :], cx.sb["s_onesr"][:, :], rrow[0:1, :],
                                    start=True, stop=True))
            MM.wait(nc.vector)
            cx.v(nc.vector.tensor_copy(rbt[:, :], P[4][:, :]))
            cx.v(nc.vector.scalar_tensor_tensor(
                _apd(attn_sb, h * TOK + bl * 512, [[NCH * TOK, 128], [1, 512]]),
                P[2][:, :], 1.0, rbt[:, :], ALU.mult, ALU.mult))
            VE.wait(nc.tensor)
            hb_mm[hs] = MM.n
    if "d_attn1" in dumps:
        VE.wait(nc.sync)
        LD.inc(nc.sync.dma_start(
            out=_apd(dump_t["d_attn1"], 0, [[TOK, 128], [128 * TOK, NCH], [1, TOK]]),
            in_=_apd(attn_sb, 0, [[NCH * TOK, 128], [TOK, NCH], [1, TOK]])), 16)
        LD.wait(nc.sync)
    cx.pop_scope()


def _emit_asm1(cx, T, attn_sb, dumps, dump_t):
    nc = cx.nc
    cx.push_scope()
    cx.push_scope()
    LD, GLD, MM, VE, SC, GP = (cx.sem[k] for k in
                               ("LD", "GLD", "MM", "VE", "SC", "GP"))
    P = [cx.ps[f"p{i}"] for i in range(6)]
    gnw = cx.sbuf("s_gnw", [128, NCH], F32)
    gnb = cx.sbuf("s_gnb", [128, NCH], F32)
    sq = cx.sbuf("s_gsq", [128, 512], F32)
    gtc = cx.sbuf("s_gtc", [128, 512], F32)
    row = cx.sbuf("s_grow", [1, 4 * 512], F32)
    ivb = cx.sbuf("s_givb", [128, 512], F32)
    m2b = cx.sbuf("s_gm2b", [128, 512], F32)
    LD.inc(nc.sync.dma_start(out=gnw[:, :], in_=_apd(T["gn_w"], 0, [[1, 128], [128, NCH]]),
                             allow_slow_non_contiguous=True), 16)
    LD.inc(nc.sync.dma_start(out=gnb[:, :], in_=_apd(T["gn_b"], 0, [[1, 128], [128, NCH]]),
                             allow_slow_non_contiguous=True), 16)
    LD.wait(nc.vector)
    mu = row[0:1, 0:512]
    var = row[0:1, 512:1024]
    m2r = row[0:1, 1024:1536]

    for ch in range(NCH):
        for th in range(2):
            asl = attn_sb[:, ch, th * 512:(th + 1) * 512]
            VE.wait(nc.scalar)
            MM.wait(nc.scalar)
            cx.s(nc.scalar.activation(sq[:, :], asl, AF.Square, bias=0.0, scale=1.0))
            SC.wait(nc.tensor)
            VE.wait(nc.tensor)
            MM.inc(nc.tensor.matmul(P[0][0:1, :], cx.sb["s_ones"][:, :], asl,
                                    start=True, stop=True))
            MM.inc(nc.tensor.matmul(P[1][0:1, :], cx.sb["s_ones"][:, :], sq[:, :],
                                    start=True, stop=True))
            MM.wait(nc.vector)
            cx.v(nc.vector.tensor_scalar(mu[:, :], P[0][0:1, :], 1.0 / 128.0, None, ALU.mult))
            cx.v(nc.vector.tensor_scalar(var[:, :], P[1][0:1, :], 1.0 / 128.0, None, ALU.mult))
            cx.v(nc.vector.tensor_tensor(m2r[:, :], mu[:, :], mu[:, :], ALU.mult))
            cx.v(nc.vector.tensor_tensor(var[:, :], var[:, :], m2r[:, :], ALU.subtract))
            cx.v(nc.vector.tensor_scalar(var[:, :], var[:, :], 1e-5, None, ALU.add))
            VE.wait(nc.scalar)
            cx.s(nc.scalar.activation(var[:, :], var[:, :], AF.Sqrt, bias=0.0, scale=1.0))
            SC.wait(nc.vector)
            cx.v(nc.vector.reciprocal(var[:, :], var[:, :]))
            cx.v(nc.vector.tensor_tensor(m2r[:, :], mu[:, :], var[:, :], ALU.mult))
            VE.wait(nc.tensor)
            MM.inc(nc.tensor.matmul(P[2][:, :], cx.sb["s_onesr"][:, :], var[:, :],
                                    start=True, stop=True))
            MM.inc(nc.tensor.matmul(P[3][:, :], cx.sb["s_onesr"][:, :], m2r[:, :],
                                    start=True, stop=True))
            MM.wait(nc.vector)
            cx.v(nc.vector.tensor_copy(ivb[:, :], P[2][:, :]))
            cx.v(nc.vector.tensor_copy(m2b[:, :], P[3][:, :]))
            VE.wait(nc.sync)
            LD.inc(nc.sync.dma_start(
                out=gtc[:, :],
                in_=_apd(T["gate1_h"], ch * 128 * TOK + th * 512, [[TOK, 128], [1, 512]])), 16)
            cx.v(nc.vector.scalar_tensor_tensor(sq[:, :], asl, 1.0, ivb[:, :],
                                                ALU.mult, ALU.mult))
            cx.v(nc.vector.tensor_tensor(sq[:, :], sq[:, :], m2b[:, :], ALU.subtract))
            cx.v(nc.vector.tensor_scalar(sq[:, :], sq[:, :], gnw[:, ch:ch + 1],
                                         gnb[:, ch:ch + 1], ALU.mult, ALU.add))
            LD.wait(nc.vector)
            cx.v(nc.vector.tensor_tensor(asl, sq[:, :], gtc[:, :], ALU.mult))
    cx.pop_scope()
    _emit_wo(cx, T, attn_sb, w="w_o1", bias="b_o1", layer=1)
    cx.pop_scope()



def _host_inputs(inputs):
    x0 = np.ascontiguousarray(np.asarray(inputs["L0_attn_hidden_states"], np.float32))
    maps = []
    inv = 1.0 / 10000.0 ** (np.arange(0, HD, 2, dtype=np.float32) / HD)
    eye32 = np.eye(128, dtype=np.float32)
    eyebf = np.eye(128).astype(BF)
    tri = np.tril(np.ones((128, 128), np.float32))
    swap = np.zeros((128, 128), np.float32)
    swap[np.arange(128), np.arange(128) ^ 64] = 1.0
    rotm = np.zeros((7, 128, 128), np.float32)
    for j in range(7):
        s = 1 << (6 - j)
        rotm[j, (np.arange(128) + s) % 128, np.arange(128)] = 1.0
    for c in range(NCORE):
        b, g = c // 4, c % 4
        tiles = [g + 4 * j for j in range(NSLOT)]
        rows = np.concatenate([np.arange(t * 128, (t + 1) * 128) for t in tiles])
        pos = rows.astype(np.float32)
        f = np.outer(pos, inv)
        bm = np.zeros((4, 128, 128), np.float32)
        for bi in range(4):
            if bi < g:
                bm[bi] = 1.0
            elif bi == g:
                bm[bi] = tri.T
        m = {
            "x0": np.ascontiguousarray(x0[b, rows, :]),
            "w_qkv": np.asarray(inputs["L0_attn_Wqkv"], np.float32),
            "w_o0": np.asarray(inputs["L0_attn_Wo"], np.float32),
            "wg_sl": np.ascontiguousarray(np.asarray(inputs["L0_ffn_w_gate"], np.float32)[c * 1024:(c + 1) * 1024]),
            "wu_sl": np.ascontiguousarray(np.asarray(inputs["L0_ffn_w_up"], np.float32)[c * 1024:(c + 1) * 1024]),
            "wd_sl": np.ascontiguousarray(np.asarray(inputs["L0_ffn_w_down"], np.float32)[c * 256:(c + 1) * 256]),
            "w_q1": np.asarray(inputs["L1_attn_Wq"], np.float32),
            "w_k1": np.asarray(inputs["L1_attn_Wk"], np.float32),
            "w_v1": np.asarray(inputs["L1_attn_Wv"], np.float32),
            "w_g1": np.asarray(inputs["L1_attn_Wg"], np.float32),
            "w_o1": np.asarray(inputs["L1_attn_Wo"], np.float32),
            "b_q1": np.asarray(inputs["L1_attn_bq"], np.float32).reshape(1, D),
            "b_k1": np.asarray(inputs["L1_attn_bk"], np.float32).reshape(1, D),
            "b_v1": np.asarray(inputs["L1_attn_bv"], np.float32).reshape(1, D),
            "b_g1": np.asarray(inputs["L1_attn_bg"], np.float32).reshape(1, D),
            "b_o1": np.asarray(inputs["L1_attn_bo"], np.float32).reshape(1, D),
            "gn_w": np.asarray(inputs["L1_attn_gn_w"], np.float32).reshape(1, D),
            "gn_b": np.asarray(inputs["L1_attn_gn_b"], np.float32).reshape(1, D),
            "decay": np.asarray(inputs["L1_attn_decay"], np.float32).reshape(1, NH),
            "wgu_sl": np.ascontiguousarray(np.asarray(inputs["L1_ffn_w_gate_up"], np.float32)[c * 2048:(c + 1) * 2048]),
            "wp_sl": np.ascontiguousarray(np.asarray(inputs["L1_ffn_w_proj"], np.float32)[c * 256:(c + 1) * 256]),
            "pos2048": np.mod(pos, 2048.0).reshape(1, TOK).astype(np.float32),
            "cos_t": np.ascontiguousarray(np.tile(np.cos(f).T.astype(np.float32), (2, 1))),
            "sin_t": np.ascontiguousarray(np.concatenate(
                [-np.sin(f).T, np.sin(f).T]).astype(np.float32)),
            "swap128": swap,
            "rotm": rotm.reshape(7 * 128, 128),
            "bmask": bm.reshape(4 * 128, 128),
            "ident32": eye32,
            "identbf": eyebf,
        }
        maps.append(m)
    return maps


def kernel(**inputs):
    global _LAST_RESULTS
    nc = _build(dumps=_DUMPS, stage=_STAGE)
    in_maps = _host_inputs(inputs)
    res = run_bass_kernel_spmd(nc, in_maps, core_ids=list(range(NCORE)))
    _LAST_RESULTS = res
    out = np.zeros((B, S, D), np.float32)
    for c in range(NCORE):
        b, g = c // 4, c % 4
        tiles = [g + 4 * j for j in range(NSLOT)]
        rows = np.concatenate([np.arange(t * 128, (t + 1) * 128) for t in tiles])
        out[b, rows, :] = np.asarray(res.results[c]["out_tok"], np.float32)
    return out


# version 15 (final)
# speedup vs baseline: 1.0480x; 1.0110x over previous
import numpy as np
import ml_dtypes
from contextlib import ExitStack

import concourse.bass as bass
import concourse.mybir as mybir
from concourse.bass_utils import run_bass_kernel_spmd

F32 = mybir.dt.float32
BF16 = mybir.dt.bfloat16
AF = mybir.ActivationFunctionType
ALU = mybir.AluOpType
AX = mybir.AxisListType
BF = ml_dtypes.bfloat16

B, S, D = 2, 4096, 2048
NCORE = 8
TOK = 1024
NSLOT = 8
NCH = 16
NH, HD = 16, 128
QKV_O = 2560
FFH = 8192
GUH = 16384
MAGIC = 12582912.0
SCALE0 = 0.08838834764831845
CAP = 30.0
LNLOG_SCALE0 = float(np.log(np.float64(SCALE0)))
TFF = 256
NSLAB = TOK // TFF

_STAGE = 99
_DUMPS = ()
_LAST_RESULTS = None
_WPREP = True
_WPARTS = {'ag0', 'l0', 'l1', 'ag1'}
_FFN0 = True
_ATTN0 = True


def _apd(t, off, dims):
    return bass.AP(t, off, [list(d) for d in dims])


class Sem:
    def __init__(self, h):
        self.h = h
        self.n = 0

    def inc(self, inst, k=1):
        inst.then_inc(self.h, k)
        self.n += k
        return self.n

    def wait(self, eng, val=None):
        eng.wait_ge(self.h, self.n if val is None else val)


class Ctx:

    def __init__(self, nc, stack):
        self.nc = nc
        self.stack = stack
        en = stack.enter_context
        self.sem = {}
        for name in ("LD", "GLD", "MM", "VE", "SC", "GP", "CC", "SLD",
                     "ATT", "ATV", "ATS", "PRJ", "FFN", "TRP", "TEV"):
            self.sem[name] = Sem(en(nc.semaphore(name.lower())))
        self.sb = {}
        self.ps = {}

    def sbuf(self, name, shape, dt):
        scope = self.scopes[-1] if getattr(self, "scopes", None) else self.stack
        if not hasattr(self, "_used_names"):
            self._used_names = {}
        k = self._used_names.get(name, 0)
        self._used_names[name] = k + 1
        uname = name if k == 0 else f"{name}__{k}"
        self.sb[name] = scope.enter_context(self.nc.sbuf_tensor(uname, shape, dt))
        if getattr(self, "scopes", None) and len(self.scopes) > 1:
            self.scope_names[-1].append(name)
        return self.sb[name]

    def push_scope(self):
        if not hasattr(self, "scopes"):
            self.scopes = [self.stack]
            self.scope_names = [[]]
        st = ExitStack()
        st.__enter__()
        self.scopes.append(st)
        self.scope_names.append([])
        return st

    def pop_scope(self):
        self.fence()
        st = self.scopes.pop()
        for n in self.scope_names.pop():
            self.sb.pop(n, None)
        st.__exit__(None, None, None)

    def fence(self):
        for ename in ("sync", "gpsimd", "vector", "scalar", "tensor"):
            eng = getattr(self.nc, ename)
            for s in ("VE", "SC", "GP", "MM", "LD", "GLD", "SLD", "TRP", "TEV"):
                self.sem[s].wait(eng)

    def psum(self, name, shape, dt=F32):
        self.ps[name] = self.stack.enter_context(self.nc.psum_tensor(name, shape, dt))
        return self.ps[name]

    def v(self, inst):
        s = self.sem["VE"]
        n = s.inc(inst)
        self.nc.vector.wait_ge(s.h, n)
        return n

    def s(self, inst):
        s = self.sem["SC"]
        n = s.inc(inst)
        self.nc.scalar.wait_ge(s.h, n)
        return n

    def g(self, inst):
        s = self.sem["GP"]
        n = s.inc(inst)
        self.nc.gpsimd.wait_ge(s.h, n)
        return n


def _build(dumps=(), stage=99):
    nc = bass.Bass("TRN2", target_bir_lowering=False, debug=False)
    dumps = set(dumps)

    def di(name, shape, dt=F32):
        return nc.dram_tensor(name, shape, dt, kind="ExternalInput")

    x0 = di("x0", [TOK, D])
    w_qkv = di("w_qkv", [D, QKV_O])
    w_o0 = di("w_o0", [D, D])
    wg_sl = di("wg_sl", [FFH // 8, D])
    wu_sl = di("wu_sl", [FFH // 8, D])
    wd_sl = di("wd_sl", [D // 8, FFH])
    w_q1 = di("w_q1", [D, D])
    w_k1 = di("w_k1", [D, D])
    w_v1 = di("w_v1", [D, D])
    w_g1 = di("w_g1", [D, D])
    w_o1 = di("w_o1", [D, D])
    b_q1 = di("b_q1", [1, D])
    b_k1 = di("b_k1", [1, D])
    b_v1 = di("b_v1", [1, D])
    b_g1 = di("b_g1", [1, D])
    b_o1 = di("b_o1", [1, D])
    gn_w = di("gn_w", [1, D])
    gn_b = di("gn_b", [1, D])
    decay = di("decay", [1, NH])
    wgu_sl = di("wgu_sl", [GUH // 8, D])
    wp_sl = di("wp_sl", [D // 8, FFH])
    pos2048 = di("pos2048", [1, TOK])
    cos_t = di("cos_t", [128, TOK])
    sin_t = di("sin_t", [128, TOK])
    swap128 = di("swap128", [128, 128], F32)
    rotm = di("rotm", [7 * 128, 128], F32)
    bmask = di("bmask", [4 * 128, 128])
    ident32 = di("ident32", [128, 128], F32)
    identbf = di("identbf", [128, 128], BF16)

    out_tok = nc.dram_tensor("out_tok", [TOK, D], F32, kind="ExternalOutput")

    dump_t = {}
    for dn, shape, dt in [
        ("d_xn", [D, TOK], F32),
        ("d_xres", [D, TOK], F32),
        ("d_q0", [NH * 2 * 128, TOK], BF16),
        ("d_ag2", [4096, 1024], BF16),
        ("d_attn0", [D, TOK], F32),
        ("d_x1", [D, TOK], F32),
        ("d_r", [1, TOK], F32),
        ("d_xq", [D, TOK], BF16),
        ("d_wsc", [1, 4], F32),
        ("d_h", [FFH, TFF], F32),
        ("d_hq", [FFH, TFF], BF16),
        ("d_ag1", [2048, 1024], BF16),
        ("d_ag1d", [2048, 1024], BF16),
        ("d_xn1", [D, TOK], F32),
        ("d_q1", [NH * 2 * 128, TOK], BF16),
        ("d_ag3", [8192, 1024], BF16),
        ("d_sig", [128, 128], F32),
        ("d_attn1", [D, TOK], F32),
        ("d_gate1", [D, TOK], F32),
        ("d_hn1", [FFH, TFF], BF16),
    ]:
        if dn in dumps:
            dump_t[dn] = nc.dram_tensor(dn, shape, dt, kind="ExternalOutput")

    st_mu = nc.dram_tensor("st_mu", [1, TOK], F32)
    st_sq = nc.dram_tensor("st_sq", [1, TOK], F32)
    st_mx = nc.dram_tensor("st_mx", [1, TOK], F32)
    x_res = nc.dram_tensor("x_res", [D, TOK], F32)
    xn_hbm = nc.dram_tensor("xn_hbm", [D, TOK], F32)
    q_spill = nc.dram_tensor("q_spill", [2 * NH * 128, TOK], BF16)
    gate1_h = nc.dram_tensor("gate1_h", [D, TOK], F32)
    h_hbm = nc.dram_tensor("h_hbm", [FFH, TOK], F32)
    ag0_in = nc.dram_tensor("ag0_in", [1, 4], F32)
    ag0_out = nc.dram_tensor("ag0_out", [8, 4], F32)
    ag1_in = nc.dram_tensor("ag1_in", [12288, 1024], BF16)
    ag1_out = nc.dram_tensor("ag1_out", [98304, 1024], BF16)
    ag2_in = nc.dram_tensor("ag2_in", [1024, 1024], BF16)
    ag2_out = nc.dram_tensor("ag2_out", [4096, 1024], BF16)
    ag3_in = nc.dram_tensor("ag3_in", [8192, 1024], BF16)
    ag3_out = nc.dram_tensor("ag3_out", [32768, 1024], BF16)
    ag3s_in = nc.dram_tensor("ag3s_in", [128, 128], F32)
    ag3s_out = nc.dram_tensor("ag3s_out", [512, 128], F32)

    with ExitStack() as stack:
        cx = Ctx(nc, stack)
        en = stack.enter_context
        blk = en(nc.Block())
        _emit(cx, locals(), dumps, dump_t, stage)
    return nc



def _emit(cx, T, dumps, dump_t, stage):
    nc = cx.nc
    LD, GLD, MM, VE, SC, GP, CC = (cx.sem[k] for k in
                                   ("LD", "GLD", "MM", "VE", "SC", "GP", "CC"))

    ident32 = cx.sbuf("s_id32", [128, 128], F32)
    identbf = cx.sbuf("s_idbf", [128, 128], BF16)
    ones_col = cx.sbuf("s_ones", [128, 1], F32)
    ones_row = cx.sbuf("s_onesr", [1, 128], F32)
    bmask_sb = cx.sbuf("s_bmask", [128, 4, 128], F32)
    cos_sb = cx.sbuf("s_cos", [128, TOK], F32)
    sin_sb = cx.sbuf("s_sin", [128, TOK], F32)
    swap_sb = cx.sbuf("s_swap", [128, 128], F32)
    rotm_sb = cx.sbuf("s_rotm", [128, 7, 128], F32)

    P = [cx.psum(f"p{i}", [128, 512], F32) for i in range(6)]
    PTB = [cx.psum(f"ptb{i}", [128, 128], BF16) for i in range(2)]

    nc.sync.dma_start(out=ident32[:, :], in_=T["ident32"][:, :]).then_inc(LD.h, 16); LD.n += 16
    nc.sync.dma_start(out=identbf[:, :], in_=T["identbf"][:, :]).then_inc(LD.h, 16); LD.n += 16
    nc.sync.dma_start(out=bmask_sb[:, :, :],
                      in_=_apd(T["bmask"], 0, [[128, 128], [128 * 128, 4], [1, 128]])).then_inc(LD.h, 16); LD.n += 16
    nc.sync.dma_start(out=cos_sb[:, :], in_=T["cos_t"][:, :]).then_inc(LD.h, 16); LD.n += 16
    nc.sync.dma_start(out=sin_sb[:, :], in_=T["sin_t"][:, :]).then_inc(LD.h, 16); LD.n += 16
    nc.sync.dma_start(out=swap_sb[:, :], in_=T["swap128"][:, :]).then_inc(LD.h, 16); LD.n += 16
    nc.sync.dma_start(out=rotm_sb[:, :, :],
                      in_=_apd(T["rotm"], 0, [[128, 128], [128 * 128, 7], [1, 128]])).then_inc(LD.h, 16); LD.n += 16
    cx.g(nc.gpsimd.memset(ones_col[:, :], 1.0))
    cx.g(nc.gpsimd.memset(ones_row[:, :], 1.0))

    cx.push_scope()
    xin = cx.sbuf("s_xin", [128, D], F32)
    tstg = cx.sbuf("s_tstg", [128, NCH * 128], F32)
    xsc = cx.sbuf("s_xsc", [128, D], F32)
    stt3 = cx.sbuf("s_stt3", [128, 3], F32)

    def _stats_hook(tt, buf):
        LD.wait(nc.scalar)
        cx.sem["SLD"].wait(nc.scalar)
        VE.wait(nc.scalar)
        cx.s(nc.scalar.activation(xsc[:, :], buf[:, :], AF.Square, bias=0.0, scale=1.0))
        cx.sem["TRP"].wait(nc.vector)
        SC.wait(nc.vector)
        cx.v(nc.vector.tensor_reduce(stt3[:, 0:1], buf[:, :], AX.X, ALU.add))
        cx.v(nc.vector.tensor_reduce(stt3[:, 1:2], xsc[:, :], AX.X, ALU.add))
        cx.v(nc.vector.tensor_scalar(stt3[:, 0:1], stt3[:, 0:1], 1.0 / D, None, ALU.mult))
        cx.v(nc.vector.tensor_scalar(stt3[:, 1:2], stt3[:, 1:2], 1.0 / D, None, ALU.mult))
        cx.v(nc.vector.tensor_scalar(xsc[:, :], buf[:, :], stt3[:, 0:1], None, ALU.subtract))
        cx.v(nc.vector.tensor_reduce(stt3[:, 2:3], xsc[:, :], AX.X, ALU.max,
                                       apply_absolute_value=True))
        VE.wait(nc.sync)
        for k, dst in enumerate(("st_mu", "st_sq", "st_mx")):
            cx.sem["TRP"].inc(nc.sync.dma_start(
                out=_apd(T[dst], tt * 128, [[1, 128], [1, 1]]),
                in_=stt3[:, k:k + 1]), 16)

    _transpose_128xD(cx, src_load=lambda tt: nc.sync.dma_start(
        out=xin[:, 0:1024], in_=T["x0"][tt * 128:(tt + 1) * 128, 0:1024]),
        src_load2=lambda tt: nc.scalar.dma_start(
            out=xin[:, 1024:2048], in_=T["x0"][tt * 128:(tt + 1) * 128, 1024:2048]),
        xin=xin, tstg=tstg, ident=ident32,
        dst_ap=lambda tt: _apd(T["x_res"], tt * 128,
                               [[TOK, 128], [128 * TOK, NCH], [1, 128]]),
        tstg_ap=_apd(tstg, 0, [[NCH * 128, 128], [128, NCH], [1, 128]]),
        per_tile=_stats_hook)
    cx.pop_scope()

    if stage < 1:
        _finish_stub(cx, T)
        return

    cx.push_scope()
    xq = cx.sbuf("s_xq", [128, NCH, TOK], BF16)
    u1g = cx.sbuf("s_u1g", [128, TOK], F32)
    u1u = cx.sbuf("s_u1u", [128, TOK], F32)
    cx.sbuf("s_wsc", [1, 4], F32)
    cx.sbuf("s_wscb", [128, 4], F32)

    cx.push_scope()
    xnh = cx.sbuf("s_xnh", [128, NCH, TOK], BF16)
    xnl = cx.sbuf("s_xnl", [128, NCH, TOK], BF16)
    rb = cx.sbuf("s_rb", [128, TOK], F32)
    sb_s = cx.sbuf("s_qs", [1, TOK], F32)

    _emit_ln(cx, T, layer=0, xnh=xnh, xnl=xnl, rb=rb, sb_s=sb_s)
    if "d_xn" in dumps:
        _dump_xn(cx, T, dump_t["d_xn"], xnh, xnl)
    if "d_r" in dumps:
        VE.wait(nc.sync)
        nc.sync.dma_start(out=dump_t["d_r"][:, :], in_=sb_s[:, :]).then_inc(LD.h, 16); LD.n += 16
    if _WPREP:
        _emit_wprep(cx, T, dumps, dump_t)
    if stage >= 2:
        _emit_proj0(cx, T, xnh, xnl, dumps, dump_t)
    if stage >= 3 and _FFN0:
        _emit_ffn0_xq(cx, T, xnh, xnl, rb, sb_s, xq, u1g, u1u, dumps, dump_t)
    cx.pop_scope()

    if stage < 2:
        cx.pop_scope()
        _finish_stub(cx, T)
        return

    if stage >= 3 and _FFN0:
        _emit_ffn0_main(cx, T, xq, u1g, u1u, dumps, dump_t)
    cx.pop_scope()
    if stage < 4:
        _finish_stub(cx, T)
        return

    cx.push_scope()
    attn_sb = cx.sbuf("s_attn", [128, NCH, TOK], F32)
    _emit_attn0(cx, T, attn_sb, dumps, dump_t)
    if stage >= 5:
        _emit_wo(cx, T, attn_sb, w="w_o0", bias=None, layer=0)
    cx.pop_scope()
    if "d_x1" in dumps:
        _dump_xres(cx, T, dump_t["d_x1"])
    if stage < 6:
        _finish_stub(cx, T)
        return

    cx.push_scope()
    invrow = cx.sbuf("s_invrow", [1, TOK], F32)

    cx.push_scope()
    xnh = cx.sbuf("s_xnh", [128, NCH, TOK], BF16)
    xnl = cx.sbuf("s_xnl", [128, NCH, TOK], BF16)
    _emit_ln(cx, T, layer=1, xnh=xnh, xnl=xnl, rb=None, sb_s=None)
    if "d_xn1" in dumps:
        _dump_xn(cx, T, dump_t["d_xn1"], xnh, xnl)
    _emit_proj1(cx, T, xnh, xnl, dumps, dump_t)
    if stage >= 7:
        _emit_ffn1_gu(cx, T, xnh, xnl, invrow, dumps, dump_t)
    cx.pop_scope()
    if stage < 7:
        cx.pop_scope()
        _finish_stub(cx, T)
        return

    _emit_ffn1_proj(cx, T, invrow, dumps, dump_t)
    cx.pop_scope()
    if stage < 8:
        _finish_stub(cx, T)
        return

    cx.push_scope()
    attn_sb = cx.sbuf("s_attn", [128, NCH, TOK], F32)
    _emit_attn1(cx, T, attn_sb, dumps, dump_t)
    if stage >= 9:
        _emit_asm1(cx, T, attn_sb, dumps, dump_t)
    cx.pop_scope()

    _emit_output(cx, T)


def _tev_copy(cx, inst):
    s = cx.sem["TEV"]
    n = s.inc(inst)
    cx.nc.vector.wait_ge(s.h, n)
    return n


def _transpose_128xD(cx, src_load, xin, tstg, ident, dst_ap, tstg_ap=None,
                     per_tile=None, src_load2=None):
    nc = cx.nc
    LD, MM, TEV, TRP = cx.sem["LD"], cx.sem["MM"], cx.sem["TEV"], cx.sem["TRP"]
    SLD = cx.sem["SLD"]
    PT = [cx.ps["p4"], cx.ps["p5"]]
    for tt in range(NSLOT):
        LD.wait(nc.sync)
        if tt > 0:
            MM.wait(nc.sync)
            if src_load2 is not None:
                MM.wait(nc.scalar)
            if per_tile is not None:
                cx.sem["VE"].wait(nc.sync)
                cx.sem["SC"].wait(nc.sync)
                if src_load2 is not None:
                    cx.sem["VE"].wait(nc.scalar)
                    cx.sem["SC"].wait(nc.scalar)
        src_load(tt).then_inc(LD.h, 16); LD.n += 16
        if src_load2 is not None:
            src_load2(tt).then_inc(SLD.h, 16); SLD.n += 16
        if per_tile is not None:
            per_tile(tt, xin)
        LD.wait(nc.tensor)
        if tt > 0:
            TRP.wait(nc.tensor)
        for ch in range(NCH):
            if ch == 8 and src_load2 is not None:
                SLD.wait(nc.tensor)
            pt = PT[ch % 2]
            if ch >= 2:
                TEV.wait(nc.tensor, TEV.n - 1)
            MM.inc(nc.tensor.transpose(pt[:, 0:128], xin[:, ch * 128:(ch + 1) * 128], ident[:, :]))
            MM.wait(nc.vector)
            _tev_copy(cx, nc.vector.tensor_copy(tstg[:, ch * 128:(ch + 1) * 128], pt[:, 0:128]))
        TEV.wait(nc.sync)
        TRP.inc(nc.sync.dma_start(
            out=dst_ap(tt),
            in_=(tstg_ap if tstg_ap is not None else tstg[:, :])), 16)
    TRP.wait(nc.sync)


def _finish_stub(cx, T):
    _emit_output(cx, T)


def _emit_output(cx, T):
    nc = cx.nc
    cx.sem["GLD"].wait(nc.sync)
    cx.sem["GP"].wait(nc.sync)
    cx.sem["TRP"].wait(nc.sync)
    cx.push_scope()
    xf = cx.sbuf("s_xin", [128, D], F32)
    tstg = cx.sbuf("s_tstg", [128, NCH * 128], F32)
    _transpose_128xD(cx, src_load=lambda tt: nc.sync.dma_start(
        out=_apd(xf, 0, [[D, 128], [128, NCH], [1, 128]]),
        in_=_apd(T["x_res"], tt * 128, [[TOK, 128], [128 * TOK, NCH], [1, 128]])),
        xin=xf, tstg=tstg, ident=cx.sb["s_id32"],
        dst_ap=lambda tt: T["out_tok"][tt * 128:(tt + 1) * 128, :])
    cx.sem["LD"].wait(nc.sync)
    cx.pop_scope()


def _dump_xres(cx, T, dst):
    nc = cx.nc
    GLD, LD = cx.sem["GLD"], cx.sem["LD"]
    GLD.wait(nc.sync)
    LD.inc(nc.sync.dma_start(out=dst[:, :], in_=T["x_res"][:, :]), 16)
    LD.wait(nc.sync)


def _dump_xn(cx, T, dst, xnh, xnl):
    nc = cx.nc
    VE, LD = cx.sem["VE"], cx.sem["LD"]
    tmp = cx.sb.get("s_dmp") or cx.sbuf("s_dmp", [128, TOK], F32)
    for ch in range(NCH):
        LD.wait(nc.vector)
        cx.v(nc.vector.tensor_tensor(tmp[:, :], xnh[:, ch, :], xnl[:, ch, :], ALU.add))
        VE.wait(nc.sync)
        LD.inc(nc.sync.dma_start(out=dst[ch * 128:(ch + 1) * 128, :], in_=tmp[:, :]), 16)
        LD.wait(nc.sync)


def _emit_wprep(cx, T, dumps, dump_t):
    nc = cx.nc
    wsc = cx.sb["s_wsc"]
    wscb = cx.sb["s_wscb"]
    cx.push_scope()
    LD, GLD, MM, VE, SC, GP, CC = (cx.sem[k] for k in
                                   ("LD", "GLD", "MM", "VE", "SC", "GP", "CC"))
    identbf = cx.sb["s_idbf"]

    wsl = cx.sbuf("s_wsl", [128, 2, 2048], F32)
    wbs = cx.sbuf("s_wbs", [128, 2048], BF16)
    wq = cx.sbuf("s_wq", [128, 2048], BF16)
    wq32 = cx.sbuf("s_wq32", [128, 2048], F32)
    wqt = cx.sbuf("s_wqt", [128, 2, 128], BF16)
    scl = cx.sbuf("s_wscl", [128, 32], F32)
    psums = cx.sbuf("s_wps", [128, 1], F32)
    acc128 = cx.sbuf("s_acc128", [128, 4], F32)
    acc4 = cx.sbuf("s_acc4", [1, 4], F32)
    g8 = cx.sbuf("s_g8", [8, 4], F32)

    do = _WPARTS
    cx.g(nc.gpsimd.memset(acc128[:, :], 0.0))
    GP.wait(nc.vector)
    specs0 = [("wg_sl", FFH // 8, D, 0), ("wu_sl", FFH // 8, D, 1), ("wd_sl", D // 8, FFH, 2)]
    SLD = cx.sem["SLD"]
    ve_wm = [0, 0]
    bufi = [0]

    def _wsl_load(name, rt, seg):
        buf = bufi[0] % 2
        bufi[0] += 1
        sem_, eng_ = (LD, nc.sync) if buf == 0 else (SLD, nc.scalar)
        VE.wait(eng_, ve_wm[buf])
        sem_.inc(eng_.dma_start(
            out=wsl[:, buf, :],
            in_=T[name][rt * 128:(rt + 1) * 128, seg * 2048:(seg + 1) * 2048]), 16)
        sem_.wait(nc.vector)
        return buf

    for name, rows, cols, slot in specs0:
        for rt in range(rows // 128):
            for seg in range(cols // 2048):
                buf = _wsl_load(name, rt, seg)
                ve_wm[buf] = cx.v(nc.vector.tensor_reduce(psums[:, :], wsl[:, buf, :],
                                               AX.X, ALU.add,
                                               apply_absolute_value=True))
                cx.v(nc.vector.tensor_tensor(acc128[:, slot:slot + 1], acc128[:, slot:slot + 1],
                                               psums[:, :], ALU.add))
    VE.wait(nc.tensor)
    MM.inc(nc.tensor.matmul(cx.ps["p0"][0:1, 0:4], cx.sb["s_ones"][:, :], acc128[:, :],
                            start=True, stop=True))
    MM.wait(nc.vector)
    cx.v(nc.vector.tensor_copy(acc4[:, :], cx.ps["p0"][0:1, 0:4]))
    VE.wait(nc.gpsimd)
    if 'ag0' in do:
        GLD.inc(nc.gpsimd.dma_start(out=T["ag0_in"][:, :], in_=acc4[:, :]), 16)
        GLD.wait(nc.gpsimd)
        CC.inc(nc.gpsimd.collective_compute(
            "AllGather", ALU.bypass, replica_groups=[list(range(8))],
            ins=[T["ag0_in"].ap().opt()], outs=[T["ag0_out"].ap().opt()]))
        CC.wait(nc.sync)
        LD.inc(nc.sync.dma_start(out=g8[:, :], in_=T["ag0_out"][:, :]), 16)
        LD.wait(nc.tensor)
        MM.inc(nc.tensor.matmul(cx.ps["p0"][0:1, 0:4], cx.sb["s_ones"][0:8, :], g8[:, :],
                                start=True, stop=True))
    else:
        VE.wait(nc.tensor)
        MM.inc(nc.tensor.matmul(cx.ps["p0"][0:1, 0:4], cx.sb["s_ones"][:, :], acc128[:, :],
                                start=True, stop=True))
    MM.wait(nc.vector)
    n0 = float(FFH * D) / (1.0 if 'ag0' in do else 8.0)
    cx.v(nc.vector.tensor_scalar(wsc[0:1, 0:4], cx.ps["p0"][0:1, 0:4], 1.0 / n0, 1e-5,
                                   ALU.mult, ALU.add))
    wrec = cx.sbuf("s_wrec", [1, 4], F32)
    cx.v(nc.vector.reciprocal(wrec[0:1, 0:4], wsc[0:1, 0:4]))
    if "d_wsc" in dumps:
        VE.wait(nc.sync)
        LD.inc(nc.sync.dma_start(out=dump_t["d_wsc"][:, :], in_=wsc[:, :]), 16)

    wrec_b = cx.sbuf("s_wrecb", [128, 4], F32)
    VE.wait(nc.tensor)
    MM.inc(nc.tensor.matmul(cx.ps["p0"][:, 0:4], cx.sb["s_onesr"][:, :], wrec[0:1, 0:4],
                            start=True, stop=True))
    MM.wait(nc.vector)
    cx.v(nc.vector.tensor_copy(wrec_b[:, :], cx.ps["p0"][:, 0:4]))
    VE.wait(nc.tensor)
    MM.inc(nc.tensor.matmul(cx.ps["p0"][:, 0:4], cx.sb["s_onesr"][:, :], wsc[0:1, 0:4],
                            start=True, stop=True))
    MM.wait(nc.vector)
    cx.v(nc.vector.tensor_copy(wscb[:, :], cx.ps["p0"][:, 0:4]))

    def quant_l0(name, rows, cols, slot, row_base):
        for rt in range(rows // 128):
            for seg in range(cols // 2048):
                buf = _wsl_load(name, rt, seg)
                cx.sem["TRP"].wait(nc.vector)
                GP.wait(nc.vector)
                ve_wm[buf] = cx.v(nc.vector.tensor_scalar(wq32[:, :], wsl[:, buf, :],
                                               wrec_b[:, slot:slot + 1], None, ALU.mult))
                cx.v(nc.vector.tensor_scalar(wbs[:, :], wq32[:, :], 0.5, None, ALU.is_gt))
                cx.v(nc.vector.tensor_scalar(wq32[:, :], wq32[:, :], -0.5, None, ALU.is_lt))
                VE.wait(nc.gpsimd)
                MM.wait(nc.gpsimd)
                cx.g(nc.gpsimd.tensor_tensor(wq[:, :], wbs[:, :], wq32[:, :],
                                               ALU.subtract))
                GP.wait(nc.tensor)
                _wq_transpose(cx, T, wq, 2048, row_base, rt, seg, cols)

    if 'l0' in do:
        quant_l0("wg_sl", FFH // 8, D, 0, 0)
        quant_l0("wu_sl", FFH // 8, D, 1, 2048)
        quant_l0("wd_sl", D // 8, FFH, 2, 4096)

    def quant_l1(name, rows, cols, row_base):
        ntile = rows // 128
        nseg = cols // 2048
        for rt in range(ntile):
            for seg in range(nseg):
                buf = _wsl_load(name, rt, seg)
                cx.sem["TRP"].wait(nc.vector)
                GP.wait(nc.vector)
                ve_wm[buf] = cx.v(nc.vector.tensor_copy(wbs[:, :2048], wsl[:, buf, :]))
                cx.v(nc.vector.tensor_reduce(
                    scl[:, :], _apd(wbs, 0, [[2048, 128], [64, 32], [1, 64]]),
                    AX.X, ALU.add, apply_absolute_value=True))
                sclb = cx.sb.get("s_sclb") or cx.sbuf("s_sclb", [128, 32], BF16)
                cx.v(nc.vector.tensor_scalar(sclb[:, :], scl[:, :], 1.0 / 64.0, None, ALU.mult))
                cx.v(nc.vector.tensor_scalar_max(scl[:, :], sclb[:, :], 1e-8))
                recg = cx.sb.get("s_recg") or cx.sbuf("s_recg", [128, 32], F32)
                cx.v(nc.vector.reciprocal(recg[:, :], scl[:, :]))
                qv = cx.sb.get("s_qv") or cx.sbuf("s_qv", [128, 2048], BF16)
                cx.v(nc.vector.tensor_tensor(
                    _apd(qv, 0, [[2048, 128], [64, 32], [1, 64]]),
                    _apd(wbs, 0, [[2048, 128], [64, 32], [1, 64]]),
                    _apd(recg, 0, [[32, 128], [1, 32], [0, 64]]), ALU.mult))
                t1 = cx.sb.get("s_t1w") or cx.sbuf("s_t1w", [128, 2048], F32)
                cx.v(nc.vector.tensor_scalar(t1[:, :2048], qv[:, :2048], 0.5, None, ALU.is_gt))
                MM.wait(nc.vector)
                cx.v(nc.vector.tensor_scalar(wq[:, :2048], qv[:, :2048], -0.5, None, ALU.is_lt))
                VE.wait(nc.gpsimd)
                cx.g(nc.gpsimd.tensor_tensor(t1[:, :2048], t1[:, :2048], wq[:, :2048],
                                               ALU.subtract))
                GP.wait(nc.vector)
                cx.v(nc.vector.tensor_tensor(
                    _apd(wq, 0, [[2048, 128], [64, 32], [1, 64]]),
                    _apd(t1, 0, [[2048, 128], [64, 32], [1, 64]]),
                    _apd(scl, 0, [[32, 128], [1, 32], [0, 64]]), ALU.mult))
                _wq_transpose(cx, T, wq, 2048, row_base, rt, seg, cols)

    if 'l1' in do:
        quant_l1("wgu_sl", GUH // 8, D, 6144)
        quant_l1("wp_sl", D // 8, FFH, 10240)

    if 'ag1' in do:
        cx.sem["TRP"].wait(nc.gpsimd)
        CC.inc(nc.gpsimd.collective_compute(
            "AllGather", ALU.bypass, replica_groups=[list(range(8))],
            ins=[T["ag1_in"].ap().opt()], outs=[T["ag1_out"].ap().opt()]))
        if "d_ag1" in dumps:
            CC.wait(nc.sync)
            LD.inc(nc.sync.dma_start(out=dump_t["d_ag1"][:, :],
                                     in_=T["ag1_out"][0:2048, :]), 16)
            LD.wait(nc.sync)
        if "d_ag1d" in dumps:
            CC.wait(nc.sync)
            LD.inc(nc.sync.dma_start(out=dump_t["d_ag1d"][:, :],
                                     in_=T["ag1_out"][6144:8192, :]), 16)
            LD.wait(nc.sync)
    cx.pop_scope()


def _wq_transpose(cx, T, wq, cols, row_base, rt, seg, full_cols):
    nc = cx.nc
    MM, VE, TRP = cx.sem["MM"], cx.sem["VE"], cx.sem["TRP"]
    identbf = cx.sb["s_idbf"]
    wqt = cx.sb["s_wqt"]
    slice_rows = {2048: 1024, 8192: 256, 16384: 2048}
    srows = {0: 1024, 2048: 1024, 4096: 256, 6144: 2048, 10240: 256}[row_base]
    TEV = cx.sem["TEV"]
    cx.sem["VE"].wait(nc.tensor)
    cx.sem["GP"].wait(nc.tensor)
    if not hasattr(cx, "_wqt_trp"):
        cx._wqt_trp = [0, 0]
    for c in range(cols // 128):
        pt = cx.ps["ptb%d" % (c % 2)]
        slot = c % 2
        if TEV.n >= 1:
            TEV.wait(nc.tensor, TEV.n - 1)
        MM.inc(nc.tensor.transpose(pt[:, :], wq[:, c * 128:(c + 1) * 128], identbf[:, :]))
        MM.wait(nc.vector)
        TRP.wait(nc.vector, cx._wqt_trp[slot])
        _tev_copy(cx, nc.vector.tensor_copy(wqt[:, slot, :], pt[:, :]))
        sem_, eng_ = (TRP, nc.sync) if slot == 0 else (TRP, nc.scalar)
        TEV.wait(eng_)
        i0 = seg * 2048 + c * 128
        flat = row_base * 1024 + i0 * srows + rt * 128
        sem_.inc(eng_.dma_start(
            out=_apd(T["ag1_in"], flat, [[srows, 128], [1, 128]]),
            in_=wqt[:, slot, :]), 16)
        cx._wqt_trp[slot] = sem_.n


def _bcast_row(cx, row_ap_fn, dst, n=TOK):
    nc = cx.nc
    MM, VE = cx.sem["MM"], cx.sem["VE"]
    P0 = cx.ps["p0"]
    VE.wait(nc.tensor)
    for th in range(n // 512):
        MM.inc(nc.tensor.matmul(P0[:, :], cx.sb["s_onesr"][:, :],
                                row_ap_fn(th), start=True, stop=True))
        MM.wait(nc.vector)
        cx.v(nc.vector.tensor_copy(dst[:, th * 512:(th + 1) * 512], P0[:, :]))
        VE.wait(nc.tensor)


def _emit_ln(cx, T, layer, xnh, xnl, rb, sb_s):
    nc = cx.nc
    cx.push_scope()
    LD, GLD, MM, VE, SC, GP = (cx.sem[k] for k in ("LD", "GLD", "MM", "VE", "SC", "GP"))
    P0, P1, P2, P3 = (cx.ps[f"p{i}"] for i in range(4))
    xch = [cx.sbuf("s_lnx0", [128, TOK], F32),
           cx.sbuf("s_lnx1", [128, TOK], F32)]
    xsq = cx.sbuf("s_lnsq", [128, TOK], F32)
    stat = cx.sbuf("s_stat", [1, 3 * TOK], F32)
    lns2 = cx.sbuf("s_lns2", [1, TOK], F32)
    invb = cx.sbuf("s_invb", [128, TOK], F32)
    m2b = cx.sbuf("s_m2b", [128, TOK], F32)

    cx.sem["TRP"].wait(nc.sync)
    GLD.wait(nc.sync)
    GLD.wait(nc.tensor)

    mu = stat[0:1, 0:TOK]
    m2 = stat[0:1, 1 * TOK:2 * TOK]
    var = stat[0:1, 2 * TOK:3 * TOK]
    inv = var

    if layer == 0:
        LD.inc(nc.sync.dma_start(out=mu[:, :], in_=T["st_mu"][:, :]), 16)
        LD.inc(nc.sync.dma_start(out=var[:, :], in_=T["st_sq"][:, :]), 16)
        LD.inc(nc.sync.dma_start(out=lns2[0:1, :], in_=T["st_mx"][:, :]), 16)
        LD.wait(nc.vector)
        cx.v(nc.vector.tensor_tensor(m2[:, :], mu[:, :], mu[:, :], ALU.mult))
        cx.v(nc.vector.tensor_tensor(var[:, :], var[:, :], m2[:, :], ALU.subtract))
    else:
        for ch in range(NCH):
            buf = xch[ch % 2]
            if ch >= 2:
                MM.wait(nc.sync)
                SC.wait(nc.sync)
            LD.inc(nc.sync.dma_start(out=buf[:, :],
                                     in_=_apd(T["x_res"], ch * 128 * TOK,
                                              [[TOK, 128], [1, TOK]])), 16)
            LD.wait(nc.scalar)
            cx.s(nc.scalar.activation(xsq[:, :], buf[:, :], AF.Square, bias=0.0, scale=1.0))
            LD.wait(nc.tensor)
            SC.wait(nc.tensor)
            for th in range(2):
                MM.inc(nc.tensor.matmul(P0[0:1, :] if th == 0 else P1[0:1, :],
                                        cx.sb["s_ones"][:, :], buf[:, th * 512:(th + 1) * 512],
                                        start=(ch == 0), stop=(ch == NCH - 1)))
                MM.inc(nc.tensor.matmul(P2[0:1, :] if th == 0 else P3[0:1, :],
                                        cx.sb["s_ones"][:, :], xsq[:, th * 512:(th + 1) * 512],
                                        start=(ch == 0), stop=(ch == NCH - 1)))
        MM.wait(nc.vector)
        for th, (ps_s, ps_q) in enumerate([(P0, P2), (P1, P3)]):
            sl = slice(th * 512, (th + 1) * 512)
            cx.v(nc.vector.tensor_scalar(mu[:, sl], ps_s[0:1, :], 1.0 / D, None, ALU.mult))
            cx.v(nc.vector.tensor_scalar(var[:, sl], ps_q[0:1, :], 1.0 / D, None, ALU.mult))
        cx.v(nc.vector.tensor_tensor(m2[:, :], mu[:, :], mu[:, :], ALU.mult))
        cx.v(nc.vector.tensor_tensor(var[:, :], var[:, :], m2[:, :], ALU.subtract))

    cx.v(nc.vector.tensor_scalar(var[:, :], var[:, :], 1e-5, None, ALU.add))
    VE.wait(nc.scalar)
    cx.s(nc.scalar.activation(inv[:, :], var[:, :], AF.Sqrt, bias=0.0, scale=1.0))
    SC.wait(nc.vector)
    cx.v(nc.vector.reciprocal(inv[:, :], inv[:, :]))
    cx.v(nc.vector.tensor_tensor(m2[:, :], mu[:, :], inv[:, :], ALU.mult))
    _bcast_row(cx, lambda th: inv[:, th * 512:(th + 1) * 512], invb)
    _bcast_row(cx, lambda th: m2[:, th * 512:(th + 1) * 512], m2b)

    for ch in range(NCH):
        buf = xch[ch % 2]
        VE.wait(nc.sync)
        MM.wait(nc.sync)
        SC.wait(nc.sync)
        LD.inc(nc.sync.dma_start(out=buf[:, :],
                                 in_=_apd(T["x_res"], ch * 128 * TOK, [[TOK, 128], [1, TOK]])), 16)
        LD.wait(nc.vector)
        xn = xsq
        cx.v(nc.vector.scalar_tensor_tensor(xn[:, :], buf[:, :], 1.0, invb[:, :],
                                              ALU.mult, ALU.mult))
        cx.v(nc.vector.tensor_tensor(xn[:, :], xn[:, :], m2b[:, :], ALU.subtract))
        if layer == 0:
            VE.wait(nc.sync)
            cx.sem["TRP"].inc(nc.sync.dma_start(
                out=_apd(T["xn_hbm"], ch * 128 * TOK, [[TOK, 128], [1, TOK]]),
                in_=xn[:, :]), 16)
        cx.v(nc.vector.tensor_copy(xnh[:, ch, :], xn[:, :]))
        cx.v(nc.vector.tensor_tensor(xnl[:, ch, :], xn[:, :], xnh[:, ch, :], ALU.subtract))

    if layer == 0:
        amx = lns2[0:1, :]
        t = mu
        cx.v(nc.vector.tensor_tensor(t[:, :], amx[:, :], inv[:, :], ALU.mult))
        cx.v(nc.vector.tensor_scalar(sb_s[0:1, :], t[:, :], 1.0 / 127.0, 1e-5,
                                       ALU.mult, ALU.add))
        r0 = amx
        cx.v(nc.vector.reciprocal(r0[:, :], sb_s[0:1, :]))
        cx.v(nc.vector.tensor_tensor(t[:, :], sb_s[0:1, :], r0[:, :], ALU.mult))
        cx.v(nc.vector.tensor_scalar(t[:, :], t[:, :], -1.0, 2.0, ALU.mult, ALU.add))
        cx.v(nc.vector.tensor_tensor(r0[:, :], r0[:, :], t[:, :], ALU.mult))
        _bcast_row(cx, lambda th: r0[:, th * 512:(th + 1) * 512], rb)
    cx.pop_scope()


def _load_w_hilo(cx, T, name, col0, ncols, whi, wlo, w32, buf, key=None):
    nc = cx.nc
    LD, GLD, GP, MM = cx.sem["LD"], cx.sem["GLD"], cx.sem["GP"], cx.sem["MM"]
    if not hasattr(cx, "wmk"):
        cx.wmk = {}
        cx.wmk_gp = {}
    wm = cx.wmk.setdefault(key or name, [0, 0])
    wmg = cx.wmk_gp.setdefault(key or name, [0, 0])
    SLD = cx.sem["SLD"]
    wt = T[name]
    wcols = wt.shape[1]
    src = _apd(wt, col0, [[wcols, 128], [wcols * 128, 16], [1, ncols]])
    srcA = _apd(wt, col0, [[wcols, 128], [wcols * 128, 8], [1, ncols]])
    srcB = _apd(wt, col0 + 8 * wcols * 128, [[wcols, 128], [wcols * 128, 8], [1, ncols]])
    MM.wait(nc.gpsimd, wm[buf])
    MM.wait(nc.sync, wm[buf])
    MM.wait(nc.scalar, wm[buf])
    GLD.inc(nc.gpsimd.dma_start(out=whi[:, buf, :, :], in_=src), 16)
    GP.wait(nc.sync, wmg[buf])
    GP.wait(nc.scalar, wmg[buf])
    LD.inc(nc.sync.dma_start(out=w32[:, buf, 0:8, 0:ncols], in_=srcA), 16)
    SLD.inc(nc.scalar.dma_start(out=w32[:, buf, 8:16, 0:ncols], in_=srcB), 16)
    GLD.wait(nc.gpsimd)
    LD.wait(nc.gpsimd)
    SLD.wait(nc.gpsimd)
    wmg[buf] = cx.g(nc.gpsimd.tensor_tensor(wlo[:, buf, :, :],
                                            w32[:, buf, :, 0:ncols],
                                            whi[:, buf, :, :], ALU.subtract))


def _wmark(cx, key, buf):
    cx.wmk.setdefault(key, [0, 0])[buf] = cx.sem["MM"].n



def _load_w_hilo2(cx, T, name, col0, ncols, whi, wlo, w32):
    nc = cx.nc
    LD, GLD, GP, MM = cx.sem["LD"], cx.sem["GLD"], cx.sem["GP"], cx.sem["MM"]
    SLD = cx.sem["SLD"]
    wt = T[name]
    wcols = wt.shape[1]
    src = _apd(wt, col0, [[wcols, 128], [wcols * 128, 16], [1, ncols]])
    srcA = _apd(wt, col0, [[wcols, 128], [wcols * 128, 8], [1, ncols]])
    srcB = _apd(wt, col0 + 8 * wcols * 128, [[wcols, 128], [wcols * 128, 8], [1, ncols]])
    MM.wait(nc.gpsimd)
    MM.wait(nc.sync)
    MM.wait(nc.scalar)
    GLD.inc(nc.gpsimd.dma_start(out=whi[:, :, :], in_=src), 16)
    GP.wait(nc.sync)
    GP.wait(nc.scalar)
    LD.inc(nc.sync.dma_start(out=w32[:, 0:8, 0:ncols], in_=srcA), 16)
    SLD.inc(nc.scalar.dma_start(out=w32[:, 8:16, 0:ncols], in_=srcB), 16)
    GLD.wait(nc.gpsimd)
    LD.wait(nc.gpsimd)
    SLD.wait(nc.gpsimd)
    cx.g(nc.gpsimd.tensor_tensor(wlo[:, :, :],
                                 w32[:, :, 0:ncols],
                                 whi[:, :, :], ALU.subtract))


def _rotary_evac(cx, ps, psB, th, qrot, q_sb):
    nc = cx.nc
    MM, VE = cx.sem["MM"], cx.sem["VE"]
    cos = cx.sb["s_cos"]
    sin = cx.sb["s_sin"]
    ta = cx.sb["s_rota"]
    sl = slice(th * 512, (th + 1) * 512)
    MM.wait(nc.vector)
    cx.v(nc.vector.tensor_copy(q_sb[:, :], ps[:, :]))
    VE.wait(nc.tensor)
    MM.inc(nc.tensor.matmul(psB[:, :], cx.sb["s_swap"][:, :], q_sb[:, :],
                            start=True, stop=True))
    cx.v(nc.vector.tensor_tensor(qrot[:, :], q_sb[:, :], cos[:, sl], ALU.mult))
    MM.wait(nc.vector)
    cx.v(nc.vector.scalar_tensor_tensor(ta[:, :], psB[:, :], 1.0, sin[:, sl],
                                        ALU.mult, ALU.mult))
    cx.v(nc.vector.tensor_tensor(qrot[:, :], qrot[:, :], ta[:, :], ALU.add))


def _emit_proj0(cx, T, xnh, xnl, dumps, dump_t):
    nc = cx.nc
    cx.push_scope()
    LD, GLD, MM, VE, SC, GP, CC = (cx.sem[k] for k in
                                   ("LD", "GLD", "MM", "VE", "SC", "GP", "CC"))
    P = [cx.ps[f"p{i}"] for i in range(6)]
    cx.push_scope()
    whi = cx.sbuf("s_pwhi", [128, 2, 16, 128], BF16)
    wlo = cx.sbuf("s_pwlo", [128, 2, 16, 128], BF16)
    w32 = cx.sbuf("s_pw32", [128, 2, 16, 256], F32)
    qrot = cx.sb.get("s_qrot") or cx.sbuf("s_qrot", [128, 512], F32)
    qh = cx.sb.get("s_qh") or cx.sbuf("s_qh", [128, 512], BF16)
    ql = cx.sb.get("s_ql") or cx.sbuf("s_ql", [128, 512], BF16)
    cx.sbuf("s_rota", [128, 512], F32) if "s_rota" not in cx.sb else None
    cx.sbuf("s_qsb", [128, 512], F32) if "s_qsb" not in cx.sb else None

    def xnh_sl(ic, th):
        return _apd(xnh, ic * TOK + th * 512, [[NCH * TOK, 128], [1, 512]])

    def xnl_sl(ic, th):
        return _apd(xnl, ic * TOK + th * 512, [[NCH * TOK, 128], [1, 512]])

    VE.wait(nc.tensor)
    for oc in range(18):
        buf = oc % 2
        _load_w_hilo(cx, T, "w_qkv", oc * 128, 128, whi, wlo, w32, buf, key="qkv")
        GP.wait(nc.tensor)
        for th in range(2):
            ps = P[(oc * 2 + th) % 2]
            psB = P[2 + (oc * 2 + th) % 2]
            VE.wait(nc.tensor)
            for ic in range(NCH):
                MM.inc(nc.tensor.matmul(ps[:, :], whi[:, buf, ic, :], xnh_sl(ic, th),
                                        start=(ic == 0), stop=False))
                MM.inc(nc.tensor.matmul(ps[:, :], whi[:, buf, ic, :], xnl_sl(ic, th),
                                        start=False, stop=False))
            for ic in range(NCH):
                MM.inc(nc.tensor.matmul(ps[:, :], wlo[:, buf, ic, :], xnh_sl(ic, th),
                                        start=False, stop=(ic == NCH - 1)))
            MM.wait(nc.vector)
            _rotary_evac(cx, ps, psB, th, qrot, cx.sb["s_qsb"])
            LD.wait(nc.vector)
            cx.v(nc.vector.tensor_copy(qh[:, :], qrot[:, :]))
            cx.v(nc.vector.tensor_tensor(ql[:, :], qrot[:, :], qh[:, :], ALU.subtract))
            VE.wait(nc.sync)
            if oc < 16:
                for half, t in ((0, qh), (1, ql)):
                    LD.inc(nc.sync.dma_start(
                        out=_apd(T["q_spill"], (half * 2048 + oc * 128) * TOK + th * 512,
                                 [[TOK, 128], [1, 512]]),
                        in_=t[:, :]), 16)
            else:
                kvh = oc - 16
                for half, t in ((0, qh), (1, ql)):
                    LD.inc(nc.sync.dma_start(
                        out=_apd(T["ag2_in"], (kvh * 256 + half * 128) * 1024 + th * 512,
                                 [[1024, 128], [1, 512]]),
                        in_=t[:, :]), 16)
        _wmark(cx, "qkv", buf)

    cx.pop_scope()
    wvhi = cx.sb.get("s_wvhi") or cx.sbuf("s_wvhi", [128, 16, 256], BF16)
    wvlo = cx.sb.get("s_wvlo") or cx.sbuf("s_wvlo", [128, 16, 256], BF16)
    wv32 = cx.sb.get("s_wv32") or cx.sbuf("s_wv32", [128, 16, 256], F32)
    _load_w_hilo2(cx, T, "w_qkv", 2304, 256, wvhi, wvlo, wv32)
    GP.wait(nc.tensor)
    for tt in range(NSLOT):
        ps = P[4 + tt % 2]
        VE.wait(nc.tensor)
        for ic in range(NCH):
            xh = _apd(xnh, ic * TOK + tt * 128, [[NCH * TOK, 128], [1, 128]])
            xl = _apd(xnl, ic * TOK + tt * 128, [[NCH * TOK, 128], [1, 128]])
            MM.inc(nc.tensor.matmul(ps[:, 0:256], xh, wvhi[:, ic, :],
                                    start=(ic == 0), stop=False))
            MM.inc(nc.tensor.matmul(ps[:, 0:256], xl, wvhi[:, ic, :],
                                    start=False, stop=False))
            MM.inc(nc.tensor.matmul(ps[:, 0:256], xh, wvlo[:, ic, :],
                                    start=False, stop=(ic == NCH - 1)))
        MM.wait(nc.vector)
        LD.wait(nc.vector)
        cx.v(nc.vector.tensor_copy(qh[:, 0:256], ps[:, 0:256]))
        cx.v(nc.vector.tensor_tensor(ql[:, 0:256], ps[:, 0:256], qh[:, 0:256], ALU.subtract))
        VE.wait(nc.sync)
        for kvh in range(2):
            for half, t in ((0, qh), (1, ql)):
                LD.inc(nc.sync.dma_start(
                    out=_apd(T["ag2_in"], (512 + (kvh * 2 + half) * 128) * 1024 + tt * 128 * 128,
                             [[128, 128], [1, 128]]),
                    in_=t[:, kvh * 128:(kvh + 1) * 128]), 16)

    LD.wait(nc.gpsimd)
    CC.inc(nc.gpsimd.collective_compute(
        "AllGather", ALU.bypass,
        replica_groups=[[0, 1, 2, 3], [4, 5, 6, 7]],
        ins=[T["ag2_in"].ap().opt()], outs=[T["ag2_out"].ap().opt()]))
    if "d_q0" in dumps:
        LD.wait(nc.sync)
        LD.inc(nc.sync.dma_start(out=dump_t["d_q0"][:, :], in_=T["q_spill"][:, :]), 16)
        LD.wait(nc.sync)
    if "d_ag2" in dumps:
        CC.wait(nc.sync)
        LD.inc(nc.sync.dma_start(out=dump_t["d_ag2"][:, :], in_=T["ag2_out"][:, :]), 16)
        LD.wait(nc.sync)
    cx.pop_scope()


def _part_max_tree(cx, hmx, n, ps):
    nc = cx.nc
    MM, VE = cx.sem["MM"], cx.sem["VE"]
    rotm = cx.sb["s_rotm"]
    for j in range(7):
        VE.wait(nc.tensor)
        MM.inc(nc.tensor.matmul(ps[:, 0:n], rotm[:, j, :], hmx[:, 0:n],
                                start=True, stop=True))
        MM.wait(nc.vector)
        cx.v(nc.vector.tensor_tensor(hmx[:, 0:n], hmx[:, 0:n], ps[:, 0:n], ALU.max))


def _emit_ffn0_xq(cx, T, xnh, xnl, rb, sb_s, xq, u1g, u1u, dumps, dump_t):
    nc = cx.nc
    LD, VE = cx.sem["LD"], cx.sem["VE"]
    wsc = cx.sb["s_wsc"]
    xsc = cx.sbuf("s_xqsc", [128, TOK], F32)
    urow = cx.sbuf("s_urow", [1, TOK], F32)
    cx.sem["TRP"].wait(nc.sync)
    for ch in range(NCH):
        VE.wait(nc.sync)
        LD.inc(nc.sync.dma_start(
            out=xsc[:, :],
            in_=_apd(T["xn_hbm"], ch * 128 * TOK, [[TOK, 128], [1, TOK]])), 16)
        LD.wait(nc.vector)
        cx.v(nc.vector.tensor_tensor(xsc[:, :], xsc[:, :], rb[:, :], ALU.mult))
        cx.v(nc.vector.tensor_scalar(xq[:, ch, :], xsc[:, :], MAGIC, MAGIC, ALU.add, ALU.subtract))
    if "d_xq" in dumps:
        VE.wait(nc.sync)
        LD.inc(nc.sync.dma_start(
            out=_apd(dump_t["d_xq"], 0, [[TOK, 128], [128 * TOK, NCH], [1, TOK]]),
            in_=_apd(xq, 0, [[NCH * TOK, 128], [TOK, NCH], [1, TOK]])), 16)
        LD.wait(nc.sync)
    cx.v(nc.vector.tensor_scalar(urow[0:1, :], sb_s[0:1, :],
                                 wsc[0:1, 0:1], None, ALU.mult))
    _bcast_row(cx, lambda th: urow[0:1, th * 512:(th + 1) * 512], u1g)
    cx.v(nc.vector.tensor_scalar(urow[0:1, :], sb_s[0:1, :],
                                 wsc[0:1, 1:2], None, ALU.mult))
    _bcast_row(cx, lambda th: urow[0:1, th * 512:(th + 1) * 512], u1u)


def _emit_ffn0_main(cx, T, xq, u1g, u1u, dumps, dump_t):
    nc = cx.nc
    cx.push_scope()
    LD, GLD, MM, VE, SC, GP, CC = (cx.sem[k] for k in
                                   ("LD", "GLD", "MM", "VE", "SC", "GP", "CC"))
    P = [cx.ps[f"p{i}"] for i in range(6)]
    hbuf = cx.sbuf("s_h", [128, 64, TFF], F32)
    hq = cx.sbuf("s_hq", [128, 64, TFF], BF16)
    wg = cx.sbuf("s_fwg", [128, 2, 16, 128], BF16)
    wu = cx.sbuf("s_fwu", [128, 2, 16, 128], BF16)
    wd = cx.sbuf("s_fwd", [128, 64, 128], BF16)
    u2b = cx.sbuf("s_u2b", [128, TFF], F32)
    r2b = cx.sbuf("s_r2b", [128, TFF], F32)
    hmx = cx.sbuf("s_hmx", [128, TFF], F32)
    tg = cx.sbuf("s_tg", [128, TFF], F32)
    gs = cx.sbuf("s_gs", [128, TFF], F32)
    tu = cx.sbuf("s_tu", [128, TFF], F32)
    fout = cx.sbuf("s_fout", [128, 2, TFF], F32)
    wsc = cx.sb["s_wsc"]

    CC.wait(nc.sync)
    CC.wait(nc.gpsimd)
    CC.wait(nc.tensor)

    def gate_tile_ap(oc, ic):
        s = oc // 8
        off = (s * 12288 + 0) * 1024 + (ic * 128) * 1024 + (oc % 8) * 128
        return _apd(T["ag1_out"], off, [[1024, 128], [1, 128]])

    def up_tile_ap(oc, ic):
        s = oc // 8
        off = (s * 12288 + 2048) * 1024 + (ic * 128) * 1024 + (oc % 8) * 128
        return _apd(T["ag1_out"], off, [[1024, 128], [1, 128]])

    def down_tile_ap(od, ic):
        s = od // 2
        off = (s * 12288 + 4096) * 1024 + (ic * 128) * 256 + (od % 2) * 128
        return _apd(T["ag1_out"], off, [[256, 128], [1, 128]])

    fo_ve0 = [0, 0]
    for slab in range(NSLAB):
        csl = slice(slab * TFF, (slab + 1) * TFF)
        VE.wait(nc.gpsimd)
        cx.g(nc.gpsimd.memset(hmx[:, :], 0.0))
        GP.wait(nc.vector)
        gu_mm = [0, 0]
        gu_ve = [fo_ve0[0], fo_ve0[1]]
        for oc in range(64):
            buf = oc % 2
            MM.wait(nc.sync, gu_mm[buf])
            LD.inc(nc.sync.dma_start(
                out=_apd(wg, buf * 16 * 128, [[2 * 16 * 128, 128], [128, 16], [1, 128]]),
                in_=_apd(T["ag1_out"], (oc // 8 * 12288) * 1024 + (oc % 8) * 128,
                         [[1024, 128], [128 * 1024, 16], [1, 128]])), 16)
            LD.inc(nc.sync.dma_start(
                out=_apd(wu, buf * 16 * 128, [[2 * 16 * 128, 128], [128, 16], [1, 128]]),
                in_=_apd(T["ag1_out"], (oc // 8 * 12288 + 2048) * 1024 + (oc % 8) * 128,
                         [[1024, 128], [128 * 1024, 16], [1, 128]])), 16)
            LD.wait(nc.tensor)
            psA, psB = P[oc % 2], P[2 + oc % 2]
            VE.wait(nc.tensor, gu_ve[oc % 2])
            for ic in range(NCH):
                MM.inc(nc.tensor.matmul(psA[:, 0:TFF], wg[:, buf, ic, :],
                                        _apd(xq, ic * TOK + slab * TFF, [[NCH * TOK, 128], [1, TFF]]),
                                        start=(ic == 0), stop=(ic == NCH - 1)))
            for ic in range(NCH):
                MM.inc(nc.tensor.matmul(psB[:, 0:TFF], wu[:, buf, ic, :],
                                        _apd(xq, ic * TOK + slab * TFF, [[NCH * TOK, 128], [1, TFF]]),
                                        start=(ic == 0), stop=(ic == NCH - 1)))
            MM.wait(nc.vector)
            MM.wait(nc.scalar)
            cx.v(nc.vector.scalar_tensor_tensor(tg[:, :], psA[:, 0:TFF], 1.0, u1g[:, csl],
                                                ALU.mult, ALU.mult))
            VE.wait(nc.scalar)
            cx.s(nc.scalar.activation(gs[:, :], tg[:, :], AF.Silu, bias=0.0, scale=1.0))
            gu_ve[oc % 2] = cx.v(nc.vector.scalar_tensor_tensor(
                tu[:, :], psB[:, 0:TFF], 1.0, u1u[:, csl], ALU.mult, ALU.mult))
            gu_mm[buf] = MM.n
            SC.wait(nc.vector)
            cx.v(nc.vector.tensor_tensor(hbuf[:, oc, :], tu[:, :], gs[:, :], ALU.mult))
            VE.wait(nc.scalar)
            cx.s(nc.scalar.activation(tg[:, :], hbuf[:, oc, :], AF.Abs, bias=0.0, scale=1.0))
            SC.wait(nc.vector)
            cx.v(nc.vector.tensor_tensor(hmx[:, :], hmx[:, :], tg[:, :], ALU.max))
        if "d_h" in dumps and slab == 0:
            VE.wait(nc.sync)
            LD.inc(nc.sync.dma_start(
                out=_apd(dump_t["d_h"], 0, [[TFF, 128], [128 * TFF, 64], [1, TFF]]),
                in_=_apd(hbuf, 0, [[64 * TFF, 128], [TFF, 64], [1, TFF]])), 16)
            LD.wait(nc.sync)
        GP.wait(nc.vector)
        _part_max_tree(cx, hmx, TFF, P[5])
        s2b = tg
        cx.v(nc.vector.tensor_scalar(s2b[:, :], hmx[:, :], 1.0 / 127.0, 1e-5,
                                     ALU.mult, ALU.add))
        cx.v(nc.vector.reciprocal(r2b[:, :], s2b[:, :]))
        cx.v(nc.vector.tensor_tensor(hmx[:, :], s2b[:, :], r2b[:, :], ALU.mult))
        cx.v(nc.vector.tensor_scalar(hmx[:, :], hmx[:, :], -1.0, 2.0, ALU.mult, ALU.add))
        cx.v(nc.vector.tensor_tensor(r2b[:, :], r2b[:, :], hmx[:, :], ALU.mult))
        cx.v(nc.vector.tensor_scalar(u2b[:, :], s2b[:, :], cx.sb["s_wscb"][:, 2:3],
                                     None, ALU.mult))
        ve_hq = []
        for oc in range(64):
            cx.v(nc.vector.tensor_tensor(tg[:, :], hbuf[:, oc, :], r2b[:, :], ALU.mult))
            ve_hq.append(cx.v(nc.vector.tensor_scalar(hq[:, oc, :], tg[:, :], MAGIC, MAGIC,
                                         ALU.add, ALU.subtract)))
        if "d_hq" in dumps and slab == 0:
            VE.wait(nc.sync)
            LD.inc(nc.sync.dma_start(
                out=_apd(dump_t["d_hq"], 0, [[TFF, 128], [128 * TFF, 64], [1, TFF]]),
                in_=_apd(hq, 0, [[64 * TFF, 128], [TFF, 64], [1, TFF]])), 16)
            LD.wait(nc.sync)
        SLD = cx.sem["SLD"]
        for od in range(NCH):
            MM.wait(nc.sync)
            MM.wait(nc.scalar)
            LD.inc(nc.sync.dma_start(
                out=wd[:, 0:32, :],
                in_=_apd(T["ag1_out"],
                         (od // 2 * 12288 + 4096) * 1024 + (od % 2) * 128,
                         [[256, 128], [128 * 256, 32], [1, 128]])), 16)
            ldh0 = LD.n
            SLD.inc(nc.scalar.dma_start(
                out=wd[:, 32:64, :],
                in_=_apd(T["ag1_out"],
                         (od // 2 * 12288 + 4096) * 1024 + 32 * 128 * 256 + (od % 2) * 128,
                         [[256, 128], [128 * 256, 32], [1, 128]])), 16)
            ldh1 = SLD.n
            ps = P[od % 2]
            VE.wait(nc.tensor, fo_ve0[od % 2])
            for ic in range(64):
                if od == 0 and ic % 8 == 0:
                    VE.wait(nc.tensor, ve_hq[ic + 7])
                if ic == 0:
                    LD.wait(nc.tensor, ldh0)
                if ic == 32:
                    SLD.wait(nc.tensor, ldh1)
                MM.inc(nc.tensor.matmul(ps[:, 0:TFF], wd[:, ic, :], hq[:, ic, :],
                                        start=(ic == 0), stop=(ic == 63)))
            MM.wait(nc.vector)
            GLD.wait(nc.vector)
            fo_ve0[od % 2] = cx.v(nc.vector.scalar_tensor_tensor(fout[:, od % 2, :], ps[:, 0:TFF], 1.0,
                                                u2b[:, :], ALU.mult, ALU.mult))
            VE.wait(nc.gpsimd)
            GLD.inc(nc.gpsimd.dma_start(
                out=_apd(T["x_res"], od * 128 * TOK + slab * TFF, [[TOK, 128], [1, TFF]]),
                in_=fout[:, od % 2, :], accum_op=ALU.add), 16)
    cx.pop_scope()


def _emit_attn0(cx, T, attn_sb, dumps, dump_t):
    nc = cx.nc
    cx.push_scope()
    LD, GLD, MM, VE, SC, GP, CC = (cx.sem[k] for k in
                                   ("LD", "GLD", "MM", "VE", "SC", "GP", "CC"))
    P = [cx.ps[f"p{i}"] for i in range(6)]
    kb = cx.sbuf("s_kb", [128, 2, 2, 32, 128], BF16)
    vb = cx.sbuf("s_vb", [128, 2, 2, 32, 128], BF16)
    qblk = cx.sbuf("s_qblk", [128, 2, 2, 4, 128], BF16)
    pf = cx.sbuf("s_pf", [128, 2, 512], F32)
    ph = cx.sbuf("s_ph", [128, 2, 512], BF16)
    pl = cx.sbuf("s_pl", [128, 2, 512], BF16)
    ts_ = cx.sbuf("s_ts", [128, 2, 512], F32)
    rrow = cx.sbuf("s_rrow", [1, 512], F32)
    rbt = cx.sbuf("s_rbt", [128, 512], F32)
    ones20 = cx.sbuf("s_ones20", [128, 1], F32)
    cx.g(nc.gpsimd.memset(ones20[:, :], 1048576.0))
    bmask = cx.sb["s_bmask"]

    SLD = cx.sem["SLD"]
    CC.wait(nc.sync)
    CC.wait(nc.scalar)
    for kvh in range(2):
        for half in range(2):
            for gp_ in range(4):
                LD.inc(nc.sync.dma_start(
                    out=_apd(kb, (kvh * 2 + half) * 32 * 128 + gp_ * 128,
                             [[2 * 2 * 32 * 128, 128], [4 * 128, 8], [1, 128]]),
                    in_=_apd(T["ag2_out"], (gp_ * 1024 + kvh * 256 + half * 128) * 1024,
                             [[1024, 128], [128, 8], [1, 128]])), 16)
                SLD.inc(nc.scalar.dma_start(
                    out=_apd(vb, (kvh * 2 + half) * 32 * 128 + gp_ * 128,
                             [[2 * 2 * 32 * 128, 128], [4 * 128, 8], [1, 128]]),
                    in_=_apd(T["ag2_out"], (gp_ * 1024 + 512 + (kvh * 2 + half) * 128) * 1024,
                             [[128, 128], [128 * 128, 8], [1, 128]])), 16)
    LD.wait(nc.tensor)
    SLD.wait(nc.tensor)

    qb_mm = [0, 0]
    bank_tanh = [0, 0]
    ld_qb = [0, 0]
    blocks = [(jj, hh) for jj in range(NSLOT) for hh in range(4)]

    def _qload(b):
        jj, hh = blocks[b]
        sl = b % 2
        MM.wait(nc.sync, qb_mm[sl])
        for half in range(2):
            LD.inc(nc.sync.dma_start(
                out=qblk[:, sl, half, :, :],
                in_=_apd(T["q_spill"], (half * 2048 + hh * 512) * TOK + jj * 128,
                         [[TOK, 128], [128 * TOK, 4], [1, 128]]), ), 16)
        ld_qb[sl] = LD.n

    blk_i = 0
    for j in range(NSLOT):
        for hg in range(4):
            kvh = hg // 2
            sl_q = blk_i % 2
            if blk_i == 0:
                _qload(0)
            if blk_i + 1 < len(blocks):
                _qload(blk_i + 1)
            LD.wait(nc.tensor, ld_qb[sl_q])
            nk = 4 * j + 4
            sc_tanh = {}
            ve_split = {}
            mm_pv = {}

            def scores0(k, sl_q=sl_q, kvh=kvh, j=j):
                ps_s = P[k % 2]
                SC.wait(nc.tensor, sc_tanh[k - 2] if k >= 2 else bank_tanh[k % 2])
                MM.inc(nc.tensor.matmul(ps_s[:, :], kb[:, kvh, 0, k, :],
                                        qblk[:, sl_q, 0, :, :], start=True, stop=False))
                MM.inc(nc.tensor.matmul(ps_s[:, :], kb[:, kvh, 0, k, :],
                                        qblk[:, sl_q, 1, :, :], start=False, stop=False))
                mm_sc = MM.inc(nc.tensor.matmul(ps_s[:, :], kb[:, kvh, 1, k, :],
                                                qblk[:, sl_q, 0, :, :], start=False, stop=True))
                MM.wait(nc.scalar, mm_sc)
                if k >= 2:
                    VE.wait(nc.scalar, ve_split[k - 2])
                    MM.wait(nc.scalar, mm_pv[k - 2])
                    MM.wait(nc.vector, mm_pv[k - 2])
                sc_tanh[k] = cx.s(nc.scalar.activation(
                    ts_[:, k % 2, :], ps_s[:, :], AF.Tanh, bias=0.0, scale=SCALE0 / CAP))
                bank_tanh[k % 2] = sc_tanh[k]
                cx.s(nc.scalar.activation(
                    pf[:, k % 2, :], ts_[:, k % 2, :], AF.Exp, bias=0.0, scale=CAP))
                SC.wait(nc.vector)
                b = k - 4 * j
                if b >= 0:
                    cx.v(nc.vector.tensor_tensor(
                        pf[:, k % 2, :], pf[:, k % 2, :],
                        _apd(bmask, b * 128, [[512, 128], [0, 4], [1, 128]]), ALU.mult))
                cx.v(nc.vector.tensor_copy(ph[:, k % 2, :], pf[:, k % 2, :]))
                ve_split[k] = cx.v(nc.vector.tensor_tensor(
                    pl[:, k % 2, :], pf[:, k % 2, :], ph[:, k % 2, :], ALU.subtract))

            def pvrow(k, kvh=kvh, nk=nk):
                VE.wait(nc.tensor, ve_split[k])
                MM.inc(nc.tensor.matmul(P[3][0:1, :], ones20[:, :], pf[:, k % 2, :],
                                        start=(k == 0), stop=(k == nk - 1)))
                MM.inc(nc.tensor.matmul(P[2][:, :], vb[:, kvh, 0, k, :], ph[:, k % 2, :],
                                        start=(k == 0), stop=False))
                MM.inc(nc.tensor.matmul(P[2][:, :], vb[:, kvh, 0, k, :], pl[:, k % 2, :],
                                        start=False, stop=False))
                mm_pv[k] = MM.inc(nc.tensor.matmul(P[2][:, :], vb[:, kvh, 1, k, :],
                                                   ph[:, k % 2, :],
                                                   start=False, stop=(k == nk - 1)))

            scores0(0)
            for k in range(1, nk):
                scores0(k)
                pvrow(k - 1)
            pvrow(nk - 1)
            qb_mm[sl_q] = MM.n
            blk_i += 1
            MM.wait(nc.vector)
            cx.v(nc.vector.reciprocal(rrow[0:1, :], P[3][0:1, :]))
            cx.v(nc.vector.tensor_scalar(rrow[0:1, :], rrow[0:1, :], 1048576.0, None, ALU.mult))
            VE.wait(nc.tensor)
            MM.inc(nc.tensor.matmul(P[4][:, :], cx.sb["s_onesr"][:, :], rrow[0:1, :],
                                    start=True, stop=True))
            MM.wait(nc.vector)
            cx.v(nc.vector.tensor_copy(rbt[:, :], P[4][:, :]))
            cx.v(nc.vector.scalar_tensor_tensor(
                _apd(attn_sb, (hg * 4) * TOK + j * 128, [[NCH * TOK, 128], [TOK, 4], [1, 128]]),
                P[2][:, :], 1.0, rbt[:, :], ALU.mult, ALU.mult))
            VE.wait(nc.tensor)
    if "d_attn0" in dumps:
        VE.wait(nc.sync)
        LD.inc(nc.sync.dma_start(
            out=_apd(dump_t["d_attn0"], 0, [[TOK, 128], [128 * TOK, NCH], [1, TOK]]),
            in_=_apd(attn_sb, 0, [[NCH * TOK, 128], [TOK, NCH], [1, TOK]])), 16)
        LD.wait(nc.sync)
    cx.pop_scope()


def _emit_wo(cx, T, attn_sb, w, bias, layer):
    nc = cx.nc
    cx.push_scope()
    LD, GLD, MM, VE, SC, GP = (cx.sem[k] for k in
                               ("LD", "GLD", "MM", "VE", "SC", "GP"))
    P = [cx.ps[f"p{i}"] for i in range(6)]
    abh = cx.sbuf("s_abh", [128, NCH, TOK], BF16)
    abl = cx.sbuf("s_abl", [128, NCH, TOK], BF16)
    whi = cx.sbuf("s_owhi", [128, 2, 16, 128], BF16)
    wlo = cx.sbuf("s_owlo", [128, 2, 16, 128], BF16)
    w32 = cx.sbuf("s_ow32", [128, 2, 16, 256], F32)
    fout = cx.sbuf("s_ofout", [128, 2, 512], F32)
    if bias is not None:
        bo = cx.sbuf("s_bo", [128, NCH], F32)
        LD.inc(nc.sync.dma_start(
            out=bo[:, :], in_=_apd(T[bias], 0, [[1, 128], [128, NCH]]),
            allow_slow_non_contiguous=True), 16)
        LD.wait(nc.vector)

    for ch in range(NCH):
        cx.v(nc.vector.tensor_copy(abh[:, ch, :], attn_sb[:, ch, :]))
        cx.v(nc.vector.tensor_tensor(abl[:, ch, :], attn_sb[:, ch, :], abh[:, ch, :],
                                     ALU.subtract))
    for oc in range(NCH):
        buf = oc % 2
        _load_w_hilo(cx, T, w, oc * 128, 128, whi, wlo, w32, buf, key=w)
        GP.wait(nc.tensor)
        for th in range(2):
            ps = P[(oc * 2 + th) % 2]
            VE.wait(nc.tensor)
            for ic in range(NCH):
                MM.inc(nc.tensor.matmul(ps[:, :], whi[:, buf, ic, :],
                                        _apd(abh, ic * TOK + th * 512, [[NCH * TOK, 128], [1, 512]]),
                                        start=(ic == 0), stop=False))
                MM.inc(nc.tensor.matmul(ps[:, :], whi[:, buf, ic, :],
                                        _apd(abl, ic * TOK + th * 512, [[NCH * TOK, 128], [1, 512]]),
                                        start=False, stop=False))
            for ic in range(NCH):
                MM.inc(nc.tensor.matmul(ps[:, :], wlo[:, buf, ic, :],
                                        _apd(abh, ic * TOK + th * 512, [[NCH * TOK, 128], [1, 512]]),
                                        start=False, stop=(ic == NCH - 1)))
            MM.wait(nc.vector)
            GLD.wait(nc.vector)
            if bias is not None:
                cx.v(nc.vector.tensor_scalar(fout[:, th, :], ps[:, :],
                                             bo[:, oc:oc + 1], None, ALU.add))
            else:
                cx.v(nc.vector.tensor_copy(fout[:, th, :], ps[:, :]))
            VE.wait(nc.gpsimd)
            GLD.inc(nc.gpsimd.dma_start(
                out=_apd(T["x_res"], oc * 128 * TOK + th * 512, [[TOK, 128], [1, 512]]),
                in_=fout[:, th, :], accum_op=ALU.add), 16)
        _wmark(cx, w, buf)
    cx.pop_scope()


def _emit_proj1(cx, T, xnh, xnl, dumps, dump_t):
    nc = cx.nc
    cx.push_scope()
    LD, GLD, MM, VE, SC, GP, CC = (cx.sem[k] for k in
                                   ("LD", "GLD", "MM", "VE", "SC", "GP", "CC"))
    P = [cx.ps[f"p{i}"] for i in range(6)]
    bias4 = cx.sbuf("s_bias4", [128, 4, NCH], F32)
    dec_b = cx.sbuf("s_decb", [128, NCH], F32)
    decn_b = cx.sbuf("s_decnb", [128, NCH], F32)
    lnsc = cx.sbuf("s_lnsc", [128, 1], F32)
    pos_b = cx.sbuf("s_posb", [128, TOK], F32)
    bv_b = cx.sbuf("s_bvb", [128, D], F32)
    posr = cx.sbuf("s_posr", [1, TOK], F32)
    decr = cx.sbuf("s_decr", [1, NH], F32)
    bvr = cx.sbuf("s_bvr", [1, D], F32)
    sg = cx.sbuf("s_sg", [128, NH, 8], F32)

    for i, nm in enumerate(("b_q1", "b_k1", "b_v1", "b_g1")):
        LD.inc(nc.sync.dma_start(
            out=bias4[:, i, :], in_=_apd(T[nm], 0, [[1, 128], [128, NCH]]),
            allow_slow_non_contiguous=True), 16)
    LD.inc(nc.sync.dma_start(out=posr[0:1, :], in_=T["pos2048"][:, :]), 16)
    LD.inc(nc.sync.dma_start(out=decr[0:1, :], in_=T["decay"][:, :]), 16)
    LD.inc(nc.sync.dma_start(out=bvr[0:1, :], in_=T["b_v1"][:, :]), 16)
    cx.g(nc.gpsimd.memset(lnsc[:, :], LNLOG_SCALE0))
    LD.wait(nc.tensor)
    MM.inc(nc.tensor.matmul(P[0][:, 0:NH], cx.sb["s_onesr"][:, :], decr[0:1, :],
                            start=True, stop=True))
    MM.wait(nc.vector)
    cx.v(nc.vector.tensor_copy(dec_b[:, :], P[0][:, 0:NH]))
    cx.v(nc.vector.tensor_scalar(decn_b[:, :], dec_b[:, :], -1.0, None, ALU.mult))
    _bcast_row(cx, lambda th: posr[0:1, th * 512:(th + 1) * 512], pos_b)
    for th in range(4):
        VE.wait(nc.tensor)
        MM.inc(nc.tensor.matmul(P[0][:, :], cx.sb["s_onesr"][:, :],
                                bvr[0:1, th * 512:(th + 1) * 512], start=True, stop=True))
        MM.wait(nc.vector)
        cx.v(nc.vector.tensor_copy(bv_b[:, th * 512:(th + 1) * 512], P[0][:, :]))
    VE.wait(nc.tensor)

    cx.push_scope()
    whi = cx.sbuf("s_pwhi1", [128, 2, 16, 128], BF16)
    wlo = cx.sbuf("s_pwlo1", [128, 2, 16, 128], BF16)
    w32 = cx.sbuf("s_pw321", [128, 2, 16, 256], F32)
    bt = cx.sbuf("s_bt", [128, TOK], F32)
    qrot = cx.sbuf("s_qrot1", [128, 512], F32)
    qh = cx.sbuf("s_qh1", [128, 512], BF16)
    ql = cx.sbuf("s_ql1", [128, 512], BF16)

    def xnh_sl(ic, th, n=512):
        return _apd(xnh, ic * TOK + th * 512, [[NCH * TOK, 128], [1, n]])

    def xnl_sl(ic, th, n=512):
        return _apd(xnl, ic * TOK + th * 512, [[NCH * TOK, 128], [1, n]])

    def proj_mms(ps, buf, th):
        for ic in range(NCH):
            MM.inc(nc.tensor.matmul(ps[:, :], whi[:, buf, ic, :], xnh_sl(ic, th),
                                    start=(ic == 0), stop=False))
            MM.inc(nc.tensor.matmul(ps[:, :], whi[:, buf, ic, :], xnl_sl(ic, th),
                                    start=False, stop=False))
        for ic in range(NCH):
            MM.inc(nc.tensor.matmul(ps[:, :], wlo[:, buf, ic, :], xnh_sl(ic, th),
                                    start=False, stop=(ic == NCH - 1)))

    for oc in range(NCH):
        buf = oc % 2
        _load_w_hilo(cx, T, "w_q1", oc * 128, 128, whi, wlo, w32, buf, key="w_q1")
        GP.wait(nc.tensor)
        VE.wait(nc.scalar)
        MM.wait(nc.scalar)
        cx.s(nc.scalar.activation(bt[:, :], pos_b[:, :], AF.Exp,
                                  bias=lnsc[:, 0:1], scale=dec_b[:, oc:oc + 1]))
        SC.wait(nc.vector)
        for th in range(2):
            ps = P[(oc * 2 + th) % 2]
            VE.wait(nc.tensor)
            proj_mms(ps, buf, th)
            MM.wait(nc.vector)
            LD.wait(nc.vector)
            cx.v(nc.vector.scalar_tensor_tensor(
                qrot[:, :], ps[:, :], bias4[:, 0, oc:oc + 1],
                bt[:, th * 512:(th + 1) * 512], ALU.add, ALU.mult))
            cx.v(nc.vector.tensor_copy(qh[:, :], qrot[:, :]))
            cx.v(nc.vector.tensor_tensor(ql[:, :], qrot[:, :], qh[:, :], ALU.subtract))
            VE.wait(nc.sync)
            for half, t in ((0, qh), (1, ql)):
                LD.inc(nc.sync.dma_start(
                    out=_apd(T["q_spill"], (half * 2048 + oc * 128) * TOK + th * 512,
                             [[TOK, 128], [1, 512]]),
                    in_=t[:, :]), 16)
        _wmark(cx, "w_q1", buf)

    for oc in range(NCH):
        buf = oc % 2
        _load_w_hilo(cx, T, "w_k1", oc * 128, 128, whi, wlo, w32, buf, key="w_k1")
        GP.wait(nc.tensor)
        VE.wait(nc.scalar)
        MM.wait(nc.scalar)
        cx.s(nc.scalar.activation(bt[:, :], pos_b[:, :], AF.Exp,
                                  bias=0.0, scale=decn_b[:, oc:oc + 1]))
        SC.wait(nc.vector)
        for th in range(2):
            ps = P[(oc * 2 + th) % 2]
            VE.wait(nc.tensor)
            proj_mms(ps, buf, th)
            MM.wait(nc.vector)
            LD.wait(nc.vector)
            cx.v(nc.vector.scalar_tensor_tensor(
                qrot[:, :], ps[:, :], bias4[:, 1, oc:oc + 1],
                bt[:, th * 512:(th + 1) * 512], ALU.add, ALU.mult))
            for sl in range(4):
                cx.v(nc.vector.tensor_reduce(sg[:, oc, th * 4 + sl:th * 4 + sl + 1],
                                             qrot[:, sl * 128:(sl + 1) * 128],
                                             AX.X, ALU.add))
            cx.v(nc.vector.tensor_copy(qh[:, :], qrot[:, :]))
            cx.v(nc.vector.tensor_tensor(ql[:, :], qrot[:, :], qh[:, :], ALU.subtract))
            VE.wait(nc.sync)
            for half, t in ((0, qh), (1, ql)):
                LD.inc(nc.sync.dma_start(
                    out=_apd(T["ag3_in"], (oc * 256 + half * 128) * 1024 + th * 512,
                             [[1024, 128], [1, 512]]),
                    in_=t[:, :]), 16)
        _wmark(cx, "w_k1", buf)
    VE.wait(nc.gpsimd)
    GLD.inc(nc.gpsimd.dma_start(
        out=T["ag3s_in"][:, :],
        in_=_apd(sg, 0, [[NH * 8, 128], [1, NH * 8]])), 16)
    GLD.wait(nc.gpsimd)
    CC.inc(nc.gpsimd.collective_compute(
        "AllGather", ALU.bypass, replica_groups=[[0, 1, 2, 3], [4, 5, 6, 7]],
        ins=[T["ag3s_in"].ap().opt()], outs=[T["ag3s_out"].ap().opt()]))

    for oc in range(NCH):
        buf = oc % 2
        _load_w_hilo(cx, T, "w_g1", oc * 128, 128, whi, wlo, w32, buf, key="w_g1")
        GP.wait(nc.tensor)
        for th in range(2):
            ps = P[(oc * 2 + th) % 2]
            SC.wait(nc.tensor)
            proj_mms(ps, buf, th)
            MM.wait(nc.scalar)
            LD.wait(nc.scalar)
            cx.s(nc.scalar.activation(qrot[:, :], ps[:, :], AF.Silu,
                                      bias=bias4[:, 3, oc:oc + 1], scale=1.0))
            SC.wait(nc.sync)
            LD.inc(nc.sync.dma_start(
                out=_apd(T["gate1_h"], oc * 128 * TOK + th * 512, [[TOK, 128], [1, 512]]),
                in_=qrot[:, :]), 16)
        _wmark(cx, "w_g1", buf)

    cx.pop_scope()
    cx.push_scope()
    wvhi = cx.sbuf("s_wvhi1", [128, 16, 512], BF16)
    wvlo = cx.sbuf("s_wvlo1", [128, 16, 512], BF16)
    wv32 = cx.sbuf("s_wv321", [128, 16, 512], F32)
    qrot = cx.sbuf("s_qrot1", [128, 512], F32)
    qh = cx.sbuf("s_qh1", [128, 512], BF16)
    ql = cx.sbuf("s_ql1", [128, 512], BF16)
    for os_ in range(4):
        _load_w_hilo2(cx, T, "w_v1", os_ * 512, 512, wvhi, wvlo, wv32)
        GP.wait(nc.tensor)
        for tt in range(NSLOT):
            ps = P[4 + tt % 2]
            VE.wait(nc.tensor)
            for ic in range(NCH):
                xh = _apd(xnh, ic * TOK + tt * 128, [[NCH * TOK, 128], [1, 128]])
                xl = _apd(xnl, ic * TOK + tt * 128, [[NCH * TOK, 128], [1, 128]])
                MM.inc(nc.tensor.matmul(ps[:, :], xh, wvhi[:, ic, :],
                                        start=(ic == 0), stop=False))
                MM.inc(nc.tensor.matmul(ps[:, :], xl, wvhi[:, ic, :],
                                        start=False, stop=False))
                MM.inc(nc.tensor.matmul(ps[:, :], xh, wvlo[:, ic, :],
                                        start=False, stop=(ic == NCH - 1)))
            MM.wait(nc.vector)
            LD.wait(nc.vector)
            cx.v(nc.vector.scalar_tensor_tensor(
                qrot[:, :], ps[:, :], 1.0, bv_b[:, os_ * 512:(os_ + 1) * 512],
                ALU.mult, ALU.add))
            cx.v(nc.vector.tensor_copy(qh[:, :], qrot[:, :]))
            cx.v(nc.vector.tensor_tensor(ql[:, :], qrot[:, :], qh[:, :], ALU.subtract))
            VE.wait(nc.sync)
            for hh in range(4):
                h = os_ * 4 + hh
                for half, t in ((0, qh), (1, ql)):
                    LD.inc(nc.sync.dma_start(
                        out=_apd(T["ag3_in"],
                                 (4096 + (h * 2 + half) * 128) * 1024 + tt * 128 * 128,
                                 [[128, 128], [1, 128]]),
                        in_=t[:, hh * 128:(hh + 1) * 128]), 16)

    cx.pop_scope()
    LD.wait(nc.gpsimd)
    CC.inc(nc.gpsimd.collective_compute(
        "AllGather", ALU.bypass, replica_groups=[[0, 1, 2, 3], [4, 5, 6, 7]],
        ins=[T["ag3_in"].ap().opt()], outs=[T["ag3_out"].ap().opt()]))
    if "d_q1" in dumps:
        LD.wait(nc.sync)
        LD.inc(nc.sync.dma_start(out=dump_t["d_q1"][:, :], in_=T["q_spill"][:, :]), 16)
        LD.wait(nc.sync)
    if "d_ag3" in dumps:
        CC.wait(nc.sync)
        LD.inc(nc.sync.dma_start(out=dump_t["d_ag3"][:, :], in_=T["ag3_out"][0:8192, :]), 16)
        LD.wait(nc.sync)
    if "d_sig" in dumps:
        CC.wait(nc.sync)
        LD.inc(nc.sync.dma_start(out=dump_t["d_sig"][:, :], in_=T["ag3s_in"][:, :]), 16)
        LD.wait(nc.sync)
    if "d_gate1" in dumps:
        LD.wait(nc.sync)
        LD.inc(nc.sync.dma_start(out=dump_t["d_gate1"][:, :], in_=T["gate1_h"][:, :]), 16)
        LD.wait(nc.sync)
    cx.pop_scope()


def _emit_ffn1_gu(cx, T, xnh, xnl, invrow, dumps, dump_t):
    nc = cx.nc
    cx.push_scope()
    LD, GLD, MM, VE, SC, GP, CC = (cx.sem[k] for k in
                                   ("LD", "GLD", "MM", "VE", "SC", "GP", "CC"))
    P = [cx.ps[f"p{i}"] for i in range(6)]
    wt = cx.sbuf("s_guw", [128, 2, 2, 16, 128], BF16)
    gsb = cx.sbuf("s_gsb", [128, 512], F32)
    hsb = cx.sbuf("s_hsb", [128, 2, 512], F32)
    sqs = cx.sbuf("s_sqs", [128, 512], F32)

    CC.wait(nc.sync)
    CC.wait(nc.tensor)
    CC.wait(nc.gpsimd)

    def gu_tile_ap(og, ic):
        s = og // 16
        off = (s * 12288 + 6144) * 1024 + (ic * 128) * 2048 + (og % 16) * 128
        return _apd(T["ag1_out"], off, [[2048, 128], [1, 128]])

    gu1_mm = [0, 0]
    for og in range(64):
        buf = og % 2
        MM.wait(nc.sync, gu1_mm[buf])
        for gi, ogx in enumerate((og, og + 64)):
            LD.inc(nc.sync.dma_start(
                out=wt[:, buf, gi, :, :],
                in_=_apd(T["ag1_out"], (ogx // 16 * 12288 + 6144) * 1024 + (ogx % 16) * 128,
                         [[2048, 128], [128 * 2048, 16], [1, 128]])), 16)
        LD.wait(nc.tensor)
        for th in range(2):
            psA, psB = P[th], P[2 + th]
            VE.wait(nc.tensor)
            SC.wait(nc.tensor)
            for ic in range(NCH):
                MM.inc(nc.tensor.matmul(psA[:, :], wt[:, buf, 0, ic, :],
                                        _apd(xnh, ic * TOK + th * 512, [[NCH * TOK, 128], [1, 512]]),
                                        start=(ic == 0), stop=False))
                MM.inc(nc.tensor.matmul(psA[:, :], wt[:, buf, 0, ic, :],
                                        _apd(xnl, ic * TOK + th * 512, [[NCH * TOK, 128], [1, 512]]),
                                        start=False, stop=(ic == NCH - 1)))
            for ic in range(NCH):
                MM.inc(nc.tensor.matmul(psB[:, :], wt[:, buf, 1, ic, :],
                                        _apd(xnh, ic * TOK + th * 512, [[NCH * TOK, 128], [1, 512]]),
                                        start=(ic == 0), stop=False))
                MM.inc(nc.tensor.matmul(psB[:, :], wt[:, buf, 1, ic, :],
                                        _apd(xnl, ic * TOK + th * 512, [[NCH * TOK, 128], [1, 512]]),
                                        start=False, stop=(ic == NCH - 1)))
            MM.wait(nc.scalar)
            VE.wait(nc.scalar)
            cx.s(nc.scalar.activation(gsb[:, :], psA[:, :], AF.Silu, bias=0.0, scale=1.0))
            SC.wait(nc.vector)
            GLD.wait(nc.vector)
            cx.v(nc.vector.tensor_tensor(hsb[:, th, :], gsb[:, :], psB[:, :], ALU.mult))
            VE.wait(nc.scalar)
            MM.wait(nc.scalar)
            cx.s(nc.scalar.activation(sqs[:, :], hsb[:, th, :], AF.Square, bias=0.0, scale=1.0))
            SC.wait(nc.tensor)
            MM.inc(nc.tensor.matmul(P[4][0:1, :] if th == 0 else P[5][0:1, :],
                                    cx.sb["s_ones"][:, :], sqs[:, :],
                                    start=(og == 0), stop=(og == 63)))
            VE.wait(nc.gpsimd)
            GLD.inc(nc.gpsimd.dma_start(
                out=_apd(T["h_hbm"], og * 128 * TOK + th * 512, [[TOK, 128], [1, 512]]),
                in_=hsb[:, th, :]), 16)
        gu1_mm[buf] = MM.n
    MM.wait(nc.vector)
    eps32 = float(np.finfo(np.float32).eps)
    for th, pp in ((0, P[4]), (1, P[5])):
        sl = slice(th * 512, (th + 1) * 512)
        cx.v(nc.vector.tensor_scalar(invrow[0:1, sl], pp[0:1, :], 1.0 / FFH, eps32,
                                     ALU.mult, ALU.add))
    VE.wait(nc.scalar)
    cx.s(nc.scalar.activation(invrow[0:1, :], invrow[0:1, :], AF.Sqrt, bias=0.0, scale=1.0))
    SC.wait(nc.vector)
    cx.v(nc.vector.reciprocal(invrow[0:1, :], invrow[0:1, :]))
    cx.pop_scope()


def _emit_ffn1_proj(cx, T, invrow, dumps, dump_t):
    nc = cx.nc
    cx.push_scope()
    LD, GLD, MM, VE, SC, GP, CC = (cx.sem[k] for k in
                                   ("LD", "GLD", "MM", "VE", "SC", "GP", "CC"))
    P = [cx.ps[f"p{i}"] for i in range(6)]
    hsl = cx.sbuf("s_hsl", [128, 64, TFF], F32)
    hnh = cx.sbuf("s_hnh", [128, 64, TFF], BF16)
    hnl = cx.sbuf("s_hnl", [128, 64, TFF], BF16)
    wp = cx.sbuf("s_wp", [128, 64, 128], BF16)
    invb = cx.sbuf("s_invb1", [128, TFF], F32)
    fout = cx.sbuf("s_fout1", [128, 2, TFF], F32)
    tsc = cx.sbuf("s_tsc1", [128, TFF], F32)

    SLDq = cx.sem["SLD"]
    fo_ve = [0, 0]
    for slab in range(NSLAB):
        VE.wait(nc.sync)
        VE.wait(nc.scalar)
        MM.wait(nc.sync)
        MM.wait(nc.scalar)
        ldq = []
        for q in range(4):
            sem_, eng_ = (LD, nc.sync) if q % 2 == 0 else (SLDq, nc.scalar)
            sem_.inc(eng_.dma_start(
                out=hsl[:, q * 16:(q + 1) * 16, :],
                in_=_apd(T["h_hbm"], slab * TFF + q * 16 * 128 * TOK,
                         [[TOK, 128], [128 * TOK, 16], [1, TFF]])), 16)
            ldq.append((sem_, sem_.n))
        VE.wait(nc.tensor)
        MM.inc(nc.tensor.matmul(P[4][:, 0:TFF], cx.sb["s_onesr"][:, :],
                                invrow[0:1, slab * TFF:(slab + 1) * TFF],
                                start=True, stop=True))
        MM.wait(nc.vector)
        cx.v(nc.vector.tensor_copy(invb[:, :], P[4][:, 0:TFF]))
        if "d_h" in dumps and slab == 0:
            LD.wait(nc.sync)
            LD.inc(nc.sync.dma_start(
                out=_apd(dump_t["d_h"], 0, [[TFF, 128], [128 * TFF, 64], [1, TFF]]),
                in_=_apd(T["h_hbm"], 0, [[TOK, 128], [128 * TOK, 64], [1, TFF]])), 16)
            LD.wait(nc.sync)
        MM.wait(nc.vector)
        ve_q = []
        for q in range(4):
            ldq[q][0].wait(nc.vector, ldq[q][1])
            for oc in range(q * 16, (q + 1) * 16):
                cx.v(nc.vector.tensor_tensor(tsc[:, :], hsl[:, oc, :], invb[:, :], ALU.mult))
                cx.v(nc.vector.tensor_copy(hnh[:, oc, :], tsc[:, :]))
                cx.v(nc.vector.tensor_tensor(hnl[:, oc, :], tsc[:, :], hnh[:, oc, :],
                                             ALU.subtract))
            ve_q.append(cx.sem["VE"].n)
        if "d_hn1" in dumps and slab == 0:
            VE.wait(nc.sync)
            LD.inc(nc.sync.dma_start(
                out=_apd(dump_t["d_hn1"], 0, [[TFF, 128], [128 * TFF, 64], [1, TFF]]),
                in_=_apd(hnh, 0, [[64 * TFF, 128], [TFF, 64], [1, TFF]])), 16)
            LD.wait(nc.sync)
        SLD = cx.sem["SLD"]
        for od in range(NCH):
            MM.wait(nc.sync)
            MM.wait(nc.scalar)
            LD.inc(nc.sync.dma_start(
                out=wp[:, 0:32, :],
                in_=_apd(T["ag1_out"],
                         (od // 2 * 12288 + 10240) * 1024 + (od % 2) * 128,
                         [[256, 128], [128 * 256, 32], [1, 128]])), 16)
            ldh0 = LD.n
            SLD.inc(nc.scalar.dma_start(
                out=wp[:, 32:64, :],
                in_=_apd(T["ag1_out"],
                         (od // 2 * 12288 + 10240) * 1024 + 32 * 128 * 256 + (od % 2) * 128,
                         [[256, 128], [128 * 256, 32], [1, 128]])), 16)
            ldh1 = SLD.n
            ps = P[od % 2]
            VE.wait(nc.tensor, fo_ve[od % 2])
            LD.wait(nc.tensor, ldh0)
            for ic in range(32):
                if od == 0 and ic % 16 == 0:
                    VE.wait(nc.tensor, ve_q[ic // 16])
                MM.inc(nc.tensor.matmul(ps[:, 0:TFF], wp[:, ic, :], hnh[:, ic, :],
                                        start=(ic == 0), stop=False))
                MM.inc(nc.tensor.matmul(ps[:, 0:TFF], wp[:, ic, :], hnl[:, ic, :],
                                        start=False, stop=False))
            SLD.wait(nc.tensor, ldh1)
            for ic in range(32, 64):
                if od == 0 and ic % 16 == 0:
                    VE.wait(nc.tensor, ve_q[ic // 16])
                MM.inc(nc.tensor.matmul(ps[:, 0:TFF], wp[:, ic, :], hnh[:, ic, :],
                                        start=False, stop=False))
                MM.inc(nc.tensor.matmul(ps[:, 0:TFF], wp[:, ic, :], hnl[:, ic, :],
                                        start=False, stop=(ic == 63)))
            MM.wait(nc.vector)
            GLD.wait(nc.vector)
            fo_ve[od % 2] = cx.v(nc.vector.tensor_copy(fout[:, od % 2, :], ps[:, 0:TFF]))
            VE.wait(nc.gpsimd)
            GLD.inc(nc.gpsimd.dma_start(
                out=_apd(T["x_res"], od * 128 * TOK + slab * TFF, [[TOK, 128], [1, TFF]]),
                in_=fout[:, od % 2, :], accum_op=ALU.add), 16)
    cx.pop_scope()


def _emit_attn1(cx, T, attn_sb, dumps, dump_t):
    nc = cx.nc
    cx.push_scope()
    LD, GLD, MM, VE, SC, GP, CC = (cx.sem[k] for k in
                                   ("LD", "GLD", "MM", "VE", "SC", "GP", "CC"))
    P = [cx.ps[f"p{i}"] for i in range(6)]
    kb1 = cx.sbuf("s_kb1", [128, 2, 2, 32, 128], BF16)
    vb1 = cx.sbuf("s_vb1", [128, 2, 2, 32, 128], BF16)
    qb1 = cx.sbuf("s_qb1", [128, 2, 2, TOK], BF16)
    sg1 = cx.sbuf("s_sg1", [128, 32], F32)
    sgz = cx.sbuf("s_sgz", [128, 32], F32)
    spref = cx.sbuf("s_spref", [128, 32], F32)
    sadj = cx.sbuf("s_sadj", [128, 8], F32)
    sh = cx.sbuf("s_sh", [128, 8], BF16)
    sl_ = cx.sbuf("s_sl", [128, 8], BF16)
    corr = cx.sbuf("s_corr", [128, 1], F32)
    cm1 = cx.sbuf("s_cm1", [128, 1], F32)
    dec_b = cx.sbuf("s_decb1", [128, NH], F32)
    decr = cx.sbuf("s_decr1", [1, NH], F32)
    pf = cx.sbuf("s_pf1", [128, 2, 512], F32)
    ph = cx.sbuf("s_ph1", [128, 2, 512], BF16)
    pl = cx.sbuf("s_pl1", [128, 2, 512], BF16)
    rrow = cx.sbuf("s_rrow1", [1, 512], F32)
    rabs = cx.sbuf("s_rabs1", [1, 512], F32)
    rbt = cx.sbuf("s_rbt1", [128, 512], F32)
    onesb = cx.sbuf("s_onesb", [128, 1], BF16)
    sgall = cx.sbuf("s_sgall", [128, 4, 128], F32)
    bmask = cx.sb["s_bmask"]
    SLD = cx.sem["SLD"]
    cx.g(nc.gpsimd.memset(onesb[:, :], 1.0))

    cx.g(nc.gpsimd.memset(sgz[:, :], 0.0))
    GP.wait(nc.tensor)
    LD.inc(nc.sync.dma_start(out=decr[0:1, :], in_=T["decay"][:, :]), 16)
    LD.wait(nc.tensor)
    MM.inc(nc.tensor.matmul(P[0][:, 0:NH], cx.sb["s_onesr"][:, :], decr[0:1, :],
                            start=True, stop=True))
    MM.wait(nc.vector)
    cx.v(nc.vector.tensor_copy(dec_b[:, :], P[0][:, 0:NH]))
    CC.wait(nc.sync)
    CC.wait(nc.tensor)
    CC.wait(nc.scalar)
    CC.wait(nc.vector)
    for gp_ in range(4):
        LD.inc(nc.sync.dma_start(
            out=sgall[:, gp_, :],
            in_=_apd(T["ag3s_out"], gp_ * 128 * 128, [[128, 128], [1, 128]])), 16)
    ld_sg = LD.n

    hb_mm = [0, 0]
    bank_ev = [0, 0]
    ld_k = [0, 0]
    ld_v = [0, 0]
    for h in range(NH):
        hs = h % 2
        MM.wait(nc.sync, hb_mm[hs])
        MM.wait(nc.scalar, hb_mm[hs])
        for half in range(2):
            for gp_ in range(4):
                LD.inc(nc.sync.dma_start(
                    out=_apd(kb1, hs * 8192 + half * 4096 + gp_ * 128,
                             [[16384, 128], [4 * 128, 8], [1, 128]]),
                    in_=_apd(T["ag3_out"],
                             (gp_ * 8192 + h * 256 + half * 128) * 1024,
                             [[1024, 128], [128, 8], [1, 128]])), 16)
                SLD.inc(nc.scalar.dma_start(
                    out=_apd(vb1, hs * 8192 + half * 4096 + gp_ * 128,
                             [[16384, 128], [4 * 128, 8], [1, 128]]),
                    in_=_apd(T["ag3_out"],
                             (gp_ * 8192 + 4096 + (h * 2 + half) * 128) * 1024,
                             [[128, 128], [128 * 128, 8], [1, 128]])), 16)
        LD.inc(nc.sync.dma_start(
            out=qb1[:, hs, :, :],
            in_=_apd(T["q_spill"], h * 128 * TOK, [[TOK, 128], [2048 * TOK, 2], [1, TOK]])), 16)
        ld_k[hs] = LD.n
        ld_v[hs] = SLD.n
        if h == 0:
            LD.wait(nc.vector, ld_sg)
        for gp_ in range(4):
            cx.v(nc.vector.tensor_copy(
                _apd(sg1, gp_, [[32, 128], [4, 8]]),
                _apd(sgall, gp_ * 128 + h * 8, [[512, 128], [1, 8]])))
        cx.v(nc.vector.tensor_tensor_scan(spref[:, :], sg1[:, :], sgz[:, :], 0.0,
                                          ALU.add, ALU.add))
        VE.wait(nc.scalar)
        MM.wait(nc.scalar)
        cx.s(nc.scalar.activation(corr[:, 0:1], dec_b[:, h:h + 1], AF.Exp,
                                  bias=0.0, scale=2048.0))
        SC.wait(nc.vector)
        cx.v(nc.vector.tensor_scalar(cm1[:, 0:1], corr[:, 0:1], -1.0, None, ALU.add))
        for j in range(NSLOT):
            if j == 0:
                cx.g(nc.gpsimd.memset(sadj[:, 0:1], 0.0))
                GP.wait(nc.vector)
            elif j < 4:
                cx.v(nc.vector.tensor_copy(sadj[:, j:j + 1], spref[:, 4 * j - 1:4 * j]))
            else:
                cx.v(nc.vector.scalar_tensor_tensor(
                    sadj[:, j:j + 1], spref[:, 15:16], cm1[:, 0:1],
                    spref[:, 4 * j - 1:4 * j], ALU.mult, ALU.add))
        cx.v(nc.vector.tensor_copy(sh[:, :], sadj[:, :]))
        cx.v(nc.vector.tensor_tensor(sl_[:, :], sadj[:, :], sh[:, :], ALU.subtract))

        for bl in range(2):
            nk = 16 * bl + 16
            VE.wait(nc.tensor)
            if bl == 0:
                LD.wait(nc.tensor, ld_k[hs])
                SLD.wait(nc.tensor, ld_v[hs])
            first = True
            for j in range(4 * bl, 4 * bl + 4):
                if j == 0:
                    continue
                co = (j - 4 * bl) * 128
                MM.inc(nc.tensor.matmul(P[3][0:1, co:co + 128], sh[:, j:j + 1],
                                        qb1[:, hs, 0, j * 128:(j + 1) * 128],
                                        start=first, stop=False))
                MM.inc(nc.tensor.matmul(P[3][0:1, co:co + 128], sl_[:, j:j + 1],
                                        qb1[:, hs, 0, j * 128:(j + 1) * 128],
                                        start=False, stop=False))
                MM.inc(nc.tensor.matmul(P[3][0:1, co:co + 128], sh[:, j:j + 1],
                                        qb1[:, hs, 1, j * 128:(j + 1) * 128],
                                        start=False, stop=False))
                first = False
            sc_evac = {}
            mm_pv = {}

            def scores1(k, hs=hs, bl=bl):
                j0 = max(4 * bl, -(-(k - 3) // 4) if k > 3 else 4 * bl)
                co = (j0 - 4 * bl) * 128
                n = (4 * bl + 4 - j0) * 128
                ps_s = P[k % 2]
                VE.wait(nc.tensor, sc_evac[k - 2] if k >= 2 else bank_ev[k % 2])
                MM.inc(nc.tensor.matmul(ps_s[:, co:co + n], kb1[:, hs, 0, k, :],
                                        qb1[:, hs, 0, j0 * 128:(4 * bl + 4) * 128],
                                        start=True, stop=False))
                MM.inc(nc.tensor.matmul(ps_s[:, co:co + n], kb1[:, hs, 0, k, :],
                                        qb1[:, hs, 1, j0 * 128:(4 * bl + 4) * 128],
                                        start=False, stop=False))
                mm_sc = MM.inc(nc.tensor.matmul(ps_s[:, co:co + n], kb1[:, hs, 1, k, :],
                                                qb1[:, hs, 0, j0 * 128:(4 * bl + 4) * 128],
                                                start=False, stop=True))
                MM.wait(nc.vector, mm_sc)
                if k >= 2:
                    MM.wait(nc.vector, mm_pv[k - 2])
                if k >= 16 * bl:
                    b = k % 4
                    cx.v(nc.vector.tensor_tensor(
                        ps_s[:, co:co + 128], ps_s[:, co:co + 128],
                        _apd(bmask, b * 128, [[512, 128], [1, 128]]), ALU.mult))
                crossing = (bl == 1 and k < 16)
                if crossing:
                    cx.v(nc.vector.tensor_scalar(pf[:, k % 2, co:co + n],
                                                 ps_s[:, co:co + n],
                                                 corr[:, 0:1], None, ALU.mult))
                else:
                    cx.v(nc.vector.tensor_copy(pf[:, k % 2, co:co + n], ps_s[:, co:co + n]))
                cx.v(nc.vector.tensor_copy(ph[:, k % 2, co:co + n], pf[:, k % 2, co:co + n]))
                sc_evac[k] = cx.v(nc.vector.tensor_tensor(
                    pl[:, k % 2, co:co + n], pf[:, k % 2, co:co + n],
                    ph[:, k % 2, co:co + n], ALU.subtract))
                bank_ev[k % 2] = sc_evac[k]

            def pvrow1(k, hs=hs, bl=bl, nk=nk):
                j0 = max(4 * bl, -(-(k - 3) // 4) if k > 3 else 4 * bl)
                co = (j0 - 4 * bl) * 128
                n = (4 * bl + 4 - j0) * 128
                VE.wait(nc.tensor, sc_evac[k])
                if k >= 16 * bl:
                    MM.inc(nc.tensor.matmul(P[3][0:1, co:co + 128], onesb[:, :],
                                            ph[:, k % 2, co:co + 128],
                                            start=False, stop=False))
                    MM.inc(nc.tensor.matmul(P[3][0:1, co:co + 128], onesb[:, :],
                                            pl[:, k % 2, co:co + 128],
                                            start=False,
                                            stop=(k == nk - 1)))
                MM.inc(nc.tensor.matmul(P[2][:, co:co + n], vb1[:, hs, 0, k, :],
                                        ph[:, k % 2, co:co + n],
                                        start=(k == 0), stop=False))
                MM.inc(nc.tensor.matmul(P[2][:, co:co + n], vb1[:, hs, 0, k, :],
                                        pl[:, k % 2, co:co + n],
                                        start=False, stop=False))
                mm_pv[k] = MM.inc(nc.tensor.matmul(P[2][:, co:co + n], vb1[:, hs, 1, k, :],
                                                   ph[:, k % 2, co:co + n],
                                                   start=False, stop=(k == nk - 1)))

            scores1(0)
            for k in range(1, nk):
                scores1(k)
                pvrow1(k - 1)
            pvrow1(nk - 1)
            MM.wait(nc.scalar)
            cx.s(nc.scalar.activation(rabs[0:1, :], P[3][0:1, :], AF.Abs,
                                      bias=0.0, scale=1.0))
            SC.wait(nc.vector)
            cx.v(nc.vector.tensor_scalar_max(rrow[0:1, :], rabs[0:1, :], 1.0))
            cx.v(nc.vector.reciprocal(rrow[0:1, :], rrow[0:1, :]))
            VE.wait(nc.tensor)
            MM.inc(nc.tensor.matmul(P[4][:, :], cx.sb["s_onesr"][:, :], rrow[0:1, :],
                                    start=True, stop=True))
            MM.wait(nc.vector)
            cx.v(nc.vector.tensor_copy(rbt[:, :], P[4][:, :]))
            cx.v(nc.vector.scalar_tensor_tensor(
                _apd(attn_sb, h * TOK + bl * 512, [[NCH * TOK, 128], [1, 512]]),
                P[2][:, :], 1.0, rbt[:, :], ALU.mult, ALU.mult))
            VE.wait(nc.tensor)
            hb_mm[hs] = MM.n
    if "d_attn1" in dumps:
        VE.wait(nc.sync)
        LD.inc(nc.sync.dma_start(
            out=_apd(dump_t["d_attn1"], 0, [[TOK, 128], [128 * TOK, NCH], [1, TOK]]),
            in_=_apd(attn_sb, 0, [[NCH * TOK, 128], [TOK, NCH], [1, TOK]])), 16)
        LD.wait(nc.sync)
    cx.pop_scope()


def _emit_asm1(cx, T, attn_sb, dumps, dump_t):
    nc = cx.nc
    cx.push_scope()
    cx.push_scope()
    LD, GLD, MM, VE, SC, GP = (cx.sem[k] for k in
                               ("LD", "GLD", "MM", "VE", "SC", "GP"))
    P = [cx.ps[f"p{i}"] for i in range(6)]
    gnw = cx.sbuf("s_gnw", [128, NCH], F32)
    gnb = cx.sbuf("s_gnb", [128, NCH], F32)
    sq = cx.sbuf("s_gsq", [128, 512], F32)
    gtc = cx.sbuf("s_gtc", [128, 512], F32)
    row = cx.sbuf("s_grow", [1, 4 * 512], F32)
    ivb = cx.sbuf("s_givb", [128, 512], F32)
    m2b = cx.sbuf("s_gm2b", [128, 512], F32)
    LD.inc(nc.sync.dma_start(out=gnw[:, :], in_=_apd(T["gn_w"], 0, [[1, 128], [128, NCH]]),
                             allow_slow_non_contiguous=True), 16)
    LD.inc(nc.sync.dma_start(out=gnb[:, :], in_=_apd(T["gn_b"], 0, [[1, 128], [128, NCH]]),
                             allow_slow_non_contiguous=True), 16)
    LD.wait(nc.vector)
    mu = row[0:1, 0:512]
    var = row[0:1, 512:1024]
    m2r = row[0:1, 1024:1536]

    for ch in range(NCH):
        for th in range(2):
            asl = attn_sb[:, ch, th * 512:(th + 1) * 512]
            VE.wait(nc.scalar)
            MM.wait(nc.scalar)
            cx.s(nc.scalar.activation(sq[:, :], asl, AF.Square, bias=0.0, scale=1.0))
            SC.wait(nc.tensor)
            VE.wait(nc.tensor)
            MM.inc(nc.tensor.matmul(P[0][0:1, :], cx.sb["s_ones"][:, :], asl,
                                    start=True, stop=True))
            MM.inc(nc.tensor.matmul(P[1][0:1, :], cx.sb["s_ones"][:, :], sq[:, :],
                                    start=True, stop=True))
            MM.wait(nc.vector)
            cx.v(nc.vector.tensor_scalar(mu[:, :], P[0][0:1, :], 1.0 / 128.0, None, ALU.mult))
            cx.v(nc.vector.tensor_scalar(var[:, :], P[1][0:1, :], 1.0 / 128.0, None, ALU.mult))
            cx.v(nc.vector.tensor_tensor(m2r[:, :], mu[:, :], mu[:, :], ALU.mult))
            cx.v(nc.vector.tensor_tensor(var[:, :], var[:, :], m2r[:, :], ALU.subtract))
            cx.v(nc.vector.tensor_scalar(var[:, :], var[:, :], 1e-5, None, ALU.add))
            VE.wait(nc.scalar)
            cx.s(nc.scalar.activation(var[:, :], var[:, :], AF.Sqrt, bias=0.0, scale=1.0))
            SC.wait(nc.vector)
            cx.v(nc.vector.reciprocal(var[:, :], var[:, :]))
            cx.v(nc.vector.tensor_tensor(m2r[:, :], mu[:, :], var[:, :], ALU.mult))
            VE.wait(nc.tensor)
            MM.inc(nc.tensor.matmul(P[2][:, :], cx.sb["s_onesr"][:, :], var[:, :],
                                    start=True, stop=True))
            MM.inc(nc.tensor.matmul(P[3][:, :], cx.sb["s_onesr"][:, :], m2r[:, :],
                                    start=True, stop=True))
            MM.wait(nc.vector)
            cx.v(nc.vector.tensor_copy(ivb[:, :], P[2][:, :]))
            cx.v(nc.vector.tensor_copy(m2b[:, :], P[3][:, :]))
            VE.wait(nc.sync)
            LD.inc(nc.sync.dma_start(
                out=gtc[:, :],
                in_=_apd(T["gate1_h"], ch * 128 * TOK + th * 512, [[TOK, 128], [1, 512]])), 16)
            cx.v(nc.vector.scalar_tensor_tensor(sq[:, :], asl, 1.0, ivb[:, :],
                                                ALU.mult, ALU.mult))
            cx.v(nc.vector.tensor_tensor(sq[:, :], sq[:, :], m2b[:, :], ALU.subtract))
            cx.v(nc.vector.tensor_scalar(sq[:, :], sq[:, :], gnw[:, ch:ch + 1],
                                         gnb[:, ch:ch + 1], ALU.mult, ALU.add))
            LD.wait(nc.vector)
            cx.v(nc.vector.tensor_tensor(asl, sq[:, :], gtc[:, :], ALU.mult))
    cx.pop_scope()
    _emit_wo(cx, T, attn_sb, w="w_o1", bias="b_o1", layer=1)
    cx.pop_scope()



def _host_inputs(inputs):
    x0 = np.ascontiguousarray(np.asarray(inputs["L0_attn_hidden_states"], np.float32))
    maps = []
    inv = 1.0 / 10000.0 ** (np.arange(0, HD, 2, dtype=np.float32) / HD)
    eye32 = np.eye(128, dtype=np.float32)
    eyebf = np.eye(128).astype(BF)
    tri = np.tril(np.ones((128, 128), np.float32))
    swap = np.zeros((128, 128), np.float32)
    swap[np.arange(128), np.arange(128) ^ 64] = 1.0
    rotm = np.zeros((7, 128, 128), np.float32)
    for j in range(7):
        s = 1 << (6 - j)
        rotm[j, (np.arange(128) + s) % 128, np.arange(128)] = 1.0
    for c in range(NCORE):
        b, g = c // 4, c % 4
        tiles = [g + 4 * j for j in range(NSLOT)]
        rows = np.concatenate([np.arange(t * 128, (t + 1) * 128) for t in tiles])
        pos = rows.astype(np.float32)
        f = np.outer(pos, inv)
        bm = np.zeros((4, 128, 128), np.float32)
        for bi in range(4):
            if bi < g:
                bm[bi] = 1.0
            elif bi == g:
                bm[bi] = tri.T
        m = {
            "x0": np.ascontiguousarray(x0[b, rows, :]),
            "w_qkv": np.asarray(inputs["L0_attn_Wqkv"], np.float32),
            "w_o0": np.asarray(inputs["L0_attn_Wo"], np.float32),
            "wg_sl": np.ascontiguousarray(np.asarray(inputs["L0_ffn_w_gate"], np.float32)[c * 1024:(c + 1) * 1024]),
            "wu_sl": np.ascontiguousarray(np.asarray(inputs["L0_ffn_w_up"], np.float32)[c * 1024:(c + 1) * 1024]),
            "wd_sl": np.ascontiguousarray(np.asarray(inputs["L0_ffn_w_down"], np.float32)[c * 256:(c + 1) * 256]),
            "w_q1": np.asarray(inputs["L1_attn_Wq"], np.float32),
            "w_k1": np.asarray(inputs["L1_attn_Wk"], np.float32),
            "w_v1": np.asarray(inputs["L1_attn_Wv"], np.float32),
            "w_g1": np.asarray(inputs["L1_attn_Wg"], np.float32),
            "w_o1": np.asarray(inputs["L1_attn_Wo"], np.float32),
            "b_q1": np.asarray(inputs["L1_attn_bq"], np.float32).reshape(1, D),
            "b_k1": np.asarray(inputs["L1_attn_bk"], np.float32).reshape(1, D),
            "b_v1": np.asarray(inputs["L1_attn_bv"], np.float32).reshape(1, D),
            "b_g1": np.asarray(inputs["L1_attn_bg"], np.float32).reshape(1, D),
            "b_o1": np.asarray(inputs["L1_attn_bo"], np.float32).reshape(1, D),
            "gn_w": np.asarray(inputs["L1_attn_gn_w"], np.float32).reshape(1, D),
            "gn_b": np.asarray(inputs["L1_attn_gn_b"], np.float32).reshape(1, D),
            "decay": np.asarray(inputs["L1_attn_decay"], np.float32).reshape(1, NH),
            "wgu_sl": np.ascontiguousarray(np.asarray(inputs["L1_ffn_w_gate_up"], np.float32)[c * 2048:(c + 1) * 2048]),
            "wp_sl": np.ascontiguousarray(np.asarray(inputs["L1_ffn_w_proj"], np.float32)[c * 256:(c + 1) * 256]),
            "pos2048": np.mod(pos, 2048.0).reshape(1, TOK).astype(np.float32),
            "cos_t": np.ascontiguousarray(np.tile(np.cos(f).T.astype(np.float32), (2, 1))),
            "sin_t": np.ascontiguousarray(np.concatenate(
                [-np.sin(f).T, np.sin(f).T]).astype(np.float32)),
            "swap128": swap,
            "rotm": rotm.reshape(7 * 128, 128),
            "bmask": bm.reshape(4 * 128, 128),
            "ident32": eye32,
            "identbf": eyebf,
        }
        maps.append(m)
    return maps


def kernel(**inputs):
    global _LAST_RESULTS
    nc = _build(dumps=_DUMPS, stage=_STAGE)
    in_maps = _host_inputs(inputs)
    res = run_bass_kernel_spmd(nc, in_maps, core_ids=list(range(NCORE)))
    _LAST_RESULTS = res
    out = np.zeros((B, S, D), np.float32)
    for c in range(NCORE):
        b, g = c // 4, c % 4
        tiles = [g + 4 * j for j in range(NSLOT)]
        rows = np.concatenate([np.arange(t * 128, (t + 1) * 128) for t in tiles])
        out[b, rows, :] = np.asarray(res.results[c]["out_tok"], np.float32)
    return out
